# version 17
# baseline (speedup 1.0000x reference)
import numpy as np
import concourse.bass as bass
import concourse.mybir as mybir
from concourse.bass_utils import run_bass_kernel_spmd

F32 = mybir.dt.float32
BF16 = mybir.dt.bfloat16
AF = mybir.ActivationFunctionType
ALU = mybir.AluOpType

D = 1024
NS = 16
H = 4096
S = 2048
BATCH = 16
NCORES = 8
SEQ_PER_CORE = BATCH // NCORES
TT = 512
TILES_PER_SEQ = S // TT
DCH = D // 128
EPS = 1e-5
DELTA_SCALE = 0.99
DELTA_MIN = 0.01
SBUF_BASE = 16512


def build_nc(spc=SEQ_PER_CORE, tps=TILES_PER_SEQ, h=H):
    nt = spc * tps
    hch = h // 128
    nhs = h // TT
    s_len = tps * TT
    nsl_fc = h // 128
    nsl_pj = D // 128

    nc = bass.Bass("TRN2", target_bir_lowering=False)

    xT = nc.dram_tensor("xT", [spc, D, s_len], F32, kind="ExternalInput").ap()
    yT = nc.dram_tensor("yT", [spc, D, s_len], F32, kind="ExternalOutput").ap()
    wfcT = nc.dram_tensor("wfcT", [D, h], F32, kind="ExternalInput").ap()
    wprojT = nc.dram_tensor("wprojT", [h, D], F32, kind="ExternalInput").ap()
    woT = nc.dram_tensor("woT", [D, D], F32, kind="ExternalInput").ap()
    wbtT = nc.dram_tensor("wbtT", [D, 32], F32, kind="ExternalInput").ap()
    wcT = nc.dram_tensor("wcT", [NS, D], F32, kind="ExternalInput").ap()
    vmat = nc.dram_tensor("vmat", [NS, D], F32, kind="ExternalInput").ap()
    uT = nc.dram_tensor("uT", [NS, h], F32, kind="ExternalInput").ap()
    fbm = nc.dram_tensor("fbm", [NS, NS], F32, kind="ExternalInput").ap()
    avec = nc.dram_tensor("avec", [NS, 1], F32, kind="ExternalInput").ap()
    bb16 = nc.dram_tensor("bb16", [NS, 1], F32, kind="ExternalInput").ap()
    bt16 = nc.dram_tensor("bt16", [NS, 1], F32, kind="ExternalInput").ap()
    gate16 = nc.dram_tensor("gate16", [NS, 1], F32, kind="ExternalInput").ap()
    bc8 = nc.dram_tensor("bc8", [128, DCH], F32, kind="ExternalInput").ap()
    bo8 = nc.dram_tensor("bo8", [128, DCH], F32, kind="ExternalInput").ap()
    dd8 = nc.dram_tensor("dd8", [128, DCH], F32, kind="ExternalInput").ap()

    wcmb_sc = nc.dram_tensor("wcmb_sc", [nhs, 128, DCH, TT], BF16, kind="Internal").ap()
    wproj_sc = nc.dram_tensor("wproj_sc", [DCH, 128, hch, 128], BF16, kind="Internal").ap()

    cur = [SBUF_BASE]

    def alloc(name, shape, dtype, at=None):
        import math
        size = math.prod(shape[1:]) * mybir.dt.size(dtype)
        size = (size + 63) // 64 * 64
        if at is None:
            at = cur[0]
            cur[0] += size
        else:
            assert at + size <= nc.SBUF_PARTITION_SIZE_BYTES, (name, at, size)
        return nc.alloc_sbuf_tensor_at(name, shape, dtype, offset=at)

    x_s = alloc("x_s", [128, 2, DCH, TT], F32)
    x_s_at = cur[0] - 2 * DCH * TT * 4
    xn32 = alloc("xn32", [128, 2, DCH, TT], F32)
    xn32_at = cur[0] - 2 * DCH * TT * 4
    xb = alloc("xb", [128, DCH, TT], BF16)
    xb_at = cur[0] - DCH * TT * 2
    yb = alloc("yb", [128, DCH, TT], BF16)
    xnb = alloc("xnb", [128, 2, DCH, TT], BF16)
    xnb_at = cur[0] - 2 * DCH * TT * 2
    r2b = alloc("r2b", [128, hch, TT], BF16)
    r2b_at = cur[0] - hch * TT * 2
    woT_b = alloc("woT_b", [128, DCH, D], BF16)
    wcmb_st = alloc("wcmb_st", [128, 2, DCH, TT], BF16)
    wcmb_st_at = cur[0] - 2 * DCH * TT * 2
    wproj_st = alloc("wproj_st", [128, 2, hch, 128], BF16)
    wproj_st_at = cur[0] - 2 * hch * 128 * 2
    statA = alloc("statA", [128, 4, TT], F32)
    statA_at = cur[0] - 4 * TT * 4
    statB = alloc("statB", [128, 4, TT], F32)
    statB_at = cur[0] - 4 * TT * 4
    tmp2 = alloc("tmp2", [128, 2, TT], F32)
    scb16 = alloc("scb16", [128, TT], BF16, at=statB_at)
    hfbc = alloc("hfbc", [128, TT], F32, at=statB_at + TT * 2)
    nhfbc = alloc("nhfbc", [128, TT], F32, at=statB_at + TT * 6)
    ones_b = alloc("ones_b", [128, 128], BF16)
    ones32 = alloc("ones32", [128, 128], F32, at=statA_at)
    wcT_b = alloc("wcT_b", [NS, DCH, 128], BF16)
    v_b = alloc("v_b", [NS, DCH, 128], BF16, at=statA_at + 512)
    e1t = alloc("e1t", [NS, TT], F32)
    rsig = e1t
    aext = e1t
    but = alloc("but", [NS, TT], F32)
    ht = alloc("ht", [NS, TT], F32)
    carry = alloc("carry", [NS, 1], F32)
    hbt = alloc("hbt", [NS, TT], BF16)
    fbT_b = alloc("fbT_b", [NS, NS], BF16, at=statA_at + 2560)
    A_s = alloc("A_s", [NS, 1], F32)
    A99 = A_s
    A01 = alloc("A01", [NS, 1], F32)
    bb_s = alloc("bb_s", [NS, 1], F32)
    bt_s = alloc("bt_s", [NS, 1], F32)
    nbt = bt_s
    gate_s = alloc("gate_s", [NS, 1], F32, at=statA_at + 2624)
    bc_s = alloc("bc_s", [128, DCH], F32)
    bo_s = alloc("bo_s", [128, DCH], F32)
    dd_s = alloc("dd_s", [128, DCH], F32)
    wbtT_b = alloc("wbtT_b", [128, DCH, 32], BF16)
    assert cur[0] <= nc.SBUF_PARTITION_SIZE_BYTES, cur[0]

    r2b_size = hch * TT * 2
    if r2b_size >= 32768:
        GTb = alloc("GTb", [NS, h], BF16, at=wcmb_st_at)
        uT_b = alloc("uT_b", [NS, h], BF16, at=wcmb_st_at + 2 * h)
        cb1 = alloc("cb1", [128, DCH, TT], BF16, at=xnb_at)
        cb2 = alloc("cb2", [128, DCH, TT], BF16, at=r2b_at)
        dbuf = alloc("dbuf", [128, DCH, TT], BF16, at=r2b_at + 8192)
        wsl = alloc("wsl", [128, DCH, TT], F32, at=r2b_at + 16384)
        wsl2 = alloc("wsl2", [128, hch, 128], F32, at=r2b_at + 16384)
        aw32 = alloc("aw32", [128, DCH, TT], F32, at=x_s_at)
        aw32p = alloc("aw32p", [128, hch, 128], F32, at=x_s_at)
        aw32B = alloc("aw32B", [128, DCH, TT], F32, at=xn32_at)
        aw32pB = alloc("aw32pB", [128, hch, 128], F32, at=xn32_at)
        wslB = alloc("wslB", [128, DCH, TT], F32, at=x_s_at + 16384)
        wsl2B = alloc("wsl2B", [128, hch, 128], F32, at=x_s_at + 16384)
        wr8 = alloc("wr8", [128, 2, DCH, TT], BF16, at=xb_at)
        tls = alloc("tls", [128, DCH, TT], F32, at=r2b_at)
    else:
        GTb = alloc("GTb", [NS, h], BF16)
        uT_b = alloc("uT_b", [NS, h], BF16)
        cb1 = alloc("cb1", [128, DCH, TT], BF16, at=xnb_at)
        cb2 = alloc("cb2", [128, DCH, TT], BF16, at=xn32_at)
        dbuf = alloc("dbuf", [128, DCH, TT], BF16, at=xn32_at + DCH * TT * 2)
        wsl = alloc("wsl", [128, DCH, TT], F32)
        wsl_at = cur[0] - DCH * TT * 4
        wsl2 = alloc("wsl2", [128, hch, 128], F32, at=wsl_at)
        aw32 = alloc("aw32", [128, DCH, TT], F32, at=x_s_at)
        aw32p = alloc("aw32p", [128, hch, 128], F32, at=x_s_at)
        aw32B = aw32
        aw32pB = aw32p
        wslB = alloc("wslB", [128, DCH, TT], F32, at=x_s_at + 16384)
        wsl2B = alloc("wsl2B", [128, hch, 128], F32, at=x_s_at + 16384)
        wr8 = alloc("wr8", [128, 2, DCH, TT], BF16, at=xb_at)
        tls = alloc("tls", [128, DCH, TT], F32, at=wsl_at)

    pb = [nc.alloc_psum_tensor(f"pb{i}", [128, TT], F32) for i in range(8)]

    sems = {}
    ctr = {}
    for name in ["s_x0", "s_x1", "s_cst", "s_out0", "s_out1",
                 "s_wc0", "s_wc1", "s_wp0", "s_wp1", "s_sc0", "s_sc1",
                 "s_wl0", "s_wl1", "s_sml", "s_T", "s_V", "s_S"]:
        sems[name] = nc.alloc_semaphore(name)
        ctr[name] = 0
    lastwait = {}

    def inc(ins, name, k=1):
        ins.then_inc(sems[name], k)
        ctr[name] += k
        return ctr[name]

    def wait(e, ename, name, val):
        if val is None or val <= 0:
            return
        key = (ename, name)
        if lastwait.get(key, 0) >= val:
            return
        e.wait_ge(sems[name], val)
        lastwait[key] = val

    blk = nc.Block()
    blk.__enter__()

    def wcmb_sem(hs):
        return "s_wc%d" % (hs % 2)

    def wcmb_ld(i, hs):
        return 16 * (i * (nhs // 2) + hs // 2 + 1)

    def wproj_sem(oc):
        return "s_wp%d" % (oc % 2)

    def wproj_ld(i, oc):
        return 16 * (i * 4 + oc // 2 + 1)
    wcmb_use = {}
    wproj_use = {}

    def x_sem(i):
        return "s_x%d" % (i % 2)

    def xin_count(i):
        return 16 * (i // 2 + 1)
    CST_ALL = 96

    def xin_ap(i):
        seq, q = i // tps, i % tps
        return xT[seq, :, q * TT:(q + 1) * TT].rearrange("(c p) t -> p c t", p=128)

    def yout_ap(i):
        seq, q = i // tps, i % tps
        return yT[seq, :, q * TT:(q + 1) * TT].rearrange("(c p) t -> p c t", p=128)

    def _loads(sy):
        for dst, src in [
            (A_s[:], avec), (bb_s[:], bb16), (bt_s[:], bt16),
            (gate_s[:], gate16), (bc_s[:], bc8), (bo_s[:], bo8), (dd_s[:], dd8),
        ]:
            inc(sy.dma_start(out=dst, in_=src), "s_sml", 16)
    blk.sync(_loads)
    sml_done = ctr["s_sml"]

    def _gx(g):
        inc(g.dma_start(out=uT_b[:], in_=uT), "s_cst", 16)
        inc(g.dma_start(out=v_b[:], in_=vmat.rearrange("n (c m) -> n c m", c=DCH)),
            "s_cst", 16)
        inc(g.dma_start(out=wcT_b[:], in_=wcT.rearrange("n (c m) -> n c m", c=DCH)),
            "s_cst", 16)
        inc(g.dma_start(out=wbtT_b[:], in_=wbtT.rearrange("(c p) m -> p c m", p=128)),
            "s_cst", 16)
        inc(g.dma_start(out=fbT_b[:], in_=fbm), "s_cst", 16)
        inc(g.dma_start(out=woT_b[:], in_=woT.rearrange("(c p) m -> p c m", p=128)),
            "s_cst", 16)
        assert ctr["s_cst"] == CST_ALL
    blk.gpsimd(_gx)

    def _memsets(ve):
        ve.memset(ones_b[:], 1.0)
        ve.memset(ones32[:], 1.0)
        wait(ve, "vector", "s_sml", sml_done)
        ve.tensor_scalar(out=A01[:], in0=A_s[:], scalar1=DELTA_MIN, scalar2=None, op0=ALU.mult)
        inc(ve.tensor_scalar(out=nbt[:], in0=bt_s[:], scalar1=-1.0, scalar2=None, op0=ALU.mult),
            "s_V")
    blk.vector(_memsets)

    def _a99(ve):
        wait(ve, "vector", "s_V", ctr["s_V"])
        inc(ve.tensor_scalar(out=A99[:], in0=A_s[:], scalar1=DELTA_SCALE,
                             scalar2=None, op0=ALU.mult), "s_V")
    blk.vector(_a99)
    memset_done = ctr["s_V"]

    ngt = h // TT
    gt_v0 = ctr["s_V"]

    def _gt_mm(te):
        wait(te, "tensor", "s_cst", CST_ALL)
        wait(te, "tensor", "s_V", memset_done)
        for j in range(ngt):
            if j >= 1:
                wait(te, "tensor", "s_V", gt_v0 + j)
            inc(te.matmul(pb[4][0:NS, :], fbT_b[:], uT_b[:, j * TT:(j + 1) * TT],
                          start=True, stop=True), "s_T")
    blk.tensor(_gt_mm)
    gt_t0 = ctr["s_T"] - ngt

    def _gt_evac(ve):
        for j in range(ngt):
            wait(ve, "vector", "s_T", gt_t0 + j + 1)
            inc(ve.tensor_scalar(out=GTb[:, j * TT:(j + 1) * TT], in0=pb[4][0:NS, :],
                                 scalar1=gate_s[:, 0:1], scalar2=None, op0=ALU.mult), "s_V")
    blk.vector(_gt_evac)
    gt_evac_done = ctr["s_V"]

    last_sv = 0
    def scw_sem(j):
        return "s_sc%d" % (j % 2)

    def scw_count(j):
        return 16 * (j // 2 + 1)

    summ_hist = {}
    cmp_hist = {}
    sc_hist = {}
    wadd_hist = {}
    wrp_hist = {}
    slice_c_done = {}
    pend = []

    def emit_fold(hs, wrp_done):
        LRB = [1, 2, 4, 5, 6, 7]
        mm_done = {}
        for dc in range(DCH):
            b = LRB[dc % 6]

            def _wlr(te, hs=hs, dc=dc, b=b):
                wait(te, "tensor", "s_cst", CST_ALL)
                if dc >= 6:
                    prev = wadd_hist.get((hs, dc - 6), 0)
                else:
                    pdc = dc + 6 if dc < 2 else dc
                    prev = wadd_hist.get((hs - 1, pdc), 0)
                wait(te, "tensor", "s_V", max(gt_evac_done, prev))
                mm_done[dc] = inc(
                    te.matmul(pb[b][:], v_b[:, dc], GTb[:, hs * TT:(hs + 1) * TT],
                              start=True, stop=True), "s_T")
            blk.tensor(_wlr)

            def _wadd(ve, hs=hs, dc=dc, b=b, wrp_done=wrp_done):
                wait(ve, "vector", "s_V", wrp_done)
                wait(ve, "vector", "s_T", mm_done[dc])
                inc(ve.tensor_tensor(out=wr8[:, hs % 2, dc], in0=wr8[:, hs % 2, dc],
                                     in1=pb[b][:], op=ALU.add), "s_V")
                wadd_hist[(hs, dc)] = ctr["s_V"]
            blk.vector(_wadd)
        slice_c_done[hs] = ctr["s_V"]

        def _wr(g, hs=hs):
            wait(g, "gpsimd", "s_V", slice_c_done[hs])
            assert inc(g.dma_start(out=wcmb_sc[hs], in_=wr8[:, hs % 2]),
                       scw_sem(hs), 16) == scw_count(hs)
        blk.gpsimd(_wr)

    def wldq(j):
        return "s_wl%d" % (j % 2)

    def wldc(j):
        return 16 * (j // 2 + 1)

    for hs in range(nhs):
        wv = wsl if hs % 2 == 0 else wslB

        def _ld(sy, hs=hs, wv=wv):
            if hs >= 2:
                wait(sy, "sync", "s_V", cmp_hist[hs - 2])
            inc(sy.dma_start(
                out=wv[:],
                in_=wfcT[:, hs * TT:(hs + 1) * TT].rearrange("(c p) h -> p c h", p=128),
            ), wldq(hs), 16)
        blk.sync(_ld)
        ld_done = ctr[wldq(hs)]

        awdb = 2 if aw32B is not aw32 else 1
        av = aw32 if hs % 2 == 0 else aw32B

        def _abs(sc, hs=hs, wv=wv, av=av, awdb=awdb, ld_done=ld_done):
            wait(sc, "scalar", wldq(hs), ld_done)
            if hs >= awdb:
                wait(sc, "scalar", "s_T", summ_hist[hs - awdb])
            inc(sc.activation(out=av[:], in_=wv[:], func=AF.Abs), "s_S")
        blk.scalar(_abs)
        abs_done = ctr["s_S"]

        def _summ(te, hs=hs, av=av, abs_done=abs_done):
            wait(te, "tensor", "s_S", abs_done)
            wait(te, "tensor", "s_V", sc_hist.get(hs - 1, 0))
            for dc in range(DCH):
                i2 = te.matmul(pb[0][:], ones32[:], av[:, dc],
                               start=(dc == 0), stop=(dc == DCH - 1))
            inc(i2, "s_T")
        blk.tensor(_summ)
        bc_done = summ_hist[hs] = ctr["s_T"]

        def _sc_evac(ve, hs=hs, bc_done=bc_done):
            wait(ve, "vector", "s_T", bc_done)
            wait(ve, "vector", "s_V", wrp_hist.get(hs - 1, 0))
            W = float(D)
            ve.tensor_scalar(out=scb16[:], in0=pb[0][:], scalar1=1.0 / W,
                             scalar2=1e-5, op0=ALU.mult, op1=ALU.max)
            ve.tensor_scalar(out=hfbc[:], in0=pb[0][:], scalar1=0.5 / W,
                             scalar2=0.5e-5, op0=ALU.mult, op1=ALU.max)
            inc(ve.tensor_scalar(out=nhfbc[:], in0=pb[0][:], scalar1=-0.5 / W,
                                 scalar2=-0.5e-5, op0=ALU.mult, op1=ALU.min), "s_V")
        blk.vector(_sc_evac)
        sc_done = sc_hist[hs] = ctr["s_V"]

        def _cmp(ve, hs=hs, wv=wv, ld_done=ld_done, sc_done=sc_done):
            wait(ve, "vector", "s_V", sc_done)
            wait(ve, "vector", wldq(hs), ld_done)
            if hs >= 2:
                wait(ve, "vector", scw_sem(hs - 2), scw_count(hs - 2))
            hf = hfbc[:, None, :].broadcast_to([128, DCH, TT])
            nhf = nhfbc[:, None, :].broadcast_to([128, DCH, TT])
            ve.tensor_tensor(out=cb1[:], in0=wv[:], in1=hf, op=ALU.is_gt)
            inc(ve.tensor_tensor(out=cb2[:], in0=wv[:], in1=nhf, op=ALU.is_lt),
                "s_V")
        blk.vector(_cmp)
        cmp_done = cmp_hist[hs] = ctr["s_V"]

        def _dsub(ve, cmp_done=cmp_done):
            wait(ve, "vector", "s_V", cmp_done)
            inc(ve.tensor_tensor(out=dbuf[:].rearrange("p c t -> p (c t)"),
                                 in0=cb1[:].rearrange("p c t -> p (c t)"),
                                 in1=cb2[:].rearrange("p c t -> p (c t)"),
                                 op=ALU.subtract), "s_V")
        blk.vector(_dsub)
        d_done = ctr["s_V"]

        def _wrp(ve, hs=hs, d_done=d_done):
            wait(ve, "vector", "s_V", d_done)
            sc = scb16[:, None, :].broadcast_to([128, DCH, TT])
            inc(ve.tensor_tensor(out=wr8[:, hs % 2], in0=dbuf[:], in1=sc,
                                 op=ALU.mult), "s_V")
        blk.vector(_wrp)
        wrp_done = wrp_hist[hs] = ctr["s_V"]

        pend.append((hs, wrp_done))
        if len(pend) > 1:
            emit_fold(*pend.pop(0))

    if pend:
        emit_fold(*pend.pop(0))
    last_sv = ctr["s_V"]

    for os_ in range(DCH):
        j = nhs + os_

        wv2 = wsl2 if j % 2 == 0 else wsl2B

        def _ld2(sy, j=j, wv2=wv2):
            wait(sy, "sync", "s_V", cmp_hist[j - 2])
            inc(sy.dma_start(
                out=wv2[:],
                in_=wprojT[:, os_ * 128:(os_ + 1) * 128].rearrange("(c p) o -> p c o", p=128),
            ), wldq(j), 16)
        blk.sync(_ld2)
        ld_done = ctr[wldq(j)]

        awdb = 2 if aw32pB is not aw32p else 1
        av2 = aw32p if j % 2 == 0 else aw32pB

        def _abs2(sc, j=j, wv2=wv2, av2=av2, awdb=awdb, ld_done=ld_done):
            wait(sc, "scalar", wldq(j), ld_done)
            wait(sc, "scalar", "s_T", summ_hist[j - awdb])
            inc(sc.activation(out=av2[:], in_=wv2[:], func=AF.Abs), "s_S")
        blk.scalar(_abs2)
        abs_done = ctr["s_S"]

        def _summ2(te, j=j, av2=av2, abs_done=abs_done):
            wait(te, "tensor", "s_S", abs_done)
            wait(te, "tensor", "s_V", sc_hist.get(j - 1, 0))
            for kc in range(hch):
                i2 = te.matmul(pb[0][:, 0:128], ones32[:], av2[:, kc],
                               start=(kc == 0), stop=(kc == hch - 1))
            inc(i2, "s_T")
        blk.tensor(_summ2)
        bc_done = summ_hist[j] = ctr["s_T"]

        def _sc_evac2(ve, j=j, bc_done=bc_done):
            wait(ve, "vector", "s_T", bc_done)
            wait(ve, "vector", "s_V", wrp_hist.get(j - 1, 0))
            W = float(h)
            ve.tensor_scalar(out=scb16[:, 0:128], in0=pb[0][:, 0:128], scalar1=1.0 / W,
                             scalar2=1e-5, op0=ALU.mult, op1=ALU.max)
            ve.tensor_scalar(out=hfbc[:, 0:128], in0=pb[0][:, 0:128], scalar1=0.5 / W,
                             scalar2=0.5e-5, op0=ALU.mult, op1=ALU.max)
            inc(ve.tensor_scalar(out=nhfbc[:, 0:128], in0=pb[0][:, 0:128],
                                 scalar1=-0.5 / W, scalar2=-0.5e-5,
                                 op0=ALU.mult, op1=ALU.min), "s_V")
        blk.vector(_sc_evac2)
        neg_done = sc_hist[j] = ctr["s_V"]

        def _cmp2(ve, j=j, wv2=wv2, ld_done=ld_done, neg_done=neg_done):
            wait(ve, "vector", "s_V", neg_done)
            wait(ve, "vector", wldq(j), ld_done)
            if j >= 2:
                wait(ve, "vector", scw_sem(j - 2), scw_count(j - 2))
            hf = hfbc[:, None, 0:128].broadcast_to([128, hch, 128])
            nhf = nhfbc[:, None, 0:128].broadcast_to([128, hch, 128])
            cb1v = cb1[:].rearrange("p c (a b) -> p (c a) b", b=128)[:, 0:hch, :]
            cb2v = cb2[:].rearrange("p c (a b) -> p (c a) b", b=128)[:, 0:hch, :]
            ve.tensor_tensor(out=cb1v, in0=wv2[:], in1=hf, op=ALU.is_gt)
            inc(ve.tensor_tensor(out=cb2v, in0=wv2[:], in1=nhf, op=ALU.is_lt), "s_V")
        blk.vector(_cmp2)
        cmp_done = cmp_hist[j] = ctr["s_V"]

        def _dsub2(ve, cmp_done=cmp_done):
            wait(ve, "vector", "s_V", cmp_done)
            inc(ve.tensor_tensor(
                out=dbuf[:].rearrange("p c (a b) -> p (c a) b", b=128)[:, 0:hch, :],
                in0=cb1[:].rearrange("p c (a b) -> p (c a) b", b=128)[:, 0:hch, :],
                in1=cb2[:].rearrange("p c (a b) -> p (c a) b", b=128)[:, 0:hch, :],
                op=ALU.subtract), "s_V")
        blk.vector(_dsub2)
        d_done = ctr["s_V"]

        def _wrp2(ve, os_=os_, j=j, d_done=d_done):
            wait(ve, "vector", "s_V", d_done)
            sc = scb16[:, None, 0:128].broadcast_to([128, hch, 128])
            inc(ve.tensor_tensor(
                out=wr8[:, j % 2].rearrange("p c (a b) -> p (c a) b", b=128)[:, 0:hch, :],
                in0=dbuf[:].rearrange("p c (a b) -> p (c a) b", b=128)[:, 0:hch, :],
                in1=sc, op=ALU.mult), "s_V")
        blk.vector(_wrp2)
        wrp_hist[j] = ctr["s_V"]
        slice2_c_done = slice2_c_done if os_ else {}
        slice2_c_done[os_] = ctr["s_V"]

        def _wr2(g, os_=os_, j=j):
            wait(g, "gpsimd", "s_V", slice2_c_done[os_])
            assert inc(g.dma_start(
                out=wproj_sc[os_],
                in_=wr8[:, j % 2].rearrange("p c (a b) -> p (c a) b", b=128)[:, 0:hch, :]),
                scw_sem(j), 16) == scw_count(j)
        blk.gpsimd(_wr2)
        last_sv = ctr["s_V"]

    sc0_total, sc1_total = ctr["s_sc0"], ctr["s_sc1"]
    tern_sv_total = ctr["s_V"]
    sums_total = ctr["s_T"]

    def _gx2(g):
        wait(g, "gpsimd", "s_T", summ_hist[nhs + DCH - 1])
        wait(g, "gpsimd", "s_V", cmp_hist[nhs + DCH - 1])
        inc(g.dma_start(out=x_s[:, 0], in_=xin_ap(0)), "s_x0", 16)
        inc(g.dma_start(out=x_s[:, 1], in_=xin_ap(1)), "s_x1", 16)
    blk.gpsimd(_gx2)

    mA, uA, auxA, rstdA = (statA[:, k] for k in range(4))
    qA = mA
    mB, uB, auxB, rstdB = (statB[:, k] for k in range(4))
    qB = mB

    hb_done = {}
    butp_done = {}
    ln1c_done = {}
    ln2_done = {}
    yb_done = {}
    wc_mm = {}
    wo_mm = {}
    wo_stt_done = {}
    final_done = {}
    out_done = {}
    r2_done = {}
    ffa_last = {}
    sq_chunks = {}
    ffb_last = {}
    run = {"statsv1": 0, "wo_stt": 0, "final": 0, "yb": 0, "tmp2": 0,
           "tls": 0, "nz": 0, "relu": 0}

    def emit_front1(j):
        p = j % 2
        src = x_s[:, p]

        def _sq(ve, j=j):
            if j == 0:
                wait(ve, "vector", "s_V", tern_sv_total)
                wait(ve, "vector", "s_sc0", sc0_total)
                wait(ve, "vector", "s_sc1", sc1_total)
            if j >= 2:
                wait(ve, "vector", "s_T", ffa_last[j - 2])
            wait(ve, "vector", x_sem(j), xin_count(j))
            ve.tensor_tensor(out=xnb[:, p], in0=src, in1=src, op=ALU.mult)
            inc(ve.tensor_copy(out=xb[:], in_=src), "s_V")
        blk.vector(_sq)
        sq_done = ctr["s_V"]

        def _stats1(te, sq_done=sq_done):
            wait(te, "tensor", "s_V", sq_done)
            for dc in range(DCH):
                te.matmul(pb[0][:], ones_b[:], xb[:, dc],
                          start=(dc == 0), stop=(dc == DCH - 1))
            for dc in range(DCH):
                i2 = te.matmul(pb[1][:], ones_b[:], xnb[:, p, dc],
                               start=(dc == 0), stop=(dc == DCH - 1))
            inc(i2, "s_T")
        blk.tensor(_stats1)
        st_done = ctr["s_T"]

        def _sv1(ve, st_done=st_done):
            wait(ve, "vector", "s_T", st_done)
            ve.tensor_scalar(out=mA, in0=pb[0][:], scalar1=1.0 / D,
                             scalar2=None, op0=ALU.mult)
            inc(ve.tensor_scalar(out=uA, in0=pb[1][:], scalar1=1.0 / D,
                                 scalar2=EPS, op0=ALU.mult, op1=ALU.add), "s_V")
        blk.vector(_sv1)
        run["statsv1"] = mu = ctr["s_V"]

        def _msq(ve, mu=mu):
            wait(ve, "vector", "s_V", mu)
            inc(ve.tensor_tensor(out=auxA, in0=mA, in1=mA, op=ALU.mult), "s_V")
        blk.vector(_msq)
        ms = ctr["s_V"]

        def _varr(ve, ms=ms):
            wait(ve, "vector", "s_V", ms)
            inc(ve.tensor_tensor(out=uA, in0=uA, in1=auxA, op=ALU.subtract), "s_V")
        blk.vector(_varr)
        va = ctr["s_V"]

        def _lnv(sc, va=va):
            wait(sc, "scalar", "s_V", va)
            inc(sc.activation(out=auxA, in_=uA, func=AF.Ln), "s_S")
        blk.scalar(_lnv)
        lv = ctr["s_S"]

        def _nh(ve, lv=lv):
            wait(ve, "vector", "s_S", lv)
            inc(ve.tensor_scalar(out=auxA, in0=auxA, scalar1=-0.5,
                                 scalar2=None, op0=ALU.mult), "s_V")
        blk.vector(_nh)
        nh = ctr["s_V"]

        def _rstd(sc, nh=nh):
            wait(sc, "scalar", "s_V", nh)
            inc(sc.activation(out=rstdA, in_=auxA, func=AF.Exp), "s_S")
        blk.scalar(_rstd)
        rs = ctr["s_S"]

        return rs

    def emit_front1b(j, rs, ffb_guard):
        p = j % 2
        src = x_s[:, p]

        def _m1(ve, j=j, rs=rs, ffb_guard=ffb_guard):
            wait(ve, "vector", "s_S", rs)
            wait(ve, "vector", "s_V", run["tls"])
            wait(ve, "vector", "s_T", ffb_guard)
            ve.tensor_tensor(out=qA, in0=mA, in1=rstdA, op=ALU.mult)
            last = None
            for dc in range(DCH):
                last = ve.tensor_tensor(out=tls[:, dc], in0=src[:, dc],
                                        in1=rstdA, op=ALU.mult)
            inc(last, "s_V")
        blk.vector(_m1)
        md = ctr["s_V"]

        def _s1(ve, j=j, md=md):
            wait(ve, "vector", "s_V", md)
            last = None
            for dc in range(DCH):
                last = ve.tensor_tensor(out=xnb[:, p, dc], in0=tls[:, dc],
                                        in1=qA, op=ALU.subtract)
            inc(last, "s_V")
        blk.vector(_s1)
        ln1c_done[j] = run["tls"] = ctr["s_V"]

    def emit_front2(j):
        p = j % 2
        seq_start = (j % tps) == 0

        def _butp(te, j=j):
            wait(te, "tensor", "s_V", ln1c_done[j])
            for dc in range(DCH):
                te.matmul(pb[2][0:NS, :], wbtT_b[:, dc, 0:NS], xnb[:, p, dc],
                          start=(dc == 0), stop=(dc == DCH - 1))
            for dc in range(DCH):
                i2 = te.matmul(pb[3][0:NS, :], wbtT_b[:, dc, NS:2 * NS], xnb[:, p, dc],
                               start=(dc == 0), stop=(dc == DCH - 1))
            inc(i2, "s_T")
        blk.tensor(_butp)
        butp_done[j] = bp = ctr["s_T"]

        def _nz(ve, bp=bp):
            wait(ve, "vector", "s_T", bp)
            ve.tensor_scalar(out=e1t[:], in0=pb[3][0:NS, :], scalar1=-1.0,
                             scalar2=nbt[:, 0:1], op0=ALU.mult, op1=ALU.add)
            inc(ve.tensor_scalar(out=but[:], in0=pb[2][0:NS, :],
                                 scalar1=bb_s[:, 0:1], scalar2=None, op0=ALU.add), "s_V")
        blk.vector(_nz)
        nz = run["nz"] = ctr["s_V"]

        def _e1(sc, nz=nz):
            wait(sc, "scalar", "s_V", nz)
            inc(sc.activation(out=e1t[:], in_=e1t[:], func=AF.Exp), "s_S")
        blk.scalar(_e1)
        e1 = ctr["s_S"]

        def _e1p(ve, e1=e1):
            wait(ve, "vector", "s_S", e1)
            inc(ve.tensor_scalar(out=e1t[:], in0=e1t[:], scalar1=1.0,
                                 scalar2=None, op0=ALU.add), "s_V")
        blk.vector(_e1p)
        ep = ctr["s_V"]

        def _rsig(ve, ep=ep):
            wait(ve, "vector", "s_V", ep)
            inc(ve.reciprocal(out=rsig[:], in_=e1t[:]), "s_V")
        blk.vector(_rsig)
        rg = ctr["s_V"]

        def _zt(ve, rg=rg):
            wait(ve, "vector", "s_V", rg)
            inc(ve.tensor_scalar(out=rsig[:], in0=rsig[:], scalar1=A99[:, 0:1],
                                 scalar2=A01[:, 0:1], op0=ALU.mult, op1=ALU.add), "s_V")
        blk.vector(_zt)
        zt = ctr["s_V"]

        def _aext(sc, zt=zt):
            wait(sc, "scalar", "s_V", zt)
            inc(sc.activation(out=aext[:], in_=rsig[:], func=AF.Exp), "s_S")
        blk.scalar(_aext)
        ax = ctr["s_S"]

        def _scan(ve, j=j, seq_start=seq_start, ax=ax):
            wait(ve, "vector", "s_S", ax)
            init = 0.0 if seq_start else carry[:, 0:1]
            inc(ve.tensor_tensor_scan(out=ht[:], data0=aext[:], data1=but[:],
                                      initial=init, op0=ALU.mult, op1=ALU.add), "s_V")
        blk.vector(_scan)
        sn = ctr["s_V"]

        def _hb(ve, j=j, sn=sn):
            wait(ve, "vector", "s_V", sn)
            ve.tensor_copy(out=carry[:], in_=ht[:, TT - 1:TT])
            inc(ve.tensor_copy(out=hbt[:], in_=ht[:]), "s_V")
        blk.vector(_hb)
        hb_done[j] = ctr["s_V"]

    def emit_back1(j):
        p = j % 2
        wcd = {}
        V0 = ctr["s_V"]

        def _wc(te, j=j, V0=V0):
            wait(te, "tensor", "s_V", hb_done[j])
            wait(te, "tensor", "s_V", run["statsv1"])
            for dc in range(DCH):
                b = dc % 2
                if dc >= 2:
                    wait(te, "tensor", "s_V", V0 + ((dc - 2) // 2 + 1) * 2)
                wcd[dc] = inc(
                    te.matmul(pb[b][:], wcT_b[:, dc], hbt[:], start=True, stop=True), "s_T")
        blk.tensor(_wc)
        wc_mm[j] = dict(wcd)

        prev_ybp = 0
        for pp in range(DCH // 2):
            def _t2(ve, j=j, pp=pp, prev_ybp=prev_ybp):
                if pp:
                    wait(ve, "vector", "s_V", prev_ybp)
                else:
                    wait(ve, "vector", "s_V", run["tmp2"])
                ve.tensor_scalar(out=tmp2[:, 0], in0=xnb[:, p, 2 * pp],
                                 scalar1=dd_s[:, 2 * pp:2 * pp + 1], scalar2=None,
                                 op0=ALU.mult)
                inc(ve.tensor_scalar(out=tmp2[:, 1], in0=xnb[:, p, 2 * pp + 1],
                                     scalar1=dd_s[:, 2 * pp + 1:2 * pp + 2], scalar2=None,
                                     op0=ALU.mult), "s_V")
            blk.vector(_t2)
            t2d = ctr["s_V"]

            def _ybp(ve, j=j, pp=pp, t2d=t2d):
                wait(ve, "vector", "s_V", t2d)
                for q in range(2):
                    dc = 2 * pp + q
                    wait(ve, "vector", "s_T", wcd[dc])
                    ins = ve.scalar_tensor_tensor(out=yb[:, dc], in0=pb[dc % 2][:],
                                                  scalar=bc_s[:, dc:dc + 1],
                                                  in1=tmp2[:, q],
                                                  op0=ALU.add, op1=ALU.add)
                inc(ins, "s_V")
            blk.vector(_ybp)
            prev_ybp = ctr["s_V"]
        yb_done[j] = run["yb"] = run["tmp2"] = ctr["s_V"]

        wod = {}
        V1 = ctr["s_V"]

        def _wo(te, j=j, V1=V1):
            wait(te, "tensor", "s_V", yb_done[j])
            for oc in range(DCH):
                b = 4 + oc % 2
                if oc >= 2:
                    wait(te, "tensor", "s_V", V1 + oc - 1)
                for kc in range(DCH):
                    i2 = te.matmul(pb[b][:], woT_b[:, kc, oc * 128:(oc + 1) * 128],
                                   yb[:, kc], start=(kc == 0), stop=(kc == DCH - 1))
                wod[oc] = inc(i2, "s_T")
        blk.tensor(_wo)
        wo_mm[j] = dict(wod)

        def _wo_stt(ve, j=j):
            if j >= 2:
                wait(ve, "vector", "s_out%d" % (j % 2), 16 * ((j - 2) // 2 + 1))
            for oc in range(DCH):
                b = 4 + oc % 2
                wait(ve, "vector", "s_T", wod[oc])
                inc(ve.scalar_tensor_tensor(out=xn32[:, p, oc], in0=pb[b][:],
                                            scalar=bo_s[:, oc:oc + 1],
                                            in1=x_s[:, p, oc],
                                            op0=ALU.add, op1=ALU.add), "s_V")
        blk.vector(_wo_stt)
        wo_stt_done[j] = run["wo_stt"] = ws = ctr["s_V"]

        if j + 2 < nt:
            def _pref(g, j=j, ws=ws):
                wait(g, "gpsimd", "s_V", ws)
                assert inc(g.dma_start(out=x_s[:, j % 2], in_=xin_ap(j + 2)),
                           x_sem(j), 16) == xin_count(j + 2)
            blk.gpsimd(_pref)

        def _sqln2(ve, j=j, ws=ws):
            wait(ve, "vector", "s_V", ws)
            ve.tensor_tensor(out=xnb[:, p], in0=xn32[:, p], in1=xn32[:, p],
                             op=ALU.mult)
            inc(ve.tensor_copy(out=yb[:], in_=xn32[:, p]), "s_V")
        blk.vector(_sqln2)
        return ctr["s_V"]

    def emit_back2(j, sq2_done):
        p = j % 2

        def _stats2(te, j=j, sq2_done=sq2_done):
            wait(te, "tensor", "s_V", sq2_done)
            wait(te, "tensor", "s_S", run["relu"])
            for dc in range(DCH):
                te.matmul(pb[2][:], ones_b[:], yb[:, dc],
                          start=(dc == 0), stop=(dc == DCH - 1))
            for dc in range(DCH):
                i2 = te.matmul(pb[3][:], ones_b[:], xnb[:, p, dc],
                               start=(dc == 0), stop=(dc == DCH - 1))
            inc(i2, "s_T")
        blk.tensor(_stats2)
        st = ctr["s_T"]

        def _sv2(ve, st=st):
            wait(ve, "vector", "s_T", st)
            ve.tensor_scalar(out=mB, in0=pb[2][:], scalar1=1.0 / D,
                             scalar2=None, op0=ALU.mult)
            inc(ve.tensor_scalar(out=uB, in0=pb[3][:], scalar1=1.0 / D,
                                 scalar2=EPS, op0=ALU.mult, op1=ALU.add), "s_V")
        blk.vector(_sv2)
        mu = ctr["s_V"]

        def _msq2(ve, mu=mu):
            wait(ve, "vector", "s_V", mu)
            inc(ve.tensor_tensor(out=auxB, in0=mB, in1=mB, op=ALU.mult), "s_V")
        blk.vector(_msq2)
        ms = ctr["s_V"]

        def _varr2(ve, ms=ms):
            wait(ve, "vector", "s_V", ms)
            inc(ve.tensor_tensor(out=uB, in0=uB, in1=auxB, op=ALU.subtract), "s_V")
        blk.vector(_varr2)
        va = ctr["s_V"]

        def _lnv2(sc, va=va):
            wait(sc, "scalar", "s_V", va)
            inc(sc.activation(out=auxB, in_=uB, func=AF.Ln), "s_S")
        blk.scalar(_lnv2)
        lv = ctr["s_S"]

        def _nh2(ve, lv=lv):
            wait(ve, "vector", "s_S", lv)
            inc(ve.tensor_scalar(out=auxB, in0=auxB, scalar1=-0.5,
                                 scalar2=None, op0=ALU.mult), "s_V")
        blk.vector(_nh2)
        nh = ctr["s_V"]

        def _rstd2(sc, nh=nh):
            wait(sc, "scalar", "s_V", nh)
            inc(sc.activation(out=rstdB, in_=auxB, func=AF.Exp), "s_S")
        blk.scalar(_rstd2)
        rs = ctr["s_S"]

        first = True
        prev_s2 = 0
        for pp in range(DCH // 2):
            def _m2(ve, j=j, pp=pp, rs=rs, first=first, prev_s2=prev_s2):
                if first:
                    wait(ve, "vector", "s_S", rs)
                    wait(ve, "vector", "s_V", run["tmp2"])
                    ve.tensor_tensor(out=qB, in0=mB, in1=rstdB, op=ALU.mult)
                else:
                    wait(ve, "vector", "s_V", prev_s2)
                ve.tensor_tensor(out=tmp2[:, 0], in0=xn32[:, p, 2 * pp],
                                 in1=rstdB, op=ALU.mult)
                inc(ve.tensor_tensor(out=tmp2[:, 1], in0=xn32[:, p, 2 * pp + 1],
                                     in1=rstdB, op=ALU.mult), "s_V")
            blk.vector(_m2)
            md = ctr["s_V"]

            def _s2(ve, j=j, pp=pp, md=md):
                wait(ve, "vector", "s_V", md)
                ve.tensor_tensor(out=xnb[:, p, 2 * pp], in0=tmp2[:, 0], in1=qB,
                                 op=ALU.subtract)
                inc(ve.tensor_tensor(out=xnb[:, p, 2 * pp + 1], in0=tmp2[:, 1],
                                     in1=qB, op=ALU.subtract), "s_V")
            blk.vector(_s2)
            prev_s2 = ctr["s_V"]
            first = False
        ln2_done[j] = run["tmp2"] = ctr["s_V"]

    def emit_ffnA(i):
        p = i % 2
        fad = {}
        S0 = ctr["s_S"]

        def _ffa(te, i=i, S0=S0):
            wait(te, "tensor", "s_V", ln2_done[i])
            wait(te, "tensor", "s_V", run["nz"])
            for hc in range(hch):
                hs = hc // 4
                wait(te, "tensor", wcmb_sem(hs), wcmb_ld(i, hs))
                if hc >= 2:
                    wait(te, "tensor", "s_S", S0 + hc - 1)
                b = 2 + hc % 2
                for dc in range(DCH):
                    i2 = te.matmul(pb[b][:],
                                   wcmb_st[:, hs % 2, dc, (hc % 4) * 128:(hc % 4 + 1) * 128],
                                   xnb[:, p, dc], start=(dc == 0), stop=(dc == DCH - 1))
                fad[hc] = inc(i2, "s_T")
                if hc % 4 == 3:
                    wcmb_use[(i, hc // 4)] = ctr["s_T"]
        blk.tensor(_ffa)
        ffa_last[i] = ctr["s_T"]

        if nhs > 2:
            def _wsB(sy, i=i):
                for hs in range(2, nhs):
                    wait(sy, "sync", "s_T", wcmb_use[(i, hs - 2)])
                    assert inc(sy.dma_start(out=wcmb_st[:, hs % 2], in_=wcmb_sc[hs]),
                               wcmb_sem(hs), 16) == wcmb_ld(i, hs)
            blk.sync(_wsB)

        def _relu(sc, S0=S0):
            for hc in range(hch):
                b = 2 + hc % 2
                wait(sc, "scalar", "s_T", fad[hc])
                inc(sc.activation(out=r2b[:, hc], in_=pb[b][:], func=AF.Relu), "s_S")
        blk.scalar(_relu)
        run["relu"] = ctr["s_S"]

        QS = max(1, hch // 4)
        sqc = {}

        def _sq2(ve, S0=S0):
            for qq in range(hch // QS):
                wait(ve, "vector", "s_S", S0 + (qq + 1) * QS)
                inc(ve.tensor_tensor(out=r2b[:, qq * QS:(qq + 1) * QS],
                                     in0=r2b[:, qq * QS:(qq + 1) * QS],
                                     in1=r2b[:, qq * QS:(qq + 1) * QS],
                                     op=ALU.mult), "s_V")
                sqc[qq] = ctr["s_V"]
        blk.vector(_sq2)
        r2_done[i] = ctr["s_V"]
        sq_chunks[i] = dict(sqc)

    def emit_ffnB(i):
        p = i % 2
        fbd = {}
        V0 = ctr["s_V"]

        QS = max(1, hch // 4)

        def _ffb(te, i=i, V0=V0):
            for oc in range(DCH):
                wait(te, "tensor", wproj_sem(oc), wproj_ld(i, oc))
                if oc >= 2:
                    wait(te, "tensor", "s_V", V0 + oc - 1)
                b = 6 + oc % 2
                for kc in range(hch):
                    if oc == 0 and kc % QS == 0:
                        wait(te, "tensor", "s_V", sq_chunks[i][kc // QS])
                    i2 = te.matmul(pb[b][:], wproj_st[:, oc % 2, kc, :],
                                   r2b[:, kc], start=(kc == 0), stop=(kc == hch - 1))
                fbd[oc] = inc(i2, "s_T")
                wproj_use[(i, oc)] = ctr["s_T"]
        blk.tensor(_ffb)
        ffb_last[i] = ctr["s_T"]

        def _wsC(sy, i=i):
            for oc in range(2, DCH):
                wait(sy, "sync", "s_T", wproj_use[(i, oc - 2)])
                assert inc(sy.dma_start(out=wproj_st[:, oc % 2], in_=wproj_sc[oc]),
                           wproj_sem(oc), 16) == wproj_ld(i, oc)
        blk.sync(_wsC)

        def _final(ve, i=i, p=p):
            for oc in range(DCH):
                b = 6 + oc % 2
                wait(ve, "vector", "s_T", fbd[oc])
                inc(ve.tensor_tensor(out=xn32[:, p, oc], in0=xn32[:, p, oc],
                                     in1=pb[b][:], op=ALU.add), "s_V")
        blk.vector(_final)
        final_done[i] = run["final"] = ctr["s_V"]

        def _outd(g, i=i, p=p):
            wait(g, "gpsimd", "s_V", final_done[i])
            out_done[i] = inc(g.dma_start(out=yout_ap(i), in_=xn32[:, p]),
                              "s_out%d" % (i % 2), 16)
        blk.gpsimd(_outd)

        if i + 1 < nt:
            def _wsA(sy, i=i):
                for hs in range(2):
                    wait(sy, "sync", "s_T", wcmb_use[(i, hs + nhs - 2)])
                    assert inc(sy.dma_start(out=wcmb_st[:, hs], in_=wcmb_sc[hs]),
                               wcmb_sem(hs), 16) == wcmb_ld(i + 1, hs)
                for oc in range(2):
                    wait(sy, "sync", "s_T", wproj_use[(i, oc + 6)])
                    assert inc(sy.dma_start(out=wproj_st[:, oc], in_=wproj_sc[oc]),
                               wproj_sem(oc), 16) == wproj_ld(i + 1, oc)
            blk.sync(_wsA)

    def _wsA0(sy):
        wait(sy, "sync", "s_sc0", sc0_total)
        wait(sy, "sync", "s_sc1", sc1_total)
        for hs in range(2):
            assert inc(sy.dma_start(out=wcmb_st[:, hs], in_=wcmb_sc[hs]),
                       wcmb_sem(hs), 16) == wcmb_ld(0, hs)
        for oc in range(2):
            assert inc(sy.dma_start(out=wproj_st[:, oc], in_=wproj_sc[oc]),
                       wproj_sem(oc), 16) == wproj_ld(0, oc)
    blk.sync(_wsA0)

    emit_front1b(0, emit_front1(0), 0)
    emit_front2(0)
    s2 = emit_back1(0)
    emit_back2(0, s2)
    emit_front1b(1, emit_front1(1), 0)
    emit_front2(1)

    for i in range(nt):
        emit_ffnA(i)
        rs1 = emit_front1(i + 2) if i + 2 < nt else None
        if i + 1 < nt:
            s2 = emit_back1(i + 1)
            emit_back2(i + 1, s2)
        emit_ffnB(i)
        if i + 2 < nt:
            emit_front1b(i + 2, rs1, ffb_last[i])
            emit_front2(i + 2)


    def _fin(sy):
        wait(sy, "sync", "s_out0", ctr["s_out0"])
        wait(sy, "sync", "s_out1", ctr["s_out1"])
    blk.sync(_fin)

    blk.__exit__(None, None, None)
    return nc


_NC_CACHE = None


def make_in_maps(x, An, W_b, b_b, W_c, b_c, Dg, W_t, b_t, W_o, b_o,
                 fast_b, w_fc, w_proj, u, v, gate, spc=SEQ_PER_CORE):
    shared = dict(
        wfcT=np.ascontiguousarray(w_fc.T),
        wprojT=np.ascontiguousarray(w_proj.T),
        woT=np.ascontiguousarray(W_o.T),
        wbtT=np.ascontiguousarray(
            np.concatenate([W_b.T, np.repeat(W_t.T, NS, axis=1)], axis=1)),
        wcT=np.ascontiguousarray(W_c.T),
        vmat=np.ascontiguousarray(v),
        uT=np.ascontiguousarray(u.T),
        fbm=np.ascontiguousarray(fast_b),
        avec=np.ascontiguousarray(An.reshape(NS, 1)),
        bb16=np.ascontiguousarray(b_b.reshape(NS, 1)),
        bt16=np.ascontiguousarray(np.broadcast_to(b_t.reshape(1, 1), (NS, 1))),
        gate16=np.ascontiguousarray(np.broadcast_to(gate.reshape(1, 1), (NS, 1))),
        bc8=np.ascontiguousarray(b_c.reshape(DCH, 128).T),
        bo8=np.ascontiguousarray(b_o.reshape(DCH, 128).T),
        dd8=np.ascontiguousarray(Dg.reshape(DCH, 128).T),
    )
    ncores = x.shape[0] // spc
    in_maps = []
    for c in range(ncores):
        m = dict(shared)
        m["xT"] = np.ascontiguousarray(
            x[c * spc:(c + 1) * spc].transpose(0, 2, 1))
        in_maps.append(m)
    return in_maps


def kernel(L0_attn_x, L0_attn_A, L0_attn_W_b, L0_attn_b_b, L0_attn_W_c,
           L0_attn_b_c, L0_attn_D, L0_attn_W_t, L0_attn_b_t, L0_attn_W_o,
           L0_attn_b_o, L0_ffn_fast_b, L0_ffn_w_fc, L0_ffn_w_proj, L0_ffn_u,
           L0_ffn_v, L0_ffn_gate):
    global _NC_CACHE
    in_maps = make_in_maps(
        np.asarray(L0_attn_x, dtype=np.float32),
        np.asarray(L0_attn_A, dtype=np.float32),
        np.asarray(L0_attn_W_b, dtype=np.float32),
        np.asarray(L0_attn_b_b, dtype=np.float32),
        np.asarray(L0_attn_W_c, dtype=np.float32),
        np.asarray(L0_attn_b_c, dtype=np.float32),
        np.asarray(L0_attn_D, dtype=np.float32),
        np.asarray(L0_attn_W_t, dtype=np.float32),
        np.asarray(L0_attn_b_t, dtype=np.float32),
        np.asarray(L0_attn_W_o, dtype=np.float32),
        np.asarray(L0_attn_b_o, dtype=np.float32),
        np.asarray(L0_ffn_fast_b, dtype=np.float32),
        np.asarray(L0_ffn_w_fc, dtype=np.float32),
        np.asarray(L0_ffn_w_proj, dtype=np.float32),
        np.asarray(L0_ffn_u, dtype=np.float32),
        np.asarray(L0_ffn_v, dtype=np.float32),
        np.asarray(L0_ffn_gate, dtype=np.float32),
    )

    if _NC_CACHE is None:
        _NC_CACHE = build_nc()
    res = run_bass_kernel_spmd(_NC_CACHE, in_maps, core_ids=list(range(NCORES)))

    out = np.empty((BATCH, S, D), dtype=np.float32)
    for c in range(NCORES):
        out[c * SEQ_PER_CORE:(c + 1) * SEQ_PER_CORE] = \
            res.results[c]["yT"].transpose(0, 2, 1)
    return out


# version 19
# speedup vs baseline: 1.0078x; 1.0078x over previous
import numpy as np
import concourse.bass as bass
import concourse.mybir as mybir
from concourse.bass_utils import run_bass_kernel_spmd

F32 = mybir.dt.float32
BF16 = mybir.dt.bfloat16
AF = mybir.ActivationFunctionType
ALU = mybir.AluOpType

D = 1024
NS = 16
H = 4096
S = 2048
BATCH = 16
NCORES = 8
SEQ_PER_CORE = BATCH // NCORES
TT = 512
TILES_PER_SEQ = S // TT
DCH = D // 128
EPS = 1e-5
DELTA_SCALE = 0.99
DELTA_MIN = 0.01
SBUF_BASE = 16512


def build_nc(spc=SEQ_PER_CORE, tps=TILES_PER_SEQ, h=H):
    nt = spc * tps
    hch = h // 128
    nhs = h // TT
    s_len = tps * TT
    nsl_fc = h // 128
    nsl_pj = D // 128

    nc = bass.Bass("TRN2", target_bir_lowering=False)

    xT = nc.dram_tensor("xT", [spc, D, s_len], F32, kind="ExternalInput").ap()
    yT = nc.dram_tensor("yT", [spc, D, s_len], F32, kind="ExternalOutput").ap()
    wfcT = nc.dram_tensor("wfcT", [D, h], F32, kind="ExternalInput").ap()
    wprojT = nc.dram_tensor("wprojT", [h, D], F32, kind="ExternalInput").ap()
    woT = nc.dram_tensor("woT", [D, D], F32, kind="ExternalInput").ap()
    wbtT = nc.dram_tensor("wbtT", [D, 32], F32, kind="ExternalInput").ap()
    wcT = nc.dram_tensor("wcT", [NS, D], F32, kind="ExternalInput").ap()
    vmat = nc.dram_tensor("vmat", [NS, D], F32, kind="ExternalInput").ap()
    uT = nc.dram_tensor("uT", [NS, h], F32, kind="ExternalInput").ap()
    fbm = nc.dram_tensor("fbm", [NS, NS], F32, kind="ExternalInput").ap()
    avec = nc.dram_tensor("avec", [NS, 1], F32, kind="ExternalInput").ap()
    bb16 = nc.dram_tensor("bb16", [NS, 1], F32, kind="ExternalInput").ap()
    bt16 = nc.dram_tensor("bt16", [NS, 1], F32, kind="ExternalInput").ap()
    gate16 = nc.dram_tensor("gate16", [NS, 1], F32, kind="ExternalInput").ap()
    bc8 = nc.dram_tensor("bc8", [128, DCH], F32, kind="ExternalInput").ap()
    bo8 = nc.dram_tensor("bo8", [128, DCH], F32, kind="ExternalInput").ap()
    dd8 = nc.dram_tensor("dd8", [128, DCH], F32, kind="ExternalInput").ap()

    wcmb_sc = nc.dram_tensor("wcmb_sc", [nhs, 128, DCH, TT], BF16, kind="Internal").ap()
    wproj_sc = nc.dram_tensor("wproj_sc", [DCH, 128, hch, 128], BF16, kind="Internal").ap()

    cur = [SBUF_BASE]

    def alloc(name, shape, dtype, at=None):
        import math
        size = math.prod(shape[1:]) * mybir.dt.size(dtype)
        size = (size + 63) // 64 * 64
        if at is None:
            at = cur[0]
            cur[0] += size
        else:
            assert at + size <= nc.SBUF_PARTITION_SIZE_BYTES, (name, at, size)
        return nc.alloc_sbuf_tensor_at(name, shape, dtype, offset=at)

    x_s = alloc("x_s", [128, 2, DCH, TT], F32)
    x_s_at = cur[0] - 2 * DCH * TT * 4
    xn32 = alloc("xn32", [128, 2, DCH, TT], F32)
    xn32_at = cur[0] - 2 * DCH * TT * 4
    xb = alloc("xb", [128, DCH, TT], BF16)
    xb_at = cur[0] - DCH * TT * 2
    yb = alloc("yb", [128, DCH, TT], BF16)
    xnb = alloc("xnb", [128, 2, DCH, TT], BF16)
    xnb_at = cur[0] - 2 * DCH * TT * 2
    r2b = alloc("r2b", [128, hch, TT], BF16)
    r2b_at = cur[0] - hch * TT * 2
    woT_b = alloc("woT_b", [128, DCH, D], BF16)
    wcmb_st = alloc("wcmb_st", [128, 2, DCH, TT], BF16)
    wcmb_st_at = cur[0] - 2 * DCH * TT * 2
    wproj_st = alloc("wproj_st", [128, 2, hch, 128], BF16)
    wproj_st_at = cur[0] - 2 * hch * 128 * 2
    statA = alloc("statA", [128, 4, TT], F32)
    statA_at = cur[0] - 4 * TT * 4
    sqbuf = alloc("sqbuf", [128, DCH, TT], BF16, at=statA_at)
    statB = alloc("statB", [128, 4, TT], F32)
    statB_at = cur[0] - 4 * TT * 4
    tmp2 = alloc("tmp2", [128, 2, TT], F32)
    scb16 = alloc("scb16", [128, TT], BF16, at=statB_at)
    hfbc = alloc("hfbc", [128, TT], F32, at=statB_at + TT * 2)
    nhfbc = alloc("nhfbc", [128, TT], F32, at=statB_at + TT * 6)
    ones_b = alloc("ones_b", [128, 128], BF16)
    ones32 = alloc("ones32", [128, 128], F32, at=statA_at)
    wcT_b = alloc("wcT_b", [NS, DCH, 128], BF16)
    v_b = alloc("v_b", [NS, DCH, 128], BF16, at=statA_at + 512)
    e1t = alloc("e1t", [NS, TT], F32)
    rsig = e1t
    aext = e1t
    but = alloc("but", [NS, TT], F32)
    ht = alloc("ht", [NS, TT], F32)
    carry = alloc("carry", [NS, 1], F32)
    hbt = alloc("hbt", [NS, TT], BF16)
    fbT_b = alloc("fbT_b", [NS, NS], BF16, at=statA_at + 2560)
    A_s = alloc("A_s", [NS, 1], F32)
    A99 = A_s
    A01 = alloc("A01", [NS, 1], F32)
    bb_s = alloc("bb_s", [NS, 1], F32)
    bt_s = alloc("bt_s", [NS, 1], F32)
    nbt = bt_s
    gate_s = alloc("gate_s", [NS, 1], F32, at=statA_at + 2624)
    bc_s = alloc("bc_s", [128, DCH], F32)
    bo_s = alloc("bo_s", [128, DCH], F32)
    dd_s = alloc("dd_s", [128, DCH], F32)
    wbtT_b = alloc("wbtT_b", [128, DCH, 32], BF16)
    assert cur[0] <= nc.SBUF_PARTITION_SIZE_BYTES, cur[0]

    r2b_size = hch * TT * 2
    if r2b_size >= 32768:
        GTb = alloc("GTb", [NS, h], BF16, at=wcmb_st_at)
        uT_b = alloc("uT_b", [NS, h], BF16, at=wcmb_st_at + 2 * h)
        cb1 = alloc("cb1", [128, DCH, TT], BF16, at=xnb_at)
        cb2 = alloc("cb2", [128, DCH, TT], BF16, at=r2b_at)
        dbuf = alloc("dbuf", [128, DCH, TT], BF16, at=r2b_at + 8192)
        wsl = alloc("wsl", [128, DCH, TT], F32, at=r2b_at + 16384)
        wsl2 = alloc("wsl2", [128, hch, 128], F32, at=r2b_at + 16384)
        aw32 = alloc("aw32", [128, DCH, TT], F32, at=x_s_at)
        aw32p = alloc("aw32p", [128, hch, 128], F32, at=x_s_at)
        aw32B = alloc("aw32B", [128, DCH, TT], F32, at=xn32_at)
        aw32pB = alloc("aw32pB", [128, hch, 128], F32, at=xn32_at)
        wslB = alloc("wslB", [128, DCH, TT], F32, at=x_s_at + 16384)
        wsl2B = alloc("wsl2B", [128, hch, 128], F32, at=x_s_at + 16384)
        wr8 = alloc("wr8", [128, 2, DCH, TT], BF16, at=xb_at)
        tls = alloc("tls", [128, DCH, TT], F32, at=r2b_at)
    else:
        GTb = alloc("GTb", [NS, h], BF16)
        uT_b = alloc("uT_b", [NS, h], BF16)
        cb1 = alloc("cb1", [128, DCH, TT], BF16, at=xnb_at)
        cb2 = alloc("cb2", [128, DCH, TT], BF16, at=xn32_at)
        dbuf = alloc("dbuf", [128, DCH, TT], BF16, at=xn32_at + DCH * TT * 2)
        wsl = alloc("wsl", [128, DCH, TT], F32)
        wsl_at = cur[0] - DCH * TT * 4
        wsl2 = alloc("wsl2", [128, hch, 128], F32, at=wsl_at)
        aw32 = alloc("aw32", [128, DCH, TT], F32, at=x_s_at)
        aw32p = alloc("aw32p", [128, hch, 128], F32, at=x_s_at)
        aw32B = aw32
        aw32pB = aw32p
        wslB = alloc("wslB", [128, DCH, TT], F32, at=x_s_at + 16384)
        wsl2B = alloc("wsl2B", [128, hch, 128], F32, at=x_s_at + 16384)
        wr8 = alloc("wr8", [128, 2, DCH, TT], BF16, at=xb_at)
        tls = alloc("tls", [128, DCH, TT], F32, at=wsl_at)

    pb = [nc.alloc_psum_tensor(f"pb{i}", [128, TT], F32) for i in range(8)]

    sems = {}
    ctr = {}
    for name in ["s_x0", "s_x1", "s_cst", "s_out0", "s_out1",
                 "s_wc0", "s_wc1", "s_wp0", "s_wp1", "s_sc0", "s_sc1",
                 "s_wl0", "s_wl1", "s_sml", "s_T", "s_V", "s_S"]:
        sems[name] = nc.alloc_semaphore(name)
        ctr[name] = 0
    lastwait = {}

    def inc(ins, name, k=1):
        ins.then_inc(sems[name], k)
        ctr[name] += k
        return ctr[name]

    def wait(e, ename, name, val):
        if val is None or val <= 0:
            return
        key = (ename, name)
        if lastwait.get(key, 0) >= val:
            return
        e.wait_ge(sems[name], val)
        lastwait[key] = val

    blk = nc.Block()
    blk.__enter__()

    def wcmb_sem(hs):
        return "s_wc%d" % (hs % 2)

    def wcmb_ld(i, hs):
        return 16 * (i * (nhs // 2) + hs // 2 + 1)

    def wproj_sem(oc):
        return "s_wp%d" % (oc % 2)

    def wproj_ld(i, oc):
        return 16 * (i * 4 + oc // 2 + 1)
    wcmb_use = {}
    wproj_use = {}

    def x_sem(i):
        return "s_x%d" % (i % 2)

    def xin_count(i):
        return 16 * (i // 2 + 1)
    CST_ALL = 96

    def xin_ap(i):
        seq, q = i // tps, i % tps
        return xT[seq, :, q * TT:(q + 1) * TT].rearrange("(c p) t -> p c t", p=128)

    def yout_ap(i):
        seq, q = i // tps, i % tps
        return yT[seq, :, q * TT:(q + 1) * TT].rearrange("(c p) t -> p c t", p=128)

    def _loads(sy):
        for dst, src in [
            (A_s[:], avec), (bb_s[:], bb16), (bt_s[:], bt16),
            (gate_s[:], gate16), (bc_s[:], bc8), (bo_s[:], bo8), (dd_s[:], dd8),
        ]:
            inc(sy.dma_start(out=dst, in_=src), "s_sml", 16)
    blk.sync(_loads)
    sml_done = ctr["s_sml"]

    def _gx(g):
        inc(g.dma_start(out=uT_b[:], in_=uT), "s_cst", 16)
        inc(g.dma_start(out=v_b[:], in_=vmat.rearrange("n (c m) -> n c m", c=DCH)),
            "s_cst", 16)
        inc(g.dma_start(out=wcT_b[:], in_=wcT.rearrange("n (c m) -> n c m", c=DCH)),
            "s_cst", 16)
        inc(g.dma_start(out=wbtT_b[:], in_=wbtT.rearrange("(c p) m -> p c m", p=128)),
            "s_cst", 16)
        inc(g.dma_start(out=fbT_b[:], in_=fbm), "s_cst", 16)
        inc(g.dma_start(out=woT_b[:], in_=woT.rearrange("(c p) m -> p c m", p=128)),
            "s_cst", 16)
        assert ctr["s_cst"] == CST_ALL
    blk.gpsimd(_gx)

    def _memsets(ve):
        ve.memset(ones_b[:], 1.0)
        ve.memset(ones32[:], 1.0)
        wait(ve, "vector", "s_sml", sml_done)
        ve.tensor_scalar(out=A01[:], in0=A_s[:], scalar1=DELTA_MIN, scalar2=None, op0=ALU.mult)
        inc(ve.tensor_scalar(out=nbt[:], in0=bt_s[:], scalar1=-1.0, scalar2=None, op0=ALU.mult),
            "s_V")
    blk.vector(_memsets)

    def _a99(ve):
        wait(ve, "vector", "s_V", ctr["s_V"])
        inc(ve.tensor_scalar(out=A99[:], in0=A_s[:], scalar1=DELTA_SCALE,
                             scalar2=None, op0=ALU.mult), "s_V")
    blk.vector(_a99)
    memset_done = ctr["s_V"]

    ngt = h // TT
    gt_v0 = ctr["s_V"]

    def _gt_mm(te):
        wait(te, "tensor", "s_cst", CST_ALL)
        wait(te, "tensor", "s_V", memset_done)
        for j in range(ngt):
            if j >= 1:
                wait(te, "tensor", "s_V", gt_v0 + j)
            inc(te.matmul(pb[4][0:NS, :], fbT_b[:], uT_b[:, j * TT:(j + 1) * TT],
                          start=True, stop=True), "s_T")
    blk.tensor(_gt_mm)
    gt_t0 = ctr["s_T"] - ngt

    def _gt_evac(ve):
        for j in range(ngt):
            wait(ve, "vector", "s_T", gt_t0 + j + 1)
            inc(ve.tensor_scalar(out=GTb[:, j * TT:(j + 1) * TT], in0=pb[4][0:NS, :],
                                 scalar1=gate_s[:, 0:1], scalar2=None, op0=ALU.mult), "s_V")
    blk.vector(_gt_evac)
    gt_evac_done = ctr["s_V"]

    last_sv = 0
    def scw_sem(j):
        return "s_sc%d" % (j % 2)

    def scw_count(j):
        return 16 * (j // 2 + 1)

    summ_hist = {}
    cmp_hist = {}
    sc_hist = {}
    wadd_hist = {}
    wrp_hist = {}
    slice_c_done = {}
    pend = []

    def emit_fold(hs, wrp_done):
        LRB = [1, 2, 4, 5, 6, 7]
        mm_done = {}
        for dc in range(DCH):
            b = LRB[dc % 6]

            def _wlr(te, hs=hs, dc=dc, b=b):
                wait(te, "tensor", "s_cst", CST_ALL)
                if dc >= 6:
                    prev = wadd_hist.get((hs, dc - 6), 0)
                else:
                    pdc = dc + 6 if dc < 2 else dc
                    prev = wadd_hist.get((hs - 1, pdc), 0)
                wait(te, "tensor", "s_V", max(gt_evac_done, prev))
                mm_done[dc] = inc(
                    te.matmul(pb[b][:], v_b[:, dc], GTb[:, hs * TT:(hs + 1) * TT],
                              start=True, stop=True), "s_T")
            blk.tensor(_wlr)

            def _wadd(ve, hs=hs, dc=dc, b=b, wrp_done=wrp_done):
                wait(ve, "vector", "s_V", wrp_done)
                wait(ve, "vector", "s_T", mm_done[dc])
                inc(ve.tensor_tensor(out=wr8[:, hs % 2, dc], in0=wr8[:, hs % 2, dc],
                                     in1=pb[b][:], op=ALU.add), "s_V")
                wadd_hist[(hs, dc)] = ctr["s_V"]
            blk.vector(_wadd)
        slice_c_done[hs] = ctr["s_V"]

        def _wr(g, hs=hs):
            wait(g, "gpsimd", "s_V", slice_c_done[hs])
            assert inc(g.dma_start(out=wcmb_sc[hs], in_=wr8[:, hs % 2]),
                       scw_sem(hs), 16) == scw_count(hs)
        blk.gpsimd(_wr)

    def wldq(j):
        return "s_wl%d" % (j % 2)

    def wldc(j):
        return 16 * (j // 2 + 1)

    for hs in range(nhs):
        wv = wsl if hs % 2 == 0 else wslB

        def _ld(sy, hs=hs, wv=wv):
            if hs >= 2:
                wait(sy, "sync", "s_V", cmp_hist[hs - 2])
            inc(sy.dma_start(
                out=wv[:],
                in_=wfcT[:, hs * TT:(hs + 1) * TT].rearrange("(c p) h -> p c h", p=128),
            ), wldq(hs), 16)
        blk.sync(_ld)
        ld_done = ctr[wldq(hs)]

        awdb = 2 if aw32B is not aw32 else 1
        av = aw32 if hs % 2 == 0 else aw32B

        def _abs(sc, hs=hs, wv=wv, av=av, awdb=awdb, ld_done=ld_done):
            wait(sc, "scalar", wldq(hs), ld_done)
            if hs >= awdb:
                wait(sc, "scalar", "s_T", summ_hist[hs - awdb])
            inc(sc.activation(out=av[:], in_=wv[:], func=AF.Abs), "s_S")
        blk.scalar(_abs)
        abs_done = ctr["s_S"]

        def _summ(te, hs=hs, av=av, abs_done=abs_done):
            wait(te, "tensor", "s_S", abs_done)
            wait(te, "tensor", "s_V", sc_hist.get(hs - 1, 0))
            for dc in range(DCH):
                i2 = te.matmul(pb[0][:], ones32[:], av[:, dc],
                               start=(dc == 0), stop=(dc == DCH - 1))
            inc(i2, "s_T")
        blk.tensor(_summ)
        bc_done = summ_hist[hs] = ctr["s_T"]

        def _sc_evac(ve, hs=hs, bc_done=bc_done):
            wait(ve, "vector", "s_T", bc_done)
            wait(ve, "vector", "s_V", wrp_hist.get(hs - 1, 0))
            W = float(D)
            ve.tensor_scalar(out=scb16[:], in0=pb[0][:], scalar1=1.0 / W,
                             scalar2=1e-5, op0=ALU.mult, op1=ALU.max)
            ve.tensor_scalar(out=hfbc[:], in0=pb[0][:], scalar1=0.5 / W,
                             scalar2=0.5e-5, op0=ALU.mult, op1=ALU.max)
            inc(ve.tensor_scalar(out=nhfbc[:], in0=pb[0][:], scalar1=-0.5 / W,
                                 scalar2=-0.5e-5, op0=ALU.mult, op1=ALU.min), "s_V")
        blk.vector(_sc_evac)
        sc_done = sc_hist[hs] = ctr["s_V"]

        def _cmp(ve, hs=hs, wv=wv, ld_done=ld_done, sc_done=sc_done):
            wait(ve, "vector", "s_V", sc_done)
            wait(ve, "vector", wldq(hs), ld_done)
            if hs >= 2:
                wait(ve, "vector", scw_sem(hs - 2), scw_count(hs - 2))
            hf = hfbc[:, None, :].broadcast_to([128, DCH, TT])
            nhf = nhfbc[:, None, :].broadcast_to([128, DCH, TT])
            ve.tensor_tensor(out=cb1[:], in0=wv[:], in1=hf, op=ALU.is_gt)
            inc(ve.tensor_tensor(out=cb2[:], in0=wv[:], in1=nhf, op=ALU.is_lt),
                "s_V")
        blk.vector(_cmp)
        cmp_done = cmp_hist[hs] = ctr["s_V"]

        def _dsub(ve, cmp_done=cmp_done):
            wait(ve, "vector", "s_V", cmp_done)
            inc(ve.tensor_tensor(out=dbuf[:].rearrange("p c t -> p (c t)"),
                                 in0=cb1[:].rearrange("p c t -> p (c t)"),
                                 in1=cb2[:].rearrange("p c t -> p (c t)"),
                                 op=ALU.subtract), "s_V")
        blk.vector(_dsub)
        d_done = ctr["s_V"]

        def _wrp(ve, hs=hs, d_done=d_done):
            wait(ve, "vector", "s_V", d_done)
            sc = scb16[:, None, :].broadcast_to([128, DCH, TT])
            inc(ve.tensor_tensor(out=wr8[:, hs % 2], in0=dbuf[:], in1=sc,
                                 op=ALU.mult), "s_V")
        blk.vector(_wrp)
        wrp_done = wrp_hist[hs] = ctr["s_V"]

        pend.append((hs, wrp_done))
        if len(pend) > 1:
            emit_fold(*pend.pop(0))

    if pend:
        emit_fold(*pend.pop(0))
    last_sv = ctr["s_V"]

    for os_ in range(DCH):
        j = nhs + os_

        wv2 = wsl2 if j % 2 == 0 else wsl2B

        def _ld2(sy, j=j, wv2=wv2):
            wait(sy, "sync", "s_V", cmp_hist[j - 2])
            inc(sy.dma_start(
                out=wv2[:],
                in_=wprojT[:, os_ * 128:(os_ + 1) * 128].rearrange("(c p) o -> p c o", p=128),
            ), wldq(j), 16)
        blk.sync(_ld2)
        ld_done = ctr[wldq(j)]

        awdb = 2 if aw32pB is not aw32p else 1
        av2 = aw32p if j % 2 == 0 else aw32pB

        def _abs2(sc, j=j, wv2=wv2, av2=av2, awdb=awdb, ld_done=ld_done):
            wait(sc, "scalar", wldq(j), ld_done)
            wait(sc, "scalar", "s_T", summ_hist[j - awdb])
            inc(sc.activation(out=av2[:], in_=wv2[:], func=AF.Abs), "s_S")
        blk.scalar(_abs2)
        abs_done = ctr["s_S"]

        def _summ2(te, j=j, av2=av2, abs_done=abs_done):
            wait(te, "tensor", "s_S", abs_done)
            wait(te, "tensor", "s_V", sc_hist.get(j - 1, 0))
            for kc in range(hch):
                i2 = te.matmul(pb[0][:, 0:128], ones32[:], av2[:, kc],
                               start=(kc == 0), stop=(kc == hch - 1))
            inc(i2, "s_T")
        blk.tensor(_summ2)
        bc_done = summ_hist[j] = ctr["s_T"]

        def _sc_evac2(ve, j=j, bc_done=bc_done):
            wait(ve, "vector", "s_T", bc_done)
            wait(ve, "vector", "s_V", wrp_hist.get(j - 1, 0))
            W = float(h)
            ve.tensor_scalar(out=scb16[:, 0:128], in0=pb[0][:, 0:128], scalar1=1.0 / W,
                             scalar2=1e-5, op0=ALU.mult, op1=ALU.max)
            ve.tensor_scalar(out=hfbc[:, 0:128], in0=pb[0][:, 0:128], scalar1=0.5 / W,
                             scalar2=0.5e-5, op0=ALU.mult, op1=ALU.max)
            inc(ve.tensor_scalar(out=nhfbc[:, 0:128], in0=pb[0][:, 0:128],
                                 scalar1=-0.5 / W, scalar2=-0.5e-5,
                                 op0=ALU.mult, op1=ALU.min), "s_V")
        blk.vector(_sc_evac2)
        neg_done = sc_hist[j] = ctr["s_V"]

        def _cmp2(ve, j=j, wv2=wv2, ld_done=ld_done, neg_done=neg_done):
            wait(ve, "vector", "s_V", neg_done)
            wait(ve, "vector", wldq(j), ld_done)
            if j >= 2:
                wait(ve, "vector", scw_sem(j - 2), scw_count(j - 2))
            hf = hfbc[:, None, 0:128].broadcast_to([128, hch, 128])
            nhf = nhfbc[:, None, 0:128].broadcast_to([128, hch, 128])
            cb1v = cb1[:].rearrange("p c (a b) -> p (c a) b", b=128)[:, 0:hch, :]
            cb2v = cb2[:].rearrange("p c (a b) -> p (c a) b", b=128)[:, 0:hch, :]
            ve.tensor_tensor(out=cb1v, in0=wv2[:], in1=hf, op=ALU.is_gt)
            inc(ve.tensor_tensor(out=cb2v, in0=wv2[:], in1=nhf, op=ALU.is_lt), "s_V")
        blk.vector(_cmp2)
        cmp_done = cmp_hist[j] = ctr["s_V"]

        def _dsub2(ve, cmp_done=cmp_done):
            wait(ve, "vector", "s_V", cmp_done)
            inc(ve.tensor_tensor(
                out=dbuf[:].rearrange("p c (a b) -> p (c a) b", b=128)[:, 0:hch, :],
                in0=cb1[:].rearrange("p c (a b) -> p (c a) b", b=128)[:, 0:hch, :],
                in1=cb2[:].rearrange("p c (a b) -> p (c a) b", b=128)[:, 0:hch, :],
                op=ALU.subtract), "s_V")
        blk.vector(_dsub2)
        d_done = ctr["s_V"]

        def _wrp2(ve, os_=os_, j=j, d_done=d_done):
            wait(ve, "vector", "s_V", d_done)
            sc = scb16[:, None, 0:128].broadcast_to([128, hch, 128])
            inc(ve.tensor_tensor(
                out=wr8[:, j % 2].rearrange("p c (a b) -> p (c a) b", b=128)[:, 0:hch, :],
                in0=dbuf[:].rearrange("p c (a b) -> p (c a) b", b=128)[:, 0:hch, :],
                in1=sc, op=ALU.mult), "s_V")
        blk.vector(_wrp2)
        wrp_hist[j] = ctr["s_V"]
        slice2_c_done = slice2_c_done if os_ else {}
        slice2_c_done[os_] = ctr["s_V"]

        def _wr2(g, os_=os_, j=j):
            wait(g, "gpsimd", "s_V", slice2_c_done[os_])
            assert inc(g.dma_start(
                out=wproj_sc[os_],
                in_=wr8[:, j % 2].rearrange("p c (a b) -> p (c a) b", b=128)[:, 0:hch, :]),
                scw_sem(j), 16) == scw_count(j)
        blk.gpsimd(_wr2)
        last_sv = ctr["s_V"]

    sc0_total, sc1_total = ctr["s_sc0"], ctr["s_sc1"]
    tern_sv_total = ctr["s_V"]
    sums_total = ctr["s_T"]

    def _gx2(g):
        wait(g, "gpsimd", "s_T", summ_hist[nhs + DCH - 1])
        wait(g, "gpsimd", "s_V", cmp_hist[nhs + DCH - 1])
        inc(g.dma_start(out=x_s[:, 0], in_=xin_ap(0)), "s_x0", 16)
        inc(g.dma_start(out=x_s[:, 1], in_=xin_ap(1)), "s_x1", 16)
    blk.gpsimd(_gx2)

    mA, uA, auxA, rstdA = (statA[:, k] for k in range(4))
    qA = mA
    mB, uB, auxB, rstdB = (statB[:, k] for k in range(4))
    qB = mB

    hb_done = {}
    butp_done = {}
    ln1c_done = {}
    ln2_done = {}
    yb_done = {}
    wc_mm = {}
    wo_mm = {}
    wo_stt_done = {}
    final_done = {}
    out_done = {}
    r2_done = {}
    ffa_last = {}
    sq_chunks = {}
    ffb_last = {}
    run = {"statsv1": 0, "wo_stt": 0, "final": 0, "yb": 0, "tmp2": 0,
           "tls": 0, "nz": 0, "relu": 0}

    def emit_front1(j):
        p = j % 2
        src = x_s[:, p]

        def _sq(ve, j=j):
            if j == 0:
                wait(ve, "vector", "s_V", tern_sv_total)
                wait(ve, "vector", "s_sc0", sc0_total)
                wait(ve, "vector", "s_sc1", sc1_total)
            if j >= 1:
                wait(ve, "vector", "s_V", ln1c_done[j - 1])
            wait(ve, "vector", x_sem(j), xin_count(j))
            ve.tensor_tensor(out=sqbuf[:], in0=src, in1=src, op=ALU.mult)
            inc(ve.tensor_copy(out=xb[:], in_=src), "s_V")
        blk.vector(_sq)
        sq_done = ctr["s_V"]

        def _stats1(te, sq_done=sq_done):
            wait(te, "tensor", "s_V", sq_done)
            for dc in range(DCH):
                te.matmul(pb[0][:], ones_b[:], xb[:, dc],
                          start=(dc == 0), stop=(dc == DCH - 1))
            for dc in range(DCH):
                i2 = te.matmul(pb[1][:], ones_b[:], sqbuf[:, dc],
                               start=(dc == 0), stop=(dc == DCH - 1))
            inc(i2, "s_T")
        blk.tensor(_stats1)
        st_done = ctr["s_T"]

        def _sv1(ve, st_done=st_done):
            wait(ve, "vector", "s_T", st_done)
            ve.tensor_scalar(out=mA, in0=pb[0][:], scalar1=1.0 / D,
                             scalar2=None, op0=ALU.mult)
            inc(ve.tensor_scalar(out=uA, in0=pb[1][:], scalar1=1.0 / D,
                                 scalar2=EPS, op0=ALU.mult, op1=ALU.add), "s_V")
        blk.vector(_sv1)
        run["statsv1"] = mu = ctr["s_V"]

        def _msq(ve, mu=mu):
            wait(ve, "vector", "s_V", mu)
            inc(ve.tensor_tensor(out=auxA, in0=mA, in1=mA, op=ALU.mult), "s_V")
        blk.vector(_msq)
        ms = ctr["s_V"]

        def _varr(ve, ms=ms):
            wait(ve, "vector", "s_V", ms)
            inc(ve.tensor_tensor(out=uA, in0=uA, in1=auxA, op=ALU.subtract), "s_V")
        blk.vector(_varr)
        va = ctr["s_V"]

        def _lnv(sc, va=va):
            wait(sc, "scalar", "s_V", va)
            inc(sc.activation(out=auxA, in_=uA, func=AF.Ln), "s_S")
        blk.scalar(_lnv)
        lv = ctr["s_S"]

        def _nh(ve, lv=lv):
            wait(ve, "vector", "s_S", lv)
            inc(ve.tensor_scalar(out=auxA, in0=auxA, scalar1=-0.5,
                                 scalar2=None, op0=ALU.mult), "s_V")
        blk.vector(_nh)
        nh = ctr["s_V"]

        def _rstd(sc, nh=nh):
            wait(sc, "scalar", "s_V", nh)
            inc(sc.activation(out=rstdA, in_=auxA, func=AF.Exp), "s_S")
        blk.scalar(_rstd)
        rs = ctr["s_S"]

        return rs

    def emit_front1b(j, rs, ffb_guard):
        p = j % 2
        src = x_s[:, p]

        def _m1(ve, j=j, rs=rs, ffb_guard=ffb_guard):
            wait(ve, "vector", "s_S", rs)
            wait(ve, "vector", "s_V", run["tls"])
            wait(ve, "vector", "s_T", ffb_guard)
            ve.tensor_tensor(out=qA, in0=mA, in1=rstdA, op=ALU.mult)
            last = None
            for dc in range(DCH):
                last = ve.tensor_tensor(out=tls[:, dc], in0=src[:, dc],
                                        in1=rstdA, op=ALU.mult)
            inc(last, "s_V")
        blk.vector(_m1)
        md = ctr["s_V"]

        def _s1(ve, j=j, md=md):
            wait(ve, "vector", "s_V", md)
            last = None
            for dc in range(DCH):
                last = ve.tensor_tensor(out=xnb[:, p, dc], in0=tls[:, dc],
                                        in1=qA, op=ALU.subtract)
            inc(last, "s_V")
        blk.vector(_s1)
        ln1c_done[j] = run["tls"] = ctr["s_V"]

    def emit_front2(j):
        p = j % 2
        seq_start = (j % tps) == 0

        def _butp(te, j=j):
            wait(te, "tensor", "s_V", ln1c_done[j])
            for dc in range(DCH):
                te.matmul(pb[2][0:NS, :], wbtT_b[:, dc, 0:NS], xnb[:, p, dc],
                          start=(dc == 0), stop=(dc == DCH - 1))
            for dc in range(DCH):
                i2 = te.matmul(pb[3][0:NS, :], wbtT_b[:, dc, NS:2 * NS], xnb[:, p, dc],
                               start=(dc == 0), stop=(dc == DCH - 1))
            inc(i2, "s_T")
        blk.tensor(_butp)
        butp_done[j] = bp = ctr["s_T"]

        def _nz(ve, bp=bp):
            wait(ve, "vector", "s_T", bp)
            ve.tensor_scalar(out=e1t[:], in0=pb[3][0:NS, :], scalar1=-1.0,
                             scalar2=nbt[:, 0:1], op0=ALU.mult, op1=ALU.add)
            inc(ve.tensor_scalar(out=but[:], in0=pb[2][0:NS, :],
                                 scalar1=bb_s[:, 0:1], scalar2=None, op0=ALU.add), "s_V")
        blk.vector(_nz)
        nz = run["nz"] = ctr["s_V"]

        def _e1(sc, nz=nz):
            wait(sc, "scalar", "s_V", nz)
            inc(sc.activation(out=e1t[:], in_=e1t[:], func=AF.Exp), "s_S")
        blk.scalar(_e1)
        e1 = ctr["s_S"]

        def _e1p(ve, e1=e1):
            wait(ve, "vector", "s_S", e1)
            inc(ve.tensor_scalar(out=e1t[:], in0=e1t[:], scalar1=1.0,
                                 scalar2=None, op0=ALU.add), "s_V")
        blk.vector(_e1p)
        ep = ctr["s_V"]

        def _rsig(ve, ep=ep):
            wait(ve, "vector", "s_V", ep)
            inc(ve.reciprocal(out=rsig[:], in_=e1t[:]), "s_V")
        blk.vector(_rsig)
        rg = ctr["s_V"]

        def _zt(ve, rg=rg):
            wait(ve, "vector", "s_V", rg)
            inc(ve.tensor_scalar(out=rsig[:], in0=rsig[:], scalar1=A99[:, 0:1],
                                 scalar2=A01[:, 0:1], op0=ALU.mult, op1=ALU.add), "s_V")
        blk.vector(_zt)
        zt = ctr["s_V"]

        def _aext(sc, zt=zt):
            wait(sc, "scalar", "s_V", zt)
            inc(sc.activation(out=aext[:], in_=rsig[:], func=AF.Exp), "s_S")
        blk.scalar(_aext)
        ax = ctr["s_S"]

        def _scan(ve, j=j, seq_start=seq_start, ax=ax):
            wait(ve, "vector", "s_S", ax)
            init = 0.0 if seq_start else carry[:, 0:1]
            inc(ve.tensor_tensor_scan(out=ht[:], data0=aext[:], data1=but[:],
                                      initial=init, op0=ALU.mult, op1=ALU.add), "s_V")
        blk.vector(_scan)
        sn = ctr["s_V"]

        def _hb(ve, j=j, sn=sn):
            wait(ve, "vector", "s_V", sn)
            ve.tensor_copy(out=carry[:], in_=ht[:, TT - 1:TT])
            inc(ve.tensor_copy(out=hbt[:], in_=ht[:]), "s_V")
        blk.vector(_hb)
        hb_done[j] = ctr["s_V"]

    def emit_back1(j):
        p = j % 2
        wcd = {}
        V0 = ctr["s_V"]

        def _wc(te, j=j, V0=V0):
            wait(te, "tensor", "s_V", hb_done[j])
            wait(te, "tensor", "s_V", run["statsv1"])
            for dc in range(DCH):
                b = dc % 2
                if dc >= 2:
                    wait(te, "tensor", "s_V", V0 + ((dc - 2) // 2 + 1) * 2)
                wcd[dc] = inc(
                    te.matmul(pb[b][:], wcT_b[:, dc], hbt[:], start=True, stop=True), "s_T")
        blk.tensor(_wc)
        wc_mm[j] = dict(wcd)

        prev_ybp = 0
        for pp in range(DCH // 2):
            def _t2(ve, j=j, pp=pp, prev_ybp=prev_ybp):
                if pp:
                    wait(ve, "vector", "s_V", prev_ybp)
                else:
                    wait(ve, "vector", "s_V", run["tmp2"])
                ve.tensor_scalar(out=tmp2[:, 0], in0=xnb[:, p, 2 * pp],
                                 scalar1=dd_s[:, 2 * pp:2 * pp + 1], scalar2=None,
                                 op0=ALU.mult)
                inc(ve.tensor_scalar(out=tmp2[:, 1], in0=xnb[:, p, 2 * pp + 1],
                                     scalar1=dd_s[:, 2 * pp + 1:2 * pp + 2], scalar2=None,
                                     op0=ALU.mult), "s_V")
            blk.vector(_t2)
            t2d = ctr["s_V"]

            def _ybp(ve, j=j, pp=pp, t2d=t2d):
                wait(ve, "vector", "s_V", t2d)
                for q in range(2):
                    dc = 2 * pp + q
                    wait(ve, "vector", "s_T", wcd[dc])
                    ins = ve.scalar_tensor_tensor(out=yb[:, dc], in0=pb[dc % 2][:],
                                                  scalar=bc_s[:, dc:dc + 1],
                                                  in1=tmp2[:, q],
                                                  op0=ALU.add, op1=ALU.add)
                inc(ins, "s_V")
            blk.vector(_ybp)
            prev_ybp = ctr["s_V"]
        yb_done[j] = run["yb"] = run["tmp2"] = ctr["s_V"]

        wod = {}
        V1 = ctr["s_V"]

        def _wo(te, j=j, V1=V1):
            wait(te, "tensor", "s_V", yb_done[j])
            wait(te, "tensor", "s_S", run["relu"])
            for oc in range(DCH):
                b = 2 + oc % 4
                if oc >= 4:
                    wait(te, "tensor", "s_V", V1 + 1)
                for kc in range(DCH):
                    i2 = te.matmul(pb[b][:], woT_b[:, kc, oc * 128:(oc + 1) * 128],
                                   yb[:, kc], start=(kc == 0), stop=(kc == DCH - 1))
                wod[oc] = inc(i2, "s_T")
        blk.tensor(_wo)
        wo_mm[j] = dict(wod)

        def _wo_stt(ve, j=j):
            if j >= 2:
                wait(ve, "vector", "s_out%d" % (j % 2), 16 * ((j - 2) // 2 + 1))
            for g in range(2):
                wait(ve, "vector", "s_T", wod[4 * g + 3])
                last = None
                for q in range(4):
                    oc = 4 * g + q
                    last = ve.scalar_tensor_tensor(out=xn32[:, p, oc], in0=pb[2 + oc % 4][:],
                                                   scalar=bo_s[:, oc:oc + 1],
                                                   in1=x_s[:, p, oc],
                                                   op0=ALU.add, op1=ALU.add)
                inc(last, "s_V")
        blk.vector(_wo_stt)
        wo_stt_done[j] = run["wo_stt"] = ws = ctr["s_V"]

        if j + 2 < nt:
            def _pref(g, j=j, ws=ws):
                wait(g, "gpsimd", "s_V", ws)
                assert inc(g.dma_start(out=x_s[:, j % 2], in_=xin_ap(j + 2)),
                           x_sem(j), 16) == xin_count(j + 2)
            blk.gpsimd(_pref)

        def _sqln2(ve, j=j, ws=ws):
            wait(ve, "vector", "s_V", ws)
            ve.tensor_tensor(out=xnb[:, p], in0=xn32[:, p], in1=xn32[:, p],
                             op=ALU.mult)
            inc(ve.tensor_copy(out=yb[:], in_=xn32[:, p]), "s_V")
        blk.vector(_sqln2)
        return ctr["s_V"]

    def emit_back2(j, sq2_done):
        p = j % 2

        def _stats2(te, j=j, sq2_done=sq2_done):
            wait(te, "tensor", "s_V", sq2_done)
            wait(te, "tensor", "s_V", yb_done[j])
            for dc in range(DCH):
                te.matmul(pb[0][:], ones_b[:], yb[:, dc],
                          start=(dc == 0), stop=(dc == DCH - 1))
            for dc in range(DCH):
                i2 = te.matmul(pb[1][:], ones_b[:], xnb[:, p, dc],
                               start=(dc == 0), stop=(dc == DCH - 1))
            inc(i2, "s_T")
        blk.tensor(_stats2)
        st = ctr["s_T"]

        def _sv2(ve, st=st):
            wait(ve, "vector", "s_T", st)
            ve.tensor_scalar(out=mB, in0=pb[0][:], scalar1=1.0 / D,
                             scalar2=None, op0=ALU.mult)
            inc(ve.tensor_scalar(out=uB, in0=pb[1][:], scalar1=1.0 / D,
                                 scalar2=EPS, op0=ALU.mult, op1=ALU.add), "s_V")
        blk.vector(_sv2)
        mu = ctr["s_V"]

        def _msq2(ve, mu=mu):
            wait(ve, "vector", "s_V", mu)
            inc(ve.tensor_tensor(out=auxB, in0=mB, in1=mB, op=ALU.mult), "s_V")
        blk.vector(_msq2)
        ms = ctr["s_V"]

        def _varr2(ve, ms=ms):
            wait(ve, "vector", "s_V", ms)
            inc(ve.tensor_tensor(out=uB, in0=uB, in1=auxB, op=ALU.subtract), "s_V")
        blk.vector(_varr2)
        va = ctr["s_V"]

        def _lnv2(sc, va=va):
            wait(sc, "scalar", "s_V", va)
            inc(sc.activation(out=auxB, in_=uB, func=AF.Ln), "s_S")
        blk.scalar(_lnv2)
        lv = ctr["s_S"]

        def _nh2(ve, lv=lv):
            wait(ve, "vector", "s_S", lv)
            inc(ve.tensor_scalar(out=auxB, in0=auxB, scalar1=-0.5,
                                 scalar2=None, op0=ALU.mult), "s_V")
        blk.vector(_nh2)
        nh = ctr["s_V"]

        def _rstd2(sc, nh=nh):
            wait(sc, "scalar", "s_V", nh)
            inc(sc.activation(out=rstdB, in_=auxB, func=AF.Exp), "s_S")
        blk.scalar(_rstd2)
        rs = ctr["s_S"]

        def _m2(ve, j=j, rs=rs):
            wait(ve, "vector", "s_S", rs)
            wait(ve, "vector", "s_V", run["tls"])
            ve.tensor_tensor(out=qB, in0=mB, in1=rstdB, op=ALU.mult)
            last = None
            for dc in range(DCH):
                last = ve.tensor_tensor(out=tls[:, dc], in0=xn32[:, p, dc],
                                        in1=rstdB, op=ALU.mult)
            inc(last, "s_V")
        blk.vector(_m2)
        md = ctr["s_V"]

        def _s2(ve, j=j, md=md):
            wait(ve, "vector", "s_V", md)
            last = None
            for dc in range(DCH):
                last = ve.tensor_tensor(out=xnb[:, p, dc], in0=tls[:, dc],
                                        in1=qB, op=ALU.subtract)
            inc(last, "s_V")
        blk.vector(_s2)
        ln2_done[j] = run["tls"] = ctr["s_V"]

    def emit_ffnA(i):
        p = i % 2
        fad = {}
        S0 = ctr["s_S"]

        def _ffa(te, i=i, S0=S0):
            wait(te, "tensor", "s_V", ln2_done[i])
            wait(te, "tensor", "s_V", run["nz"])
            for hc in range(hch):
                hs = hc // 4
                wait(te, "tensor", wcmb_sem(hs), wcmb_ld(i, hs))
                if hc >= 2:
                    wait(te, "tensor", "s_S", S0 + hc - 1)
                b = 2 + hc % 2
                for dc in range(DCH):
                    i2 = te.matmul(pb[b][:],
                                   wcmb_st[:, hs % 2, dc, (hc % 4) * 128:(hc % 4 + 1) * 128],
                                   xnb[:, p, dc], start=(dc == 0), stop=(dc == DCH - 1))
                fad[hc] = inc(i2, "s_T")
                if hc % 4 == 3:
                    wcmb_use[(i, hc // 4)] = ctr["s_T"]
        blk.tensor(_ffa)
        ffa_last[i] = ctr["s_T"]

        if nhs > 2:
            def _wsB(sy, i=i):
                for hs in range(2, nhs):
                    wait(sy, "sync", "s_T", wcmb_use[(i, hs - 2)])
                    assert inc(sy.dma_start(out=wcmb_st[:, hs % 2], in_=wcmb_sc[hs]),
                               wcmb_sem(hs), 16) == wcmb_ld(i, hs)
            blk.sync(_wsB)

        def _relu(sc, S0=S0):
            for hc in range(hch):
                b = 2 + hc % 2
                wait(sc, "scalar", "s_T", fad[hc])
                inc(sc.activation(out=r2b[:, hc], in_=pb[b][:], func=AF.Relu), "s_S")
        blk.scalar(_relu)
        run["relu"] = ctr["s_S"]

        QS = max(1, hch // 4)
        sqc = {}

        def _sq2(ve, S0=S0):
            for qq in range(hch // QS):
                wait(ve, "vector", "s_S", S0 + (qq + 1) * QS)
                inc(ve.tensor_tensor(out=r2b[:, qq * QS:(qq + 1) * QS],
                                     in0=r2b[:, qq * QS:(qq + 1) * QS],
                                     in1=r2b[:, qq * QS:(qq + 1) * QS],
                                     op=ALU.mult), "s_V")
                sqc[qq] = ctr["s_V"]
        blk.vector(_sq2)
        r2_done[i] = ctr["s_V"]
        sq_chunks[i] = dict(sqc)

    def emit_ffnB(i):
        p = i % 2
        fbd = {}
        V0 = ctr["s_V"]

        QS = max(1, hch // 4)

        def _ffb(te, i=i, V0=V0):
            for oc in range(DCH):
                wait(te, "tensor", wproj_sem(oc), wproj_ld(i, oc))
                if oc >= 2:
                    wait(te, "tensor", "s_V", V0 + oc - 1)
                b = 6 + oc % 2
                for kc in range(hch):
                    if oc == 0 and kc % QS == 0:
                        wait(te, "tensor", "s_V", sq_chunks[i][kc // QS])
                    i2 = te.matmul(pb[b][:], wproj_st[:, oc % 2, kc, :],
                                   r2b[:, kc], start=(kc == 0), stop=(kc == hch - 1))
                fbd[oc] = inc(i2, "s_T")
                wproj_use[(i, oc)] = ctr["s_T"]
        blk.tensor(_ffb)
        ffb_last[i] = ctr["s_T"]

        def _wsC(sy, i=i):
            for oc in range(2, DCH):
                wait(sy, "sync", "s_T", wproj_use[(i, oc - 2)])
                assert inc(sy.dma_start(out=wproj_st[:, oc % 2], in_=wproj_sc[oc]),
                           wproj_sem(oc), 16) == wproj_ld(i, oc)
        blk.sync(_wsC)

        def _final(ve, i=i, p=p):
            for oc in range(DCH):
                b = 6 + oc % 2
                wait(ve, "vector", "s_T", fbd[oc])
                inc(ve.tensor_tensor(out=xn32[:, p, oc], in0=xn32[:, p, oc],
                                     in1=pb[b][:], op=ALU.add), "s_V")
        blk.vector(_final)
        final_done[i] = run["final"] = ctr["s_V"]

        def _outd(g, i=i, p=p):
            wait(g, "gpsimd", "s_V", final_done[i])
            out_done[i] = inc(g.dma_start(out=yout_ap(i), in_=xn32[:, p]),
                              "s_out%d" % (i % 2), 16)
        blk.gpsimd(_outd)

        if i + 1 < nt:
            def _wsA(sy, i=i):
                for hs in range(2):
                    wait(sy, "sync", "s_T", wcmb_use[(i, hs + nhs - 2)])
                    assert inc(sy.dma_start(out=wcmb_st[:, hs], in_=wcmb_sc[hs]),
                               wcmb_sem(hs), 16) == wcmb_ld(i + 1, hs)
                for oc in range(2):
                    wait(sy, "sync", "s_T", wproj_use[(i, oc + 6)])
                    assert inc(sy.dma_start(out=wproj_st[:, oc], in_=wproj_sc[oc]),
                               wproj_sem(oc), 16) == wproj_ld(i + 1, oc)
            blk.sync(_wsA)

    def _wsA0(sy):
        wait(sy, "sync", "s_sc0", sc0_total)
        wait(sy, "sync", "s_sc1", sc1_total)
        for hs in range(2):
            assert inc(sy.dma_start(out=wcmb_st[:, hs], in_=wcmb_sc[hs]),
                       wcmb_sem(hs), 16) == wcmb_ld(0, hs)
        for oc in range(2):
            assert inc(sy.dma_start(out=wproj_st[:, oc], in_=wproj_sc[oc]),
                       wproj_sem(oc), 16) == wproj_ld(0, oc)
    blk.sync(_wsA0)

    emit_front1b(0, emit_front1(0), 0)
    emit_front2(0)
    s2 = emit_back1(0)
    emit_back2(0, s2)
    emit_front1b(1, emit_front1(1), 0)
    emit_front2(1)

    for i in range(nt):
        emit_ffnA(i)
        rs1 = emit_front1(i + 2) if i + 2 < nt else None
        emit_ffnB(i)
        if i + 1 < nt:
            s2 = emit_back1(i + 1)
        if i + 2 < nt:
            emit_front1b(i + 2, rs1, ffb_last[i])
        if i + 1 < nt:
            emit_back2(i + 1, s2)
        if i + 2 < nt:
            emit_front2(i + 2)


    def _fin(sy):
        wait(sy, "sync", "s_out0", ctr["s_out0"])
        wait(sy, "sync", "s_out1", ctr["s_out1"])
    blk.sync(_fin)

    blk.__exit__(None, None, None)
    return nc


_NC_CACHE = None


def make_in_maps(x, An, W_b, b_b, W_c, b_c, Dg, W_t, b_t, W_o, b_o,
                 fast_b, w_fc, w_proj, u, v, gate, spc=SEQ_PER_CORE):
    shared = dict(
        wfcT=np.ascontiguousarray(w_fc.T),
        wprojT=np.ascontiguousarray(w_proj.T),
        woT=np.ascontiguousarray(W_o.T),
        wbtT=np.ascontiguousarray(
            np.concatenate([W_b.T, np.repeat(W_t.T, NS, axis=1)], axis=1)),
        wcT=np.ascontiguousarray(W_c.T),
        vmat=np.ascontiguousarray(v),
        uT=np.ascontiguousarray(u.T),
        fbm=np.ascontiguousarray(fast_b),
        avec=np.ascontiguousarray(An.reshape(NS, 1)),
        bb16=np.ascontiguousarray(b_b.reshape(NS, 1)),
        bt16=np.ascontiguousarray(np.broadcast_to(b_t.reshape(1, 1), (NS, 1))),
        gate16=np.ascontiguousarray(np.broadcast_to(gate.reshape(1, 1), (NS, 1))),
        bc8=np.ascontiguousarray(b_c.reshape(DCH, 128).T),
        bo8=np.ascontiguousarray(b_o.reshape(DCH, 128).T),
        dd8=np.ascontiguousarray(Dg.reshape(DCH, 128).T),
    )
    ncores = x.shape[0] // spc
    in_maps = []
    for c in range(ncores):
        m = dict(shared)
        m["xT"] = np.ascontiguousarray(
            x[c * spc:(c + 1) * spc].transpose(0, 2, 1))
        in_maps.append(m)
    return in_maps


def kernel(L0_attn_x, L0_attn_A, L0_attn_W_b, L0_attn_b_b, L0_attn_W_c,
           L0_attn_b_c, L0_attn_D, L0_attn_W_t, L0_attn_b_t, L0_attn_W_o,
           L0_attn_b_o, L0_ffn_fast_b, L0_ffn_w_fc, L0_ffn_w_proj, L0_ffn_u,
           L0_ffn_v, L0_ffn_gate):
    global _NC_CACHE
    in_maps = make_in_maps(
        np.asarray(L0_attn_x, dtype=np.float32),
        np.asarray(L0_attn_A, dtype=np.float32),
        np.asarray(L0_attn_W_b, dtype=np.float32),
        np.asarray(L0_attn_b_b, dtype=np.float32),
        np.asarray(L0_attn_W_c, dtype=np.float32),
        np.asarray(L0_attn_b_c, dtype=np.float32),
        np.asarray(L0_attn_D, dtype=np.float32),
        np.asarray(L0_attn_W_t, dtype=np.float32),
        np.asarray(L0_attn_b_t, dtype=np.float32),
        np.asarray(L0_attn_W_o, dtype=np.float32),
        np.asarray(L0_attn_b_o, dtype=np.float32),
        np.asarray(L0_ffn_fast_b, dtype=np.float32),
        np.asarray(L0_ffn_w_fc, dtype=np.float32),
        np.asarray(L0_ffn_w_proj, dtype=np.float32),
        np.asarray(L0_ffn_u, dtype=np.float32),
        np.asarray(L0_ffn_v, dtype=np.float32),
        np.asarray(L0_ffn_gate, dtype=np.float32),
    )

    if _NC_CACHE is None:
        _NC_CACHE = build_nc()
    res = run_bass_kernel_spmd(_NC_CACHE, in_maps, core_ids=list(range(NCORES)))

    out = np.empty((BATCH, S, D), dtype=np.float32)
    for c in range(NCORES):
        out[c * SEQ_PER_CORE:(c + 1) * SEQ_PER_CORE] = \
            res.results[c]["yT"].transpose(0, 2, 1)
    return out


# version 20 (final)
# speedup vs baseline: 1.0341x; 1.0261x over previous
import numpy as np
import concourse.bass as bass
import concourse.mybir as mybir
from concourse.bass_utils import run_bass_kernel_spmd

F32 = mybir.dt.float32
BF16 = mybir.dt.bfloat16
AF = mybir.ActivationFunctionType
ALU = mybir.AluOpType

D = 1024
NS = 16
H = 4096
S = 2048
BATCH = 16
NCORES = 8
SEQ_PER_CORE = BATCH // NCORES
TT = 512
TILES_PER_SEQ = S // TT
DCH = D // 128
EPS = 1e-5
DELTA_SCALE = 0.99
DELTA_MIN = 0.01
SBUF_BASE = 16512


def build_nc(spc=SEQ_PER_CORE, tps=TILES_PER_SEQ, h=H):
    nt = spc * tps
    hch = h // 128
    nhs = h // TT
    s_len = tps * TT
    nsl_fc = h // 128
    nsl_pj = D // 128

    nc = bass.Bass("TRN2", target_bir_lowering=False)

    xT = nc.dram_tensor("xT", [spc, D, s_len], F32, kind="ExternalInput").ap()
    yT = nc.dram_tensor("yT", [spc, D, s_len], F32, kind="ExternalOutput").ap()
    wfcT = nc.dram_tensor("wfcT", [D, h], F32, kind="ExternalInput").ap()
    wprojT = nc.dram_tensor("wprojT", [h, D], F32, kind="ExternalInput").ap()
    woT = nc.dram_tensor("woT", [D, D], F32, kind="ExternalInput").ap()
    wbtT = nc.dram_tensor("wbtT", [D, 32], F32, kind="ExternalInput").ap()
    wcT = nc.dram_tensor("wcT", [NS, D], F32, kind="ExternalInput").ap()
    vmat = nc.dram_tensor("vmat", [NS, D], F32, kind="ExternalInput").ap()
    uT = nc.dram_tensor("uT", [NS, h], F32, kind="ExternalInput").ap()
    fbm = nc.dram_tensor("fbm", [NS, NS], F32, kind="ExternalInput").ap()
    avec = nc.dram_tensor("avec", [NS, 1], F32, kind="ExternalInput").ap()
    bb16 = nc.dram_tensor("bb16", [NS, 1], F32, kind="ExternalInput").ap()
    bt16 = nc.dram_tensor("bt16", [NS, 1], F32, kind="ExternalInput").ap()
    gate16 = nc.dram_tensor("gate16", [NS, 1], F32, kind="ExternalInput").ap()
    bc8 = nc.dram_tensor("bc8", [128, DCH], F32, kind="ExternalInput").ap()
    bo8 = nc.dram_tensor("bo8", [128, DCH], F32, kind="ExternalInput").ap()
    dd8 = nc.dram_tensor("dd8", [128, DCH], F32, kind="ExternalInput").ap()

    wcmb_sc = nc.dram_tensor("wcmb_sc", [nhs, 128, DCH, TT], BF16, kind="Internal").ap()
    wproj_sc = nc.dram_tensor("wproj_sc", [DCH, 128, hch, 128], BF16, kind="Internal").ap()

    cur = [SBUF_BASE]

    def alloc(name, shape, dtype, at=None):
        import math
        size = math.prod(shape[1:]) * mybir.dt.size(dtype)
        size = (size + 63) // 64 * 64
        if at is None:
            at = cur[0]
            cur[0] += size
        else:
            assert at + size <= nc.SBUF_PARTITION_SIZE_BYTES, (name, at, size)
        return nc.alloc_sbuf_tensor_at(name, shape, dtype, offset=at)

    x_s = alloc("x_s", [128, 2, DCH, TT], F32)
    x_s_at = cur[0] - 2 * DCH * TT * 4
    xn32 = alloc("xn32", [128, 2, DCH, TT], F32)
    xn32_at = cur[0] - 2 * DCH * TT * 4
    xb = alloc("xb", [128, DCH, TT], BF16)
    xb_at = cur[0] - DCH * TT * 2
    yb = alloc("yb", [128, DCH, TT], BF16)
    xnb = alloc("xnb", [128, 2, DCH, TT], BF16)
    xnb_at = cur[0] - 2 * DCH * TT * 2
    r2b = alloc("r2b", [128, hch, TT], BF16)
    r2b_at = cur[0] - hch * TT * 2
    woT_b = alloc("woT_b", [128, DCH, D], BF16)
    wcmb_st = alloc("wcmb_st", [128, 2, DCH, TT], BF16)
    wcmb_st_at = cur[0] - 2 * DCH * TT * 2
    wproj_st = alloc("wproj_st", [128, 2, hch, 128], BF16)
    wproj_st_at = cur[0] - 2 * hch * 128 * 2
    statA = alloc("statA", [128, 4, TT], F32)
    statA_at = cur[0] - 4 * TT * 4
    sqbuf = alloc("sqbuf", [128, DCH, TT], BF16, at=statA_at)
    statB = alloc("statB", [128, 4, TT], F32)
    statB_at = cur[0] - 4 * TT * 4
    tmp2 = alloc("tmp2", [128, 2, TT], F32)
    scb16 = alloc("scb16", [128, TT], BF16, at=statB_at)
    hfbc = alloc("hfbc", [128, TT], F32, at=statB_at + TT * 2)
    nhfbc = alloc("nhfbc", [128, TT], F32, at=statB_at + TT * 6)
    ones_b = alloc("ones_b", [128, 128], BF16)
    ones32 = alloc("ones32", [128, 128], F32, at=statA_at)
    wcT_b = alloc("wcT_b", [NS, DCH, 128], BF16)
    v_b = alloc("v_b", [NS, DCH, 128], BF16, at=statA_at + 512)
    e1t = alloc("e1t", [NS, TT], F32)
    rsig = e1t
    aext = e1t
    but = alloc("but", [NS, TT], F32)
    ht = alloc("ht", [NS, TT], F32)
    carry = alloc("carry", [NS, 1], F32)
    hbt = alloc("hbt", [NS, TT], BF16)
    fbT_b = alloc("fbT_b", [NS, NS], BF16, at=statA_at + 2560)
    A_s = alloc("A_s", [NS, 1], F32)
    A99 = A_s
    A01 = alloc("A01", [NS, 1], F32)
    bb_s = alloc("bb_s", [NS, 1], F32)
    bt_s = alloc("bt_s", [NS, 1], F32)
    nbt = bt_s
    gate_s = alloc("gate_s", [NS, 1], F32, at=statA_at + 2624)
    bc_s = alloc("bc_s", [128, DCH], F32)
    bo_s = alloc("bo_s", [128, DCH], F32)
    dd_s = alloc("dd_s", [128, DCH], F32)
    wbtT_b = alloc("wbtT_b", [128, DCH, 32], BF16)
    assert cur[0] <= nc.SBUF_PARTITION_SIZE_BYTES, cur[0]

    r2b_size = hch * TT * 2
    if r2b_size >= 32768:
        GTb = alloc("GTb", [NS, h], BF16, at=wcmb_st_at)
        uT_b = alloc("uT_b", [NS, h], BF16, at=wcmb_st_at + 2 * h)
        cb1 = alloc("cb1", [128, DCH, TT], BF16, at=xnb_at)
        cb2 = alloc("cb2", [128, DCH, TT], BF16, at=r2b_at)
        dbuf = alloc("dbuf", [128, DCH, TT], BF16, at=r2b_at + 8192)
        wsl = alloc("wsl", [128, DCH, TT], F32, at=r2b_at + 16384)
        wsl2 = alloc("wsl2", [128, hch, 128], F32, at=r2b_at + 16384)
        aw32 = alloc("aw32", [128, DCH, TT], F32, at=x_s_at)
        aw32p = alloc("aw32p", [128, hch, 128], F32, at=x_s_at)
        aw32B = alloc("aw32B", [128, DCH, TT], F32, at=xn32_at)
        aw32pB = alloc("aw32pB", [128, hch, 128], F32, at=xn32_at)
        wslB = alloc("wslB", [128, DCH, TT], F32, at=x_s_at + 16384)
        wsl2B = alloc("wsl2B", [128, hch, 128], F32, at=x_s_at + 16384)
        wr8 = alloc("wr8", [128, 2, DCH, TT], BF16, at=xb_at)
        tls = alloc("tls", [128, DCH, TT], F32, at=r2b_at)
    else:
        GTb = alloc("GTb", [NS, h], BF16)
        uT_b = alloc("uT_b", [NS, h], BF16)
        cb1 = alloc("cb1", [128, DCH, TT], BF16, at=xnb_at)
        cb2 = alloc("cb2", [128, DCH, TT], BF16, at=xn32_at)
        dbuf = alloc("dbuf", [128, DCH, TT], BF16, at=xn32_at + DCH * TT * 2)
        wsl = alloc("wsl", [128, DCH, TT], F32)
        wsl_at = cur[0] - DCH * TT * 4
        wsl2 = alloc("wsl2", [128, hch, 128], F32, at=wsl_at)
        aw32 = alloc("aw32", [128, DCH, TT], F32, at=x_s_at)
        aw32p = alloc("aw32p", [128, hch, 128], F32, at=x_s_at)
        aw32B = aw32
        aw32pB = aw32p
        wslB = alloc("wslB", [128, DCH, TT], F32, at=x_s_at + 16384)
        wsl2B = alloc("wsl2B", [128, hch, 128], F32, at=x_s_at + 16384)
        wr8 = alloc("wr8", [128, 2, DCH, TT], BF16, at=xb_at)
        tls = alloc("tls", [128, DCH, TT], F32, at=wsl_at)

    pb = [nc.alloc_psum_tensor(f"pb{i}", [128, TT], F32) for i in range(8)]

    sems = {}
    ctr = {}
    for name in ["s_x0", "s_x1", "s_cst", "s_out0", "s_out1",
                 "s_wc0", "s_wc1", "s_wp0", "s_wp1", "s_sc0", "s_sc1",
                 "s_wl0", "s_wl1", "s_sml", "s_T", "s_V", "s_S"]:
        sems[name] = nc.alloc_semaphore(name)
        ctr[name] = 0
    lastwait = {}

    def inc(ins, name, k=1):
        ins.then_inc(sems[name], k)
        ctr[name] += k
        return ctr[name]

    def wait(e, ename, name, val):
        if val is None or val <= 0:
            return
        key = (ename, name)
        if lastwait.get(key, 0) >= val:
            return
        e.wait_ge(sems[name], val)
        lastwait[key] = val

    blk = nc.Block()
    blk.__enter__()

    def wcmb_sem(hs):
        return "s_wc%d" % (hs % 2)

    def wcmb_ld(i, hs):
        return 16 * (i * (nhs // 2) + hs // 2 + 1)

    def wproj_sem(oc):
        return "s_wp%d" % (oc % 2)

    def wproj_ld(i, oc):
        return 16 * (i * 4 + oc // 2 + 1)
    wcmb_use = {}
    wproj_use = {}

    def x_sem(i):
        return "s_x%d" % (i % 2)

    def xin_count(i):
        return 16 * (i // 2 + 1)
    CST_ALL = 96

    def xin_ap(i):
        seq, q = i // tps, i % tps
        return xT[seq, :, q * TT:(q + 1) * TT].rearrange("(c p) t -> p c t", p=128)

    def yout_ap(i):
        seq, q = i // tps, i % tps
        return yT[seq, :, q * TT:(q + 1) * TT].rearrange("(c p) t -> p c t", p=128)

    def _loads(sy):
        for dst, src in [
            (A_s[:], avec), (bb_s[:], bb16), (bt_s[:], bt16),
            (gate_s[:], gate16), (bc_s[:], bc8), (bo_s[:], bo8), (dd_s[:], dd8),
        ]:
            inc(sy.dma_start(out=dst, in_=src), "s_sml", 16)
    blk.sync(_loads)
    sml_done = ctr["s_sml"]

    def _gx(g):
        inc(g.dma_start(out=uT_b[:], in_=uT), "s_cst", 16)
        inc(g.dma_start(out=v_b[:], in_=vmat.rearrange("n (c m) -> n c m", c=DCH)),
            "s_cst", 16)
        inc(g.dma_start(out=wcT_b[:], in_=wcT.rearrange("n (c m) -> n c m", c=DCH)),
            "s_cst", 16)
        inc(g.dma_start(out=wbtT_b[:], in_=wbtT.rearrange("(c p) m -> p c m", p=128)),
            "s_cst", 16)
        inc(g.dma_start(out=fbT_b[:], in_=fbm), "s_cst", 16)
        inc(g.dma_start(out=woT_b[:], in_=woT.rearrange("(c p) m -> p c m", p=128)),
            "s_cst", 16)
        assert ctr["s_cst"] == CST_ALL
    blk.gpsimd(_gx)

    def _memsets(ve):
        ve.memset(ones_b[:], 1.0)
        ve.memset(ones32[:], 1.0)
        wait(ve, "vector", "s_sml", sml_done)
        ve.tensor_scalar(out=A01[:], in0=A_s[:], scalar1=DELTA_MIN, scalar2=None, op0=ALU.mult)
        inc(ve.tensor_scalar(out=nbt[:], in0=bt_s[:], scalar1=-1.0, scalar2=None, op0=ALU.mult),
            "s_V")
    blk.vector(_memsets)

    def _a99(ve):
        wait(ve, "vector", "s_V", ctr["s_V"])
        inc(ve.tensor_scalar(out=A99[:], in0=A_s[:], scalar1=DELTA_SCALE,
                             scalar2=None, op0=ALU.mult), "s_V")
    blk.vector(_a99)
    memset_done = ctr["s_V"]

    ngt = h // TT
    gt_v0 = ctr["s_V"]

    def _gt_mm(te):
        wait(te, "tensor", "s_cst", CST_ALL)
        wait(te, "tensor", "s_V", memset_done)
        for j in range(ngt):
            if j >= 1:
                wait(te, "tensor", "s_V", gt_v0 + j)
            inc(te.matmul(pb[4][0:NS, :], fbT_b[:], uT_b[:, j * TT:(j + 1) * TT],
                          start=True, stop=True), "s_T")
    blk.tensor(_gt_mm)
    gt_t0 = ctr["s_T"] - ngt

    def _gt_evac(ve):
        for j in range(ngt):
            wait(ve, "vector", "s_T", gt_t0 + j + 1)
            inc(ve.tensor_scalar(out=GTb[:, j * TT:(j + 1) * TT], in0=pb[4][0:NS, :],
                                 scalar1=gate_s[:, 0:1], scalar2=None, op0=ALU.mult), "s_V")
    blk.vector(_gt_evac)
    gt_evac_done = ctr["s_V"]

    last_sv = 0
    def scw_sem(j):
        return "s_sc%d" % (j % 2)

    def scw_count(j):
        return 16 * (j // 2 + 1)

    summ_hist = {}
    cmp_hist = {}
    sc_hist = {}
    wadd_hist = {}
    wrp_hist = {}
    slice_c_done = {}
    pend = []

    def emit_fold(hs, wrp_done):
        LRB = [1, 2, 4, 5, 6, 7]
        mm_done = {}
        for dc in range(DCH):
            b = LRB[dc % 6]

            def _wlr(te, hs=hs, dc=dc, b=b):
                wait(te, "tensor", "s_cst", CST_ALL)
                if dc >= 6:
                    prev = wadd_hist.get((hs, dc - 6), 0)
                else:
                    pdc = dc + 6 if dc < 2 else dc
                    prev = wadd_hist.get((hs - 1, pdc), 0)
                wait(te, "tensor", "s_V", max(gt_evac_done, prev))
                mm_done[dc] = inc(
                    te.matmul(pb[b][:], v_b[:, dc], GTb[:, hs * TT:(hs + 1) * TT],
                              start=True, stop=True), "s_T")
            blk.tensor(_wlr)

            def _wadd(ve, hs=hs, dc=dc, b=b, wrp_done=wrp_done):
                wait(ve, "vector", "s_V", wrp_done)
                wait(ve, "vector", "s_T", mm_done[dc])
                inc(ve.tensor_tensor(out=wr8[:, hs % 2, dc], in0=wr8[:, hs % 2, dc],
                                     in1=pb[b][:], op=ALU.add), "s_V")
                wadd_hist[(hs, dc)] = ctr["s_V"]
            blk.vector(_wadd)
        slice_c_done[hs] = ctr["s_V"]

        def _wr(g, hs=hs):
            wait(g, "gpsimd", "s_V", slice_c_done[hs])
            assert inc(g.dma_start(out=wcmb_sc[hs], in_=wr8[:, hs % 2]),
                       scw_sem(hs), 16) == scw_count(hs)
        blk.gpsimd(_wr)

    def wldq(j):
        return "s_wl%d" % (j % 2)

    def wldc(j):
        return 16 * (j // 2 + 1)

    for hs in range(nhs):
        wv = wsl if hs % 2 == 0 else wslB

        def _ld(sy, hs=hs, wv=wv):
            if hs >= 2:
                wait(sy, "sync", "s_V", cmp_hist[hs - 2])
            inc(sy.dma_start(
                out=wv[:],
                in_=wfcT[:, hs * TT:(hs + 1) * TT].rearrange("(c p) h -> p c h", p=128),
            ), wldq(hs), 16)
        blk.sync(_ld)
        ld_done = ctr[wldq(hs)]

        awdb = 2 if aw32B is not aw32 else 1
        av = aw32 if hs % 2 == 0 else aw32B

        def _abs(sc, hs=hs, wv=wv, av=av, awdb=awdb, ld_done=ld_done):
            wait(sc, "scalar", wldq(hs), ld_done)
            if hs >= awdb:
                wait(sc, "scalar", "s_T", summ_hist[hs - awdb])
            inc(sc.activation(out=av[:], in_=wv[:], func=AF.Abs), "s_S")
        blk.scalar(_abs)
        abs_done = ctr["s_S"]

        def _summ(te, hs=hs, av=av, abs_done=abs_done):
            wait(te, "tensor", "s_S", abs_done)
            wait(te, "tensor", "s_V", sc_hist.get(hs - 1, 0))
            for dc in range(DCH):
                i2 = te.matmul(pb[0][:], ones32[:], av[:, dc],
                               start=(dc == 0), stop=(dc == DCH - 1))
            inc(i2, "s_T")
        blk.tensor(_summ)
        bc_done = summ_hist[hs] = ctr["s_T"]

        def _sc_evac(ve, hs=hs, bc_done=bc_done):
            wait(ve, "vector", "s_T", bc_done)
            wait(ve, "vector", "s_V", wrp_hist.get(hs - 1, 0))
            W = float(D)
            ve.tensor_scalar(out=scb16[:], in0=pb[0][:], scalar1=1.0 / W,
                             scalar2=1e-5, op0=ALU.mult, op1=ALU.max)
            ve.tensor_scalar(out=hfbc[:], in0=pb[0][:], scalar1=0.5 / W,
                             scalar2=0.5e-5, op0=ALU.mult, op1=ALU.max)
            inc(ve.tensor_scalar(out=nhfbc[:], in0=pb[0][:], scalar1=-0.5 / W,
                                 scalar2=-0.5e-5, op0=ALU.mult, op1=ALU.min), "s_V")
        blk.vector(_sc_evac)
        sc_done = sc_hist[hs] = ctr["s_V"]

        def _cmp(ve, hs=hs, wv=wv, ld_done=ld_done, sc_done=sc_done):
            wait(ve, "vector", "s_V", sc_done)
            wait(ve, "vector", wldq(hs), ld_done)
            if hs >= 2:
                wait(ve, "vector", scw_sem(hs - 2), scw_count(hs - 2))
            hf = hfbc[:, None, :].broadcast_to([128, DCH, TT])
            nhf = nhfbc[:, None, :].broadcast_to([128, DCH, TT])
            ve.tensor_tensor(out=cb1[:], in0=wv[:], in1=hf, op=ALU.is_gt)
            inc(ve.tensor_tensor(out=cb2[:], in0=wv[:], in1=nhf, op=ALU.is_lt),
                "s_V")
        blk.vector(_cmp)
        cmp_done = cmp_hist[hs] = ctr["s_V"]

        def _dsub(ve, cmp_done=cmp_done):
            wait(ve, "vector", "s_V", cmp_done)
            inc(ve.tensor_tensor(out=dbuf[:].rearrange("p c t -> p (c t)"),
                                 in0=cb1[:].rearrange("p c t -> p (c t)"),
                                 in1=cb2[:].rearrange("p c t -> p (c t)"),
                                 op=ALU.subtract), "s_V")
        blk.vector(_dsub)
        d_done = ctr["s_V"]

        def _wrp(ve, hs=hs, d_done=d_done):
            wait(ve, "vector", "s_V", d_done)
            sc = scb16[:, None, :].broadcast_to([128, DCH, TT])
            inc(ve.tensor_tensor(out=wr8[:, hs % 2], in0=dbuf[:], in1=sc,
                                 op=ALU.mult), "s_V")
        blk.vector(_wrp)
        wrp_done = wrp_hist[hs] = ctr["s_V"]

        pend.append((hs, wrp_done))
        if len(pend) > 1:
            emit_fold(*pend.pop(0))

    if pend:
        emit_fold(*pend.pop(0))
    last_sv = ctr["s_V"]

    for os_ in range(DCH):
        j = nhs + os_

        wv2 = wsl2 if j % 2 == 0 else wsl2B

        def _ld2(sy, j=j, wv2=wv2):
            wait(sy, "sync", "s_V", cmp_hist[j - 2])
            inc(sy.dma_start(
                out=wv2[:],
                in_=wprojT[:, os_ * 128:(os_ + 1) * 128].rearrange("(c p) o -> p c o", p=128),
            ), wldq(j), 16)
        blk.sync(_ld2)
        ld_done = ctr[wldq(j)]

        awdb = 2 if aw32pB is not aw32p else 1
        av2 = aw32p if j % 2 == 0 else aw32pB

        def _abs2(sc, j=j, wv2=wv2, av2=av2, awdb=awdb, ld_done=ld_done):
            wait(sc, "scalar", wldq(j), ld_done)
            wait(sc, "scalar", "s_T", summ_hist[j - awdb])
            inc(sc.activation(out=av2[:], in_=wv2[:], func=AF.Abs), "s_S")
        blk.scalar(_abs2)
        abs_done = ctr["s_S"]

        def _summ2(te, j=j, av2=av2, abs_done=abs_done):
            wait(te, "tensor", "s_S", abs_done)
            wait(te, "tensor", "s_V", sc_hist.get(j - 1, 0))
            for kc in range(hch):
                i2 = te.matmul(pb[0][:, 0:128], ones32[:], av2[:, kc],
                               start=(kc == 0), stop=(kc == hch - 1))
            inc(i2, "s_T")
        blk.tensor(_summ2)
        bc_done = summ_hist[j] = ctr["s_T"]

        def _sc_evac2(ve, j=j, bc_done=bc_done):
            wait(ve, "vector", "s_T", bc_done)
            wait(ve, "vector", "s_V", wrp_hist.get(j - 1, 0))
            W = float(h)
            ve.tensor_scalar(out=scb16[:, 0:128], in0=pb[0][:, 0:128], scalar1=1.0 / W,
                             scalar2=1e-5, op0=ALU.mult, op1=ALU.max)
            ve.tensor_scalar(out=hfbc[:, 0:128], in0=pb[0][:, 0:128], scalar1=0.5 / W,
                             scalar2=0.5e-5, op0=ALU.mult, op1=ALU.max)
            inc(ve.tensor_scalar(out=nhfbc[:, 0:128], in0=pb[0][:, 0:128],
                                 scalar1=-0.5 / W, scalar2=-0.5e-5,
                                 op0=ALU.mult, op1=ALU.min), "s_V")
        blk.vector(_sc_evac2)
        neg_done = sc_hist[j] = ctr["s_V"]

        def _cmp2(ve, j=j, wv2=wv2, ld_done=ld_done, neg_done=neg_done):
            wait(ve, "vector", "s_V", neg_done)
            wait(ve, "vector", wldq(j), ld_done)
            if j >= 2:
                wait(ve, "vector", scw_sem(j - 2), scw_count(j - 2))
            hf = hfbc[:, None, 0:128].broadcast_to([128, hch, 128])
            nhf = nhfbc[:, None, 0:128].broadcast_to([128, hch, 128])
            cb1v = cb1[:].rearrange("p c (a b) -> p (c a) b", b=128)[:, 0:hch, :]
            cb2v = cb2[:].rearrange("p c (a b) -> p (c a) b", b=128)[:, 0:hch, :]
            ve.tensor_tensor(out=cb1v, in0=wv2[:], in1=hf, op=ALU.is_gt)
            inc(ve.tensor_tensor(out=cb2v, in0=wv2[:], in1=nhf, op=ALU.is_lt), "s_V")
        blk.vector(_cmp2)
        cmp_done = cmp_hist[j] = ctr["s_V"]

        def _dsub2(ve, cmp_done=cmp_done):
            wait(ve, "vector", "s_V", cmp_done)
            inc(ve.tensor_tensor(
                out=dbuf[:].rearrange("p c (a b) -> p (c a) b", b=128)[:, 0:hch, :],
                in0=cb1[:].rearrange("p c (a b) -> p (c a) b", b=128)[:, 0:hch, :],
                in1=cb2[:].rearrange("p c (a b) -> p (c a) b", b=128)[:, 0:hch, :],
                op=ALU.subtract), "s_V")
        blk.vector(_dsub2)
        d_done = ctr["s_V"]

        def _wrp2(ve, os_=os_, j=j, d_done=d_done):
            wait(ve, "vector", "s_V", d_done)
            sc = scb16[:, None, 0:128].broadcast_to([128, hch, 128])
            inc(ve.tensor_tensor(
                out=wr8[:, j % 2].rearrange("p c (a b) -> p (c a) b", b=128)[:, 0:hch, :],
                in0=dbuf[:].rearrange("p c (a b) -> p (c a) b", b=128)[:, 0:hch, :],
                in1=sc, op=ALU.mult), "s_V")
        blk.vector(_wrp2)
        wrp_hist[j] = ctr["s_V"]
        slice2_c_done = slice2_c_done if os_ else {}
        slice2_c_done[os_] = ctr["s_V"]

        def _wr2(g, os_=os_, j=j):
            wait(g, "gpsimd", "s_V", slice2_c_done[os_])
            assert inc(g.dma_start(
                out=wproj_sc[os_],
                in_=wr8[:, j % 2].rearrange("p c (a b) -> p (c a) b", b=128)[:, 0:hch, :]),
                scw_sem(j), 16) == scw_count(j)
        blk.gpsimd(_wr2)
        last_sv = ctr["s_V"]

    sc0_total, sc1_total = ctr["s_sc0"], ctr["s_sc1"]
    tern_sv_total = ctr["s_V"]
    sums_total = ctr["s_T"]

    def _gx2(g):
        wait(g, "gpsimd", "s_T", summ_hist[nhs + DCH - 1])
        wait(g, "gpsimd", "s_V", cmp_hist[nhs + DCH - 1])
        inc(g.dma_start(out=x_s[:, 0], in_=xin_ap(0)), "s_x0", 16)
        inc(g.dma_start(out=x_s[:, 1], in_=xin_ap(1)), "s_x1", 16)
    blk.gpsimd(_gx2)

    mA, uA, auxA, rstdA = (statA[:, k] for k in range(4))
    qA = mA
    mB, uB, auxB, rstdB = (statB[:, k] for k in range(4))
    qB = mB

    hb_done = {}
    butp_done = {}
    ln1c_done = {}
    ln2_done = {}
    yb_done = {}
    wc_mm = {}
    wo_mm = {}
    wo_stt_done = {}
    final_done = {}
    out_done = {}
    r2_done = {}
    ffa_last = {}
    back1_sqh = {}
    sq_chunks = {}
    ffb_last = {}
    run = {"statsv1": 0, "wo_stt": 0, "final": 0, "yb": 0, "tmp2": 0,
           "tls": 0, "nz": 0}

    def emit_front1(j):
        p = j % 2
        src = x_s[:, p]

        def _sq(ve, j=j):
            if j == 0:
                wait(ve, "vector", "s_V", tern_sv_total)
                wait(ve, "vector", "s_sc0", sc0_total)
                wait(ve, "vector", "s_sc1", sc1_total)
            if j >= 1:
                wait(ve, "vector", "s_V", ln1c_done[j - 1])
            wait(ve, "vector", x_sem(j), xin_count(j))
            ve.tensor_tensor(out=sqbuf[:], in0=src, in1=src, op=ALU.mult)
            inc(ve.tensor_copy(out=xb[:], in_=src), "s_V")
        blk.vector(_sq)
        sq_done = ctr["s_V"]

        def _stats1(te, sq_done=sq_done):
            wait(te, "tensor", "s_V", sq_done)
            for dc in range(DCH):
                te.matmul(pb[0][:], ones_b[:], xb[:, dc],
                          start=(dc == 0), stop=(dc == DCH - 1))
            for dc in range(DCH):
                i2 = te.matmul(pb[1][:], ones_b[:], sqbuf[:, dc],
                               start=(dc == 0), stop=(dc == DCH - 1))
            inc(i2, "s_T")
        blk.tensor(_stats1)
        st_done = ctr["s_T"]

        def _sv1(ve, st_done=st_done):
            wait(ve, "vector", "s_T", st_done)
            ve.tensor_scalar(out=mA, in0=pb[0][:], scalar1=1.0 / D,
                             scalar2=None, op0=ALU.mult)
            inc(ve.tensor_scalar(out=uA, in0=pb[1][:], scalar1=1.0 / D,
                                 scalar2=EPS, op0=ALU.mult, op1=ALU.add), "s_V")
        blk.vector(_sv1)
        run["statsv1"] = mu = ctr["s_V"]

        def _msq(ve, mu=mu):
            wait(ve, "vector", "s_V", mu)
            inc(ve.tensor_tensor(out=auxA, in0=mA, in1=mA, op=ALU.mult), "s_V")
        blk.vector(_msq)
        ms = ctr["s_V"]

        def _varr(ve, ms=ms):
            wait(ve, "vector", "s_V", ms)
            inc(ve.tensor_tensor(out=uA, in0=uA, in1=auxA, op=ALU.subtract), "s_V")
        blk.vector(_varr)
        va = ctr["s_V"]

        def _lnv(sc, va=va):
            wait(sc, "scalar", "s_V", va)
            inc(sc.activation(out=auxA, in_=uA, func=AF.Ln), "s_S")
        blk.scalar(_lnv)
        lv = ctr["s_S"]

        def _nh(ve, lv=lv):
            wait(ve, "vector", "s_S", lv)
            inc(ve.tensor_scalar(out=auxA, in0=auxA, scalar1=-0.5,
                                 scalar2=None, op0=ALU.mult), "s_V")
        blk.vector(_nh)
        nh = ctr["s_V"]

        def _rstd(sc, nh=nh):
            wait(sc, "scalar", "s_V", nh)
            inc(sc.activation(out=rstdA, in_=auxA, func=AF.Exp), "s_S")
        blk.scalar(_rstd)
        rs = ctr["s_S"]

        return rs

    def emit_front1b(j, rs, ffb_guard):
        p = j % 2
        src = x_s[:, p]

        def _m1(ve, j=j, rs=rs, ffb_guard=ffb_guard):
            wait(ve, "vector", "s_S", rs)
            wait(ve, "vector", "s_V", run["tls"])
            wait(ve, "vector", "s_T", ffb_guard)
            ve.tensor_tensor(out=qA, in0=mA, in1=rstdA, op=ALU.mult)
            last = None
            for dc in range(DCH):
                last = ve.tensor_tensor(out=tls[:, dc], in0=src[:, dc],
                                        in1=rstdA, op=ALU.mult)
            inc(last, "s_V")
        blk.vector(_m1)
        md = ctr["s_V"]

        def _s1(ve, j=j, md=md):
            wait(ve, "vector", "s_V", md)
            last = None
            for dc in range(DCH):
                last = ve.tensor_tensor(out=xnb[:, p, dc], in0=tls[:, dc],
                                        in1=qA, op=ALU.subtract)
            inc(last, "s_V")
        blk.vector(_s1)
        ln1c_done[j] = run["tls"] = ctr["s_V"]

    def emit_front2(j):
        p = j % 2
        seq_start = (j % tps) == 0

        def _butp(te, j=j):
            wait(te, "tensor", "s_V", ln1c_done[j])
            for dc in range(DCH):
                te.matmul(pb[2][0:NS, :], wbtT_b[:, dc, 0:NS], xnb[:, p, dc],
                          start=(dc == 0), stop=(dc == DCH - 1))
            for dc in range(DCH):
                i2 = te.matmul(pb[3][0:NS, :], wbtT_b[:, dc, NS:2 * NS], xnb[:, p, dc],
                               start=(dc == 0), stop=(dc == DCH - 1))
            inc(i2, "s_T")
        blk.tensor(_butp)
        butp_done[j] = bp = ctr["s_T"]

        def _nz(ve, bp=bp):
            wait(ve, "vector", "s_T", bp)
            ve.tensor_scalar(out=e1t[:], in0=pb[3][0:NS, :], scalar1=-1.0,
                             scalar2=nbt[:, 0:1], op0=ALU.mult, op1=ALU.add)
            inc(ve.tensor_scalar(out=but[:], in0=pb[2][0:NS, :],
                                 scalar1=bb_s[:, 0:1], scalar2=None, op0=ALU.add), "s_V")
        blk.vector(_nz)
        nz = run["nz"] = ctr["s_V"]

        def _e1(sc, nz=nz):
            wait(sc, "scalar", "s_V", nz)
            inc(sc.activation(out=e1t[:], in_=e1t[:], func=AF.Exp), "s_S")
        blk.scalar(_e1)
        e1 = ctr["s_S"]

        def _e1p(ve, e1=e1):
            wait(ve, "vector", "s_S", e1)
            inc(ve.tensor_scalar(out=e1t[:], in0=e1t[:], scalar1=1.0,
                                 scalar2=None, op0=ALU.add), "s_V")
        blk.vector(_e1p)
        ep = ctr["s_V"]

        def _rsig(ve, ep=ep):
            wait(ve, "vector", "s_V", ep)
            inc(ve.reciprocal(out=rsig[:], in_=e1t[:]), "s_V")
        blk.vector(_rsig)
        rg = ctr["s_V"]

        def _zt(ve, rg=rg):
            wait(ve, "vector", "s_V", rg)
            inc(ve.tensor_scalar(out=rsig[:], in0=rsig[:], scalar1=A99[:, 0:1],
                                 scalar2=A01[:, 0:1], op0=ALU.mult, op1=ALU.add), "s_V")
        blk.vector(_zt)
        zt = ctr["s_V"]

        def _aext(sc, zt=zt):
            wait(sc, "scalar", "s_V", zt)
            inc(sc.activation(out=aext[:], in_=rsig[:], func=AF.Exp), "s_S")
        blk.scalar(_aext)
        ax = ctr["s_S"]

        def _scan(ve, j=j, seq_start=seq_start, ax=ax):
            wait(ve, "vector", "s_S", ax)
            init = 0.0 if seq_start else carry[:, 0:1]
            inc(ve.tensor_tensor_scan(out=ht[:], data0=aext[:], data1=but[:],
                                      initial=init, op0=ALU.mult, op1=ALU.add), "s_V")
        blk.vector(_scan)
        sn = ctr["s_V"]

        def _hb(ve, j=j, sn=sn):
            wait(ve, "vector", "s_V", sn)
            ve.tensor_copy(out=carry[:], in_=ht[:, TT - 1:TT])
            inc(ve.tensor_copy(out=hbt[:], in_=ht[:]), "s_V")
        blk.vector(_hb)
        hb_done[j] = ctr["s_V"]

    def emit_back1(j):
        p = j % 2
        wcd = {}
        V0 = ctr["s_V"]

        def _wc(te, j=j, V0=V0):
            wait(te, "tensor", "s_V", hb_done[j])
            wait(te, "tensor", "s_V", run["statsv1"])
            for dc in range(DCH):
                b = dc % 2
                if dc >= 2:
                    wait(te, "tensor", "s_V", V0 + ((dc - 2) // 2 + 1) * 2)
                wcd[dc] = inc(
                    te.matmul(pb[b][:], wcT_b[:, dc], hbt[:], start=True, stop=True), "s_T")
        blk.tensor(_wc)
        wc_mm[j] = dict(wcd)

        prev_ybp = 0
        for pp in range(DCH // 2):
            def _t2(ve, j=j, pp=pp, prev_ybp=prev_ybp):
                if pp:
                    wait(ve, "vector", "s_V", prev_ybp)
                else:
                    wait(ve, "vector", "s_V", run["tmp2"])
                ve.tensor_scalar(out=tmp2[:, 0], in0=xnb[:, p, 2 * pp],
                                 scalar1=dd_s[:, 2 * pp:2 * pp + 1], scalar2=None,
                                 op0=ALU.mult)
                inc(ve.tensor_scalar(out=tmp2[:, 1], in0=xnb[:, p, 2 * pp + 1],
                                     scalar1=dd_s[:, 2 * pp + 1:2 * pp + 2], scalar2=None,
                                     op0=ALU.mult), "s_V")
            blk.vector(_t2)
            t2d = ctr["s_V"]

            def _ybp(ve, j=j, pp=pp, t2d=t2d):
                wait(ve, "vector", "s_V", t2d)
                for q in range(2):
                    dc = 2 * pp + q
                    wait(ve, "vector", "s_T", wcd[dc])
                    ins = ve.scalar_tensor_tensor(out=yb[:, dc], in0=pb[dc % 2][:],
                                                  scalar=bc_s[:, dc:dc + 1],
                                                  in1=tmp2[:, q],
                                                  op0=ALU.add, op1=ALU.add)
                inc(ins, "s_V")
            blk.vector(_ybp)
            prev_ybp = ctr["s_V"]
        yb_done[j] = run["yb"] = run["tmp2"] = ctr["s_V"]

        wod = {}
        V1 = ctr["s_V"]

        STTN = [1, 2, 3, 4, 6, 7, 8, 9]

        def _wo(te, j=j, V1=V1):
            wait(te, "tensor", "s_V", yb_done[j])
            for oc in range(DCH):
                b = 4 + oc % 2
                if oc >= 2:
                    wait(te, "tensor", "s_V", V1 + STTN[oc - 2])
                for kc in range(DCH):
                    i2 = te.matmul(pb[b][:], woT_b[:, kc, oc * 128:(oc + 1) * 128],
                                   yb[:, kc], start=(kc == 0), stop=(kc == DCH - 1))
                wod[oc] = inc(i2, "s_T")
        blk.tensor(_wo)
        wo_mm[j] = dict(wod)

        sqh = {}

        def _wo_stt(ve, j=j):
            if j >= 2:
                wait(ve, "vector", "s_out%d" % (j % 2), 16 * ((j - 2) // 2 + 1))
            for g2 in range(2):
                for q in range(4):
                    oc = 4 * g2 + q
                    b = 4 + oc % 2
                    wait(ve, "vector", "s_T", wod[oc])
                    inc(ve.scalar_tensor_tensor(out=xn32[:, p, oc], in0=pb[b][:],
                                                scalar=bo_s[:, oc:oc + 1],
                                                in1=x_s[:, p, oc],
                                                op0=ALU.add, op1=ALU.add), "s_V")
                hw = ctr["s_V"]

                def _sqh(ve2, j=j, g2=g2, hw=hw):
                    wait(ve2, "vector", "s_V", hw)
                    sl = slice(4 * g2, 4 * g2 + 4)
                    inc(ve2.tensor_tensor(out=xnb[:, p, sl], in0=xn32[:, p, sl],
                                          in1=xn32[:, p, sl], op=ALU.mult), "s_V")
                _sqh(ve)
                sqh[g2] = ctr["s_V"]
            inc(ve.tensor_copy(out=yb[:], in_=xn32[:, p]), "s_V")
            sqh[2] = ctr["s_V"]
        blk.vector(_wo_stt)
        wo_stt_done[j] = run["wo_stt"] = ws = ctr["s_V"]

        if j + 2 < nt:
            def _pref(g, j=j, ws=ws):
                wait(g, "gpsimd", "s_V", ws)
                assert inc(g.dma_start(out=x_s[:, j % 2], in_=xin_ap(j + 2)),
                           x_sem(j), 16) == xin_count(j + 2)
            blk.gpsimd(_pref)

        back1_sqh[j] = dict(sqh)
        return ctr["s_V"]

    def emit_back2(j, sq2_done):
        p = j % 2

        def _stats2(te, j=j, sq2_done=sq2_done):
            wait(te, "tensor", "s_V", back1_sqh[j][0])
            wait(te, "tensor", "s_V", run["final"])
            for dc in range(4):
                te.matmul(pb[7][:], ones_b[:], xnb[:, p, dc],
                          start=(dc == 0), stop=False)
            wait(te, "tensor", "s_V", back1_sqh[j][1])
            for dc in range(4, DCH):
                te.matmul(pb[7][:], ones_b[:], xnb[:, p, dc],
                          start=False, stop=(dc == DCH - 1))
            wait(te, "tensor", "s_V", back1_sqh[j][2])
            for dc in range(DCH):
                i2 = te.matmul(pb[6][:], ones_b[:], yb[:, dc],
                               start=(dc == 0), stop=(dc == DCH - 1))
            inc(i2, "s_T")
        blk.tensor(_stats2)
        st = ctr["s_T"]

        def _sv2(ve, st=st):
            wait(ve, "vector", "s_T", st)
            ve.tensor_scalar(out=mB, in0=pb[6][:], scalar1=1.0 / D,
                             scalar2=None, op0=ALU.mult)
            inc(ve.tensor_scalar(out=uB, in0=pb[7][:], scalar1=1.0 / D,
                                 scalar2=EPS, op0=ALU.mult, op1=ALU.add), "s_V")
        blk.vector(_sv2)
        mu = ctr["s_V"]

        def _msq2(ve, mu=mu):
            wait(ve, "vector", "s_V", mu)
            inc(ve.tensor_tensor(out=auxB, in0=mB, in1=mB, op=ALU.mult), "s_V")
        blk.vector(_msq2)
        ms = ctr["s_V"]

        def _varr2(ve, ms=ms):
            wait(ve, "vector", "s_V", ms)
            inc(ve.tensor_tensor(out=uB, in0=uB, in1=auxB, op=ALU.subtract), "s_V")
        blk.vector(_varr2)
        va = ctr["s_V"]

        def _lnv2(sc, va=va):
            wait(sc, "scalar", "s_V", va)
            inc(sc.activation(out=auxB, in_=uB, func=AF.Ln), "s_S")
        blk.scalar(_lnv2)
        lv = ctr["s_S"]

        def _nh2(ve, lv=lv):
            wait(ve, "vector", "s_S", lv)
            inc(ve.tensor_scalar(out=auxB, in0=auxB, scalar1=-0.5,
                                 scalar2=None, op0=ALU.mult), "s_V")
        blk.vector(_nh2)
        nh = ctr["s_V"]

        def _rstd2(sc, nh=nh):
            wait(sc, "scalar", "s_V", nh)
            inc(sc.activation(out=rstdB, in_=auxB, func=AF.Exp), "s_S")
        blk.scalar(_rstd2)
        rs = ctr["s_S"]

        def _m2(ve, j=j, rs=rs):
            wait(ve, "vector", "s_S", rs)
            wait(ve, "vector", "s_V", run["tls"])
            ve.tensor_tensor(out=qB, in0=mB, in1=rstdB, op=ALU.mult)
            last = None
            for dc in range(DCH):
                last = ve.tensor_tensor(out=tls[:, dc], in0=xn32[:, p, dc],
                                        in1=rstdB, op=ALU.mult)
            inc(last, "s_V")
        blk.vector(_m2)
        md = ctr["s_V"]

        def _s2(ve, j=j, md=md):
            wait(ve, "vector", "s_V", md)
            last = None
            for dc in range(DCH):
                last = ve.tensor_tensor(out=xnb[:, p, dc], in0=tls[:, dc],
                                        in1=qB, op=ALU.subtract)
            inc(last, "s_V")
        blk.vector(_s2)
        ln2_done[j] = run["tls"] = ctr["s_V"]

    def emit_ffnA(i):
        p = i % 2
        fad = {}
        S0 = ctr["s_S"]

        def _ffa(te, i=i, S0=S0):
            wait(te, "tensor", "s_V", ln2_done[i])
            wait(te, "tensor", "s_V", run["nz"])
            for hc in range(hch):
                hs = hc // 4
                wait(te, "tensor", wcmb_sem(hs), wcmb_ld(i, hs))
                if hc >= 2:
                    wait(te, "tensor", "s_S", S0 + hc - 1)
                b = 2 + hc % 2
                for dc in range(DCH):
                    i2 = te.matmul(pb[b][:],
                                   wcmb_st[:, hs % 2, dc, (hc % 4) * 128:(hc % 4 + 1) * 128],
                                   xnb[:, p, dc], start=(dc == 0), stop=(dc == DCH - 1))
                fad[hc] = inc(i2, "s_T")
                if hc % 4 == 3:
                    wcmb_use[(i, hc // 4)] = ctr["s_T"]
        blk.tensor(_ffa)
        ffa_last[i] = ctr["s_T"]

        if nhs > 2:
            def _wsB(sy, i=i):
                for hs in range(2, nhs):
                    wait(sy, "sync", "s_T", wcmb_use[(i, hs - 2)])
                    assert inc(sy.dma_start(out=wcmb_st[:, hs % 2], in_=wcmb_sc[hs]),
                               wcmb_sem(hs), 16) == wcmb_ld(i, hs)
            blk.sync(_wsB)

        def _relu(sc, S0=S0):
            for hc in range(hch):
                b = 2 + hc % 2
                wait(sc, "scalar", "s_T", fad[hc])
                inc(sc.activation(out=r2b[:, hc], in_=pb[b][:], func=AF.Relu), "s_S")
        blk.scalar(_relu)

        QS = max(1, hch // 4)
        sqc = {}

        def _sq2(ve, S0=S0):
            for qq in range(hch // QS):
                wait(ve, "vector", "s_S", S0 + (qq + 1) * QS)
                inc(ve.tensor_tensor(out=r2b[:, qq * QS:(qq + 1) * QS],
                                     in0=r2b[:, qq * QS:(qq + 1) * QS],
                                     in1=r2b[:, qq * QS:(qq + 1) * QS],
                                     op=ALU.mult), "s_V")
                sqc[qq] = ctr["s_V"]
        blk.vector(_sq2)
        r2_done[i] = ctr["s_V"]
        sq_chunks[i] = dict(sqc)

    def emit_ffnB(i):
        p = i % 2
        fbd = {}
        V0 = ctr["s_V"]

        QS = max(1, hch // 4)

        def _ffb(te, i=i, V0=V0):
            for oc in range(DCH):
                wait(te, "tensor", wproj_sem(oc), wproj_ld(i, oc))
                if oc >= 2:
                    wait(te, "tensor", "s_V", V0 + oc - 1)
                b = 6 + oc % 2
                for kc in range(hch):
                    if oc == 0 and kc % QS == 0:
                        wait(te, "tensor", "s_V", sq_chunks[i][kc // QS])
                    i2 = te.matmul(pb[b][:], wproj_st[:, oc % 2, kc, :],
                                   r2b[:, kc], start=(kc == 0), stop=(kc == hch - 1))
                fbd[oc] = inc(i2, "s_T")
                wproj_use[(i, oc)] = ctr["s_T"]
        blk.tensor(_ffb)
        ffb_last[i] = ctr["s_T"]

        def _wsC(sy, i=i):
            for oc in range(2, DCH):
                wait(sy, "sync", "s_T", wproj_use[(i, oc - 2)])
                assert inc(sy.dma_start(out=wproj_st[:, oc % 2], in_=wproj_sc[oc]),
                           wproj_sem(oc), 16) == wproj_ld(i, oc)
        blk.sync(_wsC)

        def _final(ve, i=i, p=p):
            for oc in range(DCH):
                b = 6 + oc % 2
                wait(ve, "vector", "s_T", fbd[oc])
                inc(ve.tensor_tensor(out=xn32[:, p, oc], in0=xn32[:, p, oc],
                                     in1=pb[b][:], op=ALU.add), "s_V")
        blk.vector(_final)
        final_done[i] = run["final"] = ctr["s_V"]

        def _outd(g, i=i, p=p):
            wait(g, "gpsimd", "s_V", final_done[i])
            out_done[i] = inc(g.dma_start(out=yout_ap(i), in_=xn32[:, p]),
                              "s_out%d" % (i % 2), 16)
        blk.gpsimd(_outd)

        if i + 1 < nt:
            def _wsA(sy, i=i):
                for hs in range(2):
                    wait(sy, "sync", "s_T", wcmb_use[(i, hs + nhs - 2)])
                    assert inc(sy.dma_start(out=wcmb_st[:, hs], in_=wcmb_sc[hs]),
                               wcmb_sem(hs), 16) == wcmb_ld(i + 1, hs)
                for oc in range(2):
                    wait(sy, "sync", "s_T", wproj_use[(i, oc + 6)])
                    assert inc(sy.dma_start(out=wproj_st[:, oc], in_=wproj_sc[oc]),
                               wproj_sem(oc), 16) == wproj_ld(i + 1, oc)
            blk.sync(_wsA)

    def _wsA0(sy):
        wait(sy, "sync", "s_sc0", sc0_total)
        wait(sy, "sync", "s_sc1", sc1_total)
        for hs in range(2):
            assert inc(sy.dma_start(out=wcmb_st[:, hs], in_=wcmb_sc[hs]),
                       wcmb_sem(hs), 16) == wcmb_ld(0, hs)
        for oc in range(2):
            assert inc(sy.dma_start(out=wproj_st[:, oc], in_=wproj_sc[oc]),
                       wproj_sem(oc), 16) == wproj_ld(0, oc)
    blk.sync(_wsA0)

    emit_front1b(0, emit_front1(0), 0)
    emit_front2(0)
    s2 = emit_back1(0)
    emit_back2(0, s2)
    emit_front1b(1, emit_front1(1), 0)
    emit_front2(1)

    for i in range(nt):
        emit_ffnA(i)
        rs1 = emit_front1(i + 2) if i + 2 < nt else None
        emit_ffnB(i)
        if i + 1 < nt:
            s2 = emit_back1(i + 1)
        if i + 2 < nt:
            emit_front1b(i + 2, rs1, ffb_last[i])
        if i + 1 < nt:
            emit_back2(i + 1, s2)
        if i + 2 < nt:
            emit_front2(i + 2)


    def _fin(sy):
        wait(sy, "sync", "s_out0", ctr["s_out0"])
        wait(sy, "sync", "s_out1", ctr["s_out1"])
    blk.sync(_fin)

    blk.__exit__(None, None, None)
    return nc


_NC_CACHE = None


def make_in_maps(x, An, W_b, b_b, W_c, b_c, Dg, W_t, b_t, W_o, b_o,
                 fast_b, w_fc, w_proj, u, v, gate, spc=SEQ_PER_CORE):
    shared = dict(
        wfcT=np.ascontiguousarray(w_fc.T),
        wprojT=np.ascontiguousarray(w_proj.T),
        woT=np.ascontiguousarray(W_o.T),
        wbtT=np.ascontiguousarray(
            np.concatenate([W_b.T, np.repeat(W_t.T, NS, axis=1)], axis=1)),
        wcT=np.ascontiguousarray(W_c.T),
        vmat=np.ascontiguousarray(v),
        uT=np.ascontiguousarray(u.T),
        fbm=np.ascontiguousarray(fast_b),
        avec=np.ascontiguousarray(An.reshape(NS, 1)),
        bb16=np.ascontiguousarray(b_b.reshape(NS, 1)),
        bt16=np.ascontiguousarray(np.broadcast_to(b_t.reshape(1, 1), (NS, 1))),
        gate16=np.ascontiguousarray(np.broadcast_to(gate.reshape(1, 1), (NS, 1))),
        bc8=np.ascontiguousarray(b_c.reshape(DCH, 128).T),
        bo8=np.ascontiguousarray(b_o.reshape(DCH, 128).T),
        dd8=np.ascontiguousarray(Dg.reshape(DCH, 128).T),
    )
    ncores = x.shape[0] // spc
    in_maps = []
    for c in range(ncores):
        m = dict(shared)
        m["xT"] = np.ascontiguousarray(
            x[c * spc:(c + 1) * spc].transpose(0, 2, 1))
        in_maps.append(m)
    return in_maps


def kernel(L0_attn_x, L0_attn_A, L0_attn_W_b, L0_attn_b_b, L0_attn_W_c,
           L0_attn_b_c, L0_attn_D, L0_attn_W_t, L0_attn_b_t, L0_attn_W_o,
           L0_attn_b_o, L0_ffn_fast_b, L0_ffn_w_fc, L0_ffn_w_proj, L0_ffn_u,
           L0_ffn_v, L0_ffn_gate):
    global _NC_CACHE
    in_maps = make_in_maps(
        np.asarray(L0_attn_x, dtype=np.float32),
        np.asarray(L0_attn_A, dtype=np.float32),
        np.asarray(L0_attn_W_b, dtype=np.float32),
        np.asarray(L0_attn_b_b, dtype=np.float32),
        np.asarray(L0_attn_W_c, dtype=np.float32),
        np.asarray(L0_attn_b_c, dtype=np.float32),
        np.asarray(L0_attn_D, dtype=np.float32),
        np.asarray(L0_attn_W_t, dtype=np.float32),
        np.asarray(L0_attn_b_t, dtype=np.float32),
        np.asarray(L0_attn_W_o, dtype=np.float32),
        np.asarray(L0_attn_b_o, dtype=np.float32),
        np.asarray(L0_ffn_fast_b, dtype=np.float32),
        np.asarray(L0_ffn_w_fc, dtype=np.float32),
        np.asarray(L0_ffn_w_proj, dtype=np.float32),
        np.asarray(L0_ffn_u, dtype=np.float32),
        np.asarray(L0_ffn_v, dtype=np.float32),
        np.asarray(L0_ffn_gate, dtype=np.float32),
    )

    if _NC_CACHE is None:
        _NC_CACHE = build_nc()
    res = run_bass_kernel_spmd(_NC_CACHE, in_maps, core_ids=list(range(NCORES)))

    out = np.empty((BATCH, S, D), dtype=np.float32)
    for c in range(NCORES):
        out[c * SEQ_PER_CORE:(c + 1) * SEQ_PER_CORE] = \
            res.results[c]["yT"].transpose(0, 2, 1)
    return out


# version 21
# speedup vs baseline: 1.0376x; 1.0033x over previous
import numpy as np
import concourse.bass as bass
import concourse.mybir as mybir
from concourse.bass_utils import run_bass_kernel_spmd

F32 = mybir.dt.float32
BF16 = mybir.dt.bfloat16
AF = mybir.ActivationFunctionType
ALU = mybir.AluOpType

D = 1024
NS = 16
H = 4096
S = 2048
BATCH = 16
NCORES = 8
SEQ_PER_CORE = BATCH // NCORES
TT = 512
TILES_PER_SEQ = S // TT
DCH = D // 128
EPS = 1e-5
DELTA_SCALE = 0.99
DELTA_MIN = 0.01
SBUF_BASE = 16512


def build_nc(spc=SEQ_PER_CORE, tps=TILES_PER_SEQ, h=H):
    nt = spc * tps
    hch = h // 128
    nhs = h // TT
    s_len = tps * TT
    nsl_fc = h // 128
    nsl_pj = D // 128

    nc = bass.Bass("TRN2", target_bir_lowering=False)

    xT = nc.dram_tensor("xT", [spc, D, s_len], F32, kind="ExternalInput").ap()
    yT = nc.dram_tensor("yT", [spc, D, s_len], F32, kind="ExternalOutput").ap()
    wfcT = nc.dram_tensor("wfcT", [D, h], F32, kind="ExternalInput").ap()
    wprojT = nc.dram_tensor("wprojT", [h, D], F32, kind="ExternalInput").ap()
    woT = nc.dram_tensor("woT", [D, D], F32, kind="ExternalInput").ap()
    wbtT = nc.dram_tensor("wbtT", [D, 32], F32, kind="ExternalInput").ap()
    wcT = nc.dram_tensor("wcT", [NS, D], F32, kind="ExternalInput").ap()
    vmat = nc.dram_tensor("vmat", [NS, D], F32, kind="ExternalInput").ap()
    uT = nc.dram_tensor("uT", [NS, h], F32, kind="ExternalInput").ap()
    fbm = nc.dram_tensor("fbm", [NS, NS], F32, kind="ExternalInput").ap()
    avec = nc.dram_tensor("avec", [NS, 1], F32, kind="ExternalInput").ap()
    bb16 = nc.dram_tensor("bb16", [NS, 1], F32, kind="ExternalInput").ap()
    bt16 = nc.dram_tensor("bt16", [NS, 1], F32, kind="ExternalInput").ap()
    gate16 = nc.dram_tensor("gate16", [NS, 1], F32, kind="ExternalInput").ap()
    bc8 = nc.dram_tensor("bc8", [128, DCH], F32, kind="ExternalInput").ap()
    bo8 = nc.dram_tensor("bo8", [128, DCH], F32, kind="ExternalInput").ap()
    dd8 = nc.dram_tensor("dd8", [128, DCH], F32, kind="ExternalInput").ap()

    wcmb_sc = nc.dram_tensor("wcmb_sc", [nhs, 128, DCH, TT], BF16, kind="Internal").ap()
    wproj_sc = nc.dram_tensor("wproj_sc", [DCH, 128, hch, 128], BF16, kind="Internal").ap()

    cur = [SBUF_BASE]

    def alloc(name, shape, dtype, at=None):
        import math
        size = math.prod(shape[1:]) * mybir.dt.size(dtype)
        size = (size + 63) // 64 * 64
        if at is None:
            at = cur[0]
            cur[0] += size
        else:
            assert at + size <= nc.SBUF_PARTITION_SIZE_BYTES, (name, at, size)
        return nc.alloc_sbuf_tensor_at(name, shape, dtype, offset=at)

    x_s = alloc("x_s", [128, 2, DCH, TT], F32)
    x_s_at = cur[0] - 2 * DCH * TT * 4
    xn32 = alloc("xn32", [128, 2, DCH, TT], F32)
    xn32_at = cur[0] - 2 * DCH * TT * 4
    xb = alloc("xb", [128, DCH, TT], BF16)
    xb_at = cur[0] - DCH * TT * 2
    yb = alloc("yb", [128, DCH, TT], BF16)
    xnb = alloc("xnb", [128, 2, DCH, TT], BF16)
    xnb_at = cur[0] - 2 * DCH * TT * 2
    r2b = alloc("r2b", [128, hch, TT], BF16)
    r2b_at = cur[0] - hch * TT * 2
    woT_b = alloc("woT_b", [128, DCH, D], BF16)
    wcmb_st = alloc("wcmb_st", [128, 2, DCH, TT], BF16)
    wcmb_st_at = cur[0] - 2 * DCH * TT * 2
    wproj_st = alloc("wproj_st", [128, 2, hch, 128], BF16)
    wproj_st_at = cur[0] - 2 * hch * 128 * 2
    statA = alloc("statA", [128, 4, TT], F32)
    statA_at = cur[0] - 4 * TT * 4
    sqbuf = alloc("sqbuf", [128, DCH, TT], BF16, at=statA_at)
    statB = alloc("statB", [128, 4, TT], F32)
    statB_at = cur[0] - 4 * TT * 4
    tmp2 = alloc("tmp2", [128, 2, TT], F32)
    scb16 = alloc("scb16", [128, TT], BF16, at=statB_at)
    hfbc = alloc("hfbc", [128, TT], F32, at=statB_at + TT * 2)
    nhfbc = alloc("nhfbc", [128, TT], F32, at=statB_at + TT * 6)
    ones_b = alloc("ones_b", [128, 128], BF16)
    ones32 = alloc("ones32", [128, 128], F32, at=statA_at)
    wcT_b = alloc("wcT_b", [NS, DCH, 128], BF16)
    v_b = alloc("v_b", [NS, DCH, 128], BF16, at=statA_at + 512)
    e1t = alloc("e1t", [NS, TT], F32)
    rsig = e1t
    aext = e1t
    but = alloc("but", [NS, TT], F32)
    ht = alloc("ht", [NS, TT], F32)
    carry = alloc("carry", [NS, 1], F32)
    hbt = alloc("hbt", [NS, TT], BF16)
    fbT_b = alloc("fbT_b", [NS, NS], BF16, at=statA_at + 2560)
    A_s = alloc("A_s", [NS, 1], F32)
    A99 = A_s
    A01 = alloc("A01", [NS, 1], F32)
    bb_s = alloc("bb_s", [NS, 1], F32)
    bt_s = alloc("bt_s", [NS, 1], F32)
    nbt = bt_s
    gate_s = alloc("gate_s", [NS, 1], F32, at=statA_at + 2624)
    bc_s = alloc("bc_s", [128, DCH], F32)
    bo_s = alloc("bo_s", [128, DCH], F32)
    dd_s = alloc("dd_s", [128, DCH], F32)
    wbtT_b = alloc("wbtT_b", [128, DCH, 32], BF16)
    assert cur[0] <= nc.SBUF_PARTITION_SIZE_BYTES, cur[0]

    r2b_size = hch * TT * 2
    if r2b_size >= 32768:
        GTb = alloc("GTb", [NS, h], BF16, at=wcmb_st_at)
        uT_b = alloc("uT_b", [NS, h], BF16, at=wcmb_st_at + 2 * h)
        cb1 = alloc("cb1", [128, DCH, TT], BF16, at=xnb_at)
        cb2 = alloc("cb2", [128, DCH, TT], BF16, at=r2b_at)
        dbuf = alloc("dbuf", [128, DCH, TT], BF16, at=r2b_at + 8192)
        wsl = alloc("wsl", [128, DCH, TT], F32, at=r2b_at + 16384)
        wsl2 = alloc("wsl2", [128, hch, 128], F32, at=r2b_at + 16384)
        aw32 = alloc("aw32", [128, DCH, TT], F32, at=x_s_at)
        aw32p = alloc("aw32p", [128, hch, 128], F32, at=x_s_at)
        aw32B = alloc("aw32B", [128, DCH, TT], F32, at=xn32_at)
        aw32pB = alloc("aw32pB", [128, hch, 128], F32, at=xn32_at)
        wslB = alloc("wslB", [128, DCH, TT], F32, at=x_s_at + 16384)
        wsl2B = alloc("wsl2B", [128, hch, 128], F32, at=x_s_at + 16384)
        wr8 = alloc("wr8", [128, 2, DCH, TT], BF16, at=xb_at)
        tls = alloc("tls", [128, DCH, TT], F32, at=r2b_at)
    else:
        GTb = alloc("GTb", [NS, h], BF16)
        uT_b = alloc("uT_b", [NS, h], BF16)
        cb1 = alloc("cb1", [128, DCH, TT], BF16, at=xnb_at)
        cb2 = alloc("cb2", [128, DCH, TT], BF16, at=xn32_at)
        dbuf = alloc("dbuf", [128, DCH, TT], BF16, at=xn32_at + DCH * TT * 2)
        wsl = alloc("wsl", [128, DCH, TT], F32)
        wsl_at = cur[0] - DCH * TT * 4
        wsl2 = alloc("wsl2", [128, hch, 128], F32, at=wsl_at)
        aw32 = alloc("aw32", [128, DCH, TT], F32, at=x_s_at)
        aw32p = alloc("aw32p", [128, hch, 128], F32, at=x_s_at)
        aw32B = aw32
        aw32pB = aw32p
        wslB = alloc("wslB", [128, DCH, TT], F32, at=x_s_at + 16384)
        wsl2B = alloc("wsl2B", [128, hch, 128], F32, at=x_s_at + 16384)
        wr8 = alloc("wr8", [128, 2, DCH, TT], BF16, at=xb_at)
        tls = alloc("tls", [128, DCH, TT], F32, at=wsl_at)

    pb = [nc.alloc_psum_tensor(f"pb{i}", [128, TT], F32) for i in range(8)]

    sems = {}
    ctr = {}
    for name in ["s_x0", "s_x1", "s_cst", "s_out0", "s_out1",
                 "s_wc0", "s_wc1", "s_wp0", "s_wp1", "s_sc0", "s_sc1",
                 "s_wl0", "s_wl1", "s_sml", "s_T", "s_V", "s_S"]:
        sems[name] = nc.alloc_semaphore(name)
        ctr[name] = 0
    lastwait = {}

    def inc(ins, name, k=1):
        ins.then_inc(sems[name], k)
        ctr[name] += k
        return ctr[name]

    def wait(e, ename, name, val):
        if val is None or val <= 0:
            return
        key = (ename, name)
        if lastwait.get(key, 0) >= val:
            return
        e.wait_ge(sems[name], val)
        lastwait[key] = val

    blk = nc.Block()
    blk.__enter__()

    def wcmb_sem(hs):
        return "s_wc%d" % (hs % 2)

    def wcmb_ld(i, hs):
        return 16 * (i * (nhs // 2) + hs // 2 + 1)

    def wproj_sem(oc):
        return "s_wp%d" % (oc % 2)

    def wproj_ld(i, oc):
        return 16 * (i * 4 + oc // 2 + 1)
    wcmb_use = {}
    wproj_use = {}

    def x_sem(i):
        return "s_x%d" % (i % 2)

    def xin_count(i):
        return 16 * (i // 2 + 1)
    CST_ALL = 96

    def xin_ap(i):
        seq, q = i // tps, i % tps
        return xT[seq, :, q * TT:(q + 1) * TT].rearrange("(c p) t -> p c t", p=128)

    def yout_ap(i):
        seq, q = i // tps, i % tps
        return yT[seq, :, q * TT:(q + 1) * TT].rearrange("(c p) t -> p c t", p=128)

    def _loads(sy):
        for dst, src in [
            (A_s[:], avec), (bb_s[:], bb16), (bt_s[:], bt16),
            (gate_s[:], gate16), (bc_s[:], bc8), (bo_s[:], bo8), (dd_s[:], dd8),
        ]:
            inc(sy.dma_start(out=dst, in_=src), "s_sml", 16)
    blk.sync(_loads)
    sml_done = ctr["s_sml"]

    def _gx(g):
        inc(g.dma_start(out=uT_b[:], in_=uT), "s_cst", 16)
        inc(g.dma_start(out=v_b[:], in_=vmat.rearrange("n (c m) -> n c m", c=DCH)),
            "s_cst", 16)
        inc(g.dma_start(out=wcT_b[:], in_=wcT.rearrange("n (c m) -> n c m", c=DCH)),
            "s_cst", 16)
        inc(g.dma_start(out=wbtT_b[:], in_=wbtT.rearrange("(c p) m -> p c m", p=128)),
            "s_cst", 16)
        inc(g.dma_start(out=fbT_b[:], in_=fbm), "s_cst", 16)
        inc(g.dma_start(out=woT_b[:], in_=woT.rearrange("(c p) m -> p c m", p=128)),
            "s_cst", 16)
        assert ctr["s_cst"] == CST_ALL
    blk.gpsimd(_gx)

    def _memsets(ve):
        ve.memset(ones_b[:], 1.0)
        ve.memset(ones32[:], 1.0)
        wait(ve, "vector", "s_sml", sml_done)
        ve.tensor_scalar(out=A01[:], in0=A_s[:], scalar1=DELTA_MIN, scalar2=None, op0=ALU.mult)
        inc(ve.tensor_scalar(out=nbt[:], in0=bt_s[:], scalar1=-1.0, scalar2=None, op0=ALU.mult),
            "s_V")
    blk.vector(_memsets)

    def _a99(ve):
        wait(ve, "vector", "s_V", ctr["s_V"])
        inc(ve.tensor_scalar(out=A99[:], in0=A_s[:], scalar1=DELTA_SCALE,
                             scalar2=None, op0=ALU.mult), "s_V")
    blk.vector(_a99)
    memset_done = ctr["s_V"]

    ngt = h // TT
    gt_v0 = ctr["s_V"]

    def _gt_mm(te):
        wait(te, "tensor", "s_cst", CST_ALL)
        wait(te, "tensor", "s_V", memset_done)
        for j in range(ngt):
            if j >= 1:
                wait(te, "tensor", "s_V", gt_v0 + j)
            inc(te.matmul(pb[4][0:NS, :], fbT_b[:], uT_b[:, j * TT:(j + 1) * TT],
                          start=True, stop=True), "s_T")
    blk.tensor(_gt_mm)
    gt_t0 = ctr["s_T"] - ngt

    def _gt_evac(ve):
        for j in range(ngt):
            wait(ve, "vector", "s_T", gt_t0 + j + 1)
            inc(ve.tensor_scalar(out=GTb[:, j * TT:(j + 1) * TT], in0=pb[4][0:NS, :],
                                 scalar1=gate_s[:, 0:1], scalar2=None, op0=ALU.mult), "s_V")
    blk.vector(_gt_evac)
    gt_evac_done = ctr["s_V"]

    last_sv = 0
    def scw_sem(j):
        return "s_sc%d" % (j % 2)

    def scw_count(j):
        return 16 * (j // 2 + 1)

    summ_hist = {}
    cmp_hist = {}
    sc_hist = {}
    wadd_hist = {}
    wrp_hist = {}
    slice_c_done = {}
    pend = []

    def emit_fold(hs, wrp_done):
        LRB = [1, 2, 4, 5, 6, 7]
        mm_done = {}
        for dc in range(DCH):
            b = LRB[dc % 6]

            def _wlr(te, hs=hs, dc=dc, b=b):
                wait(te, "tensor", "s_cst", CST_ALL)
                if dc >= 6:
                    prev = wadd_hist.get((hs, dc - 6), 0)
                else:
                    pdc = dc + 6 if dc < 2 else dc
                    prev = wadd_hist.get((hs - 1, pdc), 0)
                wait(te, "tensor", "s_V", max(gt_evac_done, prev))
                mm_done[dc] = inc(
                    te.matmul(pb[b][:], v_b[:, dc], GTb[:, hs * TT:(hs + 1) * TT],
                              start=True, stop=True), "s_T")
            blk.tensor(_wlr)

            def _wadd(ve, hs=hs, dc=dc, b=b, wrp_done=wrp_done):
                wait(ve, "vector", "s_V", wrp_done)
                wait(ve, "vector", "s_T", mm_done[dc])
                inc(ve.tensor_tensor(out=wr8[:, hs % 2, dc], in0=wr8[:, hs % 2, dc],
                                     in1=pb[b][:], op=ALU.add), "s_V")
                wadd_hist[(hs, dc)] = ctr["s_V"]
            blk.vector(_wadd)
        slice_c_done[hs] = ctr["s_V"]

        def _wr(g, hs=hs):
            wait(g, "gpsimd", "s_V", slice_c_done[hs])
            assert inc(g.dma_start(out=wcmb_sc[hs], in_=wr8[:, hs % 2]),
                       scw_sem(hs), 16) == scw_count(hs)
        blk.gpsimd(_wr)

    def wldq(j):
        return "s_wl%d" % (j % 2)

    def wldc(j):
        return 16 * (j // 2 + 1)

    for hs in range(nhs):
        wv = wsl if hs % 2 == 0 else wslB

        def _ld(sy, hs=hs, wv=wv):
            if hs >= 2:
                wait(sy, "sync", "s_V", cmp_hist[hs - 2])
            inc(sy.dma_start(
                out=wv[:],
                in_=wfcT[:, hs * TT:(hs + 1) * TT].rearrange("(c p) h -> p c h", p=128),
            ), wldq(hs), 16)
        blk.sync(_ld)
        ld_done = ctr[wldq(hs)]

        awdb = 2 if aw32B is not aw32 else 1
        av = aw32 if hs % 2 == 0 else aw32B

        def _abs(sc, hs=hs, wv=wv, av=av, awdb=awdb, ld_done=ld_done):
            wait(sc, "scalar", wldq(hs), ld_done)
            if hs >= awdb:
                wait(sc, "scalar", "s_T", summ_hist[hs - awdb])
            inc(sc.activation(out=av[:], in_=wv[:], func=AF.Abs), "s_S")
        blk.scalar(_abs)
        abs_done = ctr["s_S"]

        sb = pb[0] if hs % 2 == 0 else pb[3]

        def _summ(te, hs=hs, av=av, sb=sb, abs_done=abs_done):
            wait(te, "tensor", "s_S", abs_done)
            wait(te, "tensor", "s_V", sc_hist.get(hs - 2, 0))
            for dc in range(DCH):
                i2 = te.matmul(sb[:], ones32[:], av[:, dc],
                               start=(dc == 0), stop=(dc == DCH - 1))
            inc(i2, "s_T")
        blk.tensor(_summ)
        bc_done = summ_hist[hs] = ctr["s_T"]

        def _sc_evac(ve, hs=hs, sb=sb, bc_done=bc_done):
            wait(ve, "vector", "s_T", bc_done)
            wait(ve, "vector", "s_V", wrp_hist.get(hs - 1, 0))
            W = float(D)
            ve.tensor_scalar(out=scb16[:], in0=sb[:], scalar1=1.0 / W,
                             scalar2=1e-5, op0=ALU.mult, op1=ALU.max)
            ve.tensor_scalar(out=hfbc[:], in0=sb[:], scalar1=0.5 / W,
                             scalar2=0.5e-5, op0=ALU.mult, op1=ALU.max)
            inc(ve.tensor_scalar(out=nhfbc[:], in0=sb[:], scalar1=-0.5 / W,
                                 scalar2=-0.5e-5, op0=ALU.mult, op1=ALU.min), "s_V")
        blk.vector(_sc_evac)
        sc_done = sc_hist[hs] = ctr["s_V"]

        def _cmp(ve, hs=hs, wv=wv, ld_done=ld_done, sc_done=sc_done):
            wait(ve, "vector", "s_V", sc_done)
            wait(ve, "vector", wldq(hs), ld_done)
            if hs >= 2:
                wait(ve, "vector", scw_sem(hs - 2), scw_count(hs - 2))
            hf = hfbc[:, None, :].broadcast_to([128, DCH, TT])
            nhf = nhfbc[:, None, :].broadcast_to([128, DCH, TT])
            ve.tensor_tensor(out=cb1[:], in0=wv[:], in1=hf, op=ALU.is_gt)
            inc(ve.tensor_tensor(out=cb2[:], in0=wv[:], in1=nhf, op=ALU.is_lt),
                "s_V")
        blk.vector(_cmp)
        cmp_done = cmp_hist[hs] = ctr["s_V"]

        def _dsub(ve, cmp_done=cmp_done):
            wait(ve, "vector", "s_V", cmp_done)
            inc(ve.tensor_tensor(out=dbuf[:].rearrange("p c t -> p (c t)"),
                                 in0=cb1[:].rearrange("p c t -> p (c t)"),
                                 in1=cb2[:].rearrange("p c t -> p (c t)"),
                                 op=ALU.subtract), "s_V")
        blk.vector(_dsub)
        d_done = ctr["s_V"]

        def _wrp(ve, hs=hs, d_done=d_done):
            wait(ve, "vector", "s_V", d_done)
            sc = scb16[:, None, :].broadcast_to([128, DCH, TT])
            inc(ve.tensor_tensor(out=wr8[:, hs % 2], in0=dbuf[:], in1=sc,
                                 op=ALU.mult), "s_V")
        blk.vector(_wrp)
        wrp_done = wrp_hist[hs] = ctr["s_V"]

        pend.append((hs, wrp_done))
        if len(pend) > 1:
            emit_fold(*pend.pop(0))

    if pend:
        emit_fold(*pend.pop(0))
    last_sv = ctr["s_V"]

    for os_ in range(DCH):
        j = nhs + os_

        wv2 = wsl2 if j % 2 == 0 else wsl2B

        def _ld2(sy, j=j, wv2=wv2):
            wait(sy, "sync", "s_V", cmp_hist[j - 2])
            inc(sy.dma_start(
                out=wv2[:],
                in_=wprojT[:, os_ * 128:(os_ + 1) * 128].rearrange("(c p) o -> p c o", p=128),
            ), wldq(j), 16)
        blk.sync(_ld2)
        ld_done = ctr[wldq(j)]

        awdb = 2 if aw32pB is not aw32p else 1
        av2 = aw32p if j % 2 == 0 else aw32pB

        def _abs2(sc, j=j, wv2=wv2, av2=av2, awdb=awdb, ld_done=ld_done):
            wait(sc, "scalar", wldq(j), ld_done)
            wait(sc, "scalar", "s_T", summ_hist[j - awdb])
            inc(sc.activation(out=av2[:], in_=wv2[:], func=AF.Abs), "s_S")
        blk.scalar(_abs2)
        abs_done = ctr["s_S"]

        sb = pb[0] if j % 2 == 0 else pb[3]

        def _summ2(te, j=j, av2=av2, sb=sb, abs_done=abs_done):
            wait(te, "tensor", "s_S", abs_done)
            wait(te, "tensor", "s_V", sc_hist.get(j - 2, 0))
            for kc in range(hch):
                i2 = te.matmul(sb[:, 0:128], ones32[:], av2[:, kc],
                               start=(kc == 0), stop=(kc == hch - 1))
            inc(i2, "s_T")
        blk.tensor(_summ2)
        bc_done = summ_hist[j] = ctr["s_T"]

        def _sc_evac2(ve, j=j, sb=sb, bc_done=bc_done):
            wait(ve, "vector", "s_T", bc_done)
            wait(ve, "vector", "s_V", wrp_hist.get(j - 1, 0))
            W = float(h)
            ve.tensor_scalar(out=scb16[:, 0:128], in0=sb[:, 0:128], scalar1=1.0 / W,
                             scalar2=1e-5, op0=ALU.mult, op1=ALU.max)
            ve.tensor_scalar(out=hfbc[:, 0:128], in0=sb[:, 0:128], scalar1=0.5 / W,
                             scalar2=0.5e-5, op0=ALU.mult, op1=ALU.max)
            inc(ve.tensor_scalar(out=nhfbc[:, 0:128], in0=sb[:, 0:128],
                                 scalar1=-0.5 / W, scalar2=-0.5e-5,
                                 op0=ALU.mult, op1=ALU.min), "s_V")
        blk.vector(_sc_evac2)
        neg_done = sc_hist[j] = ctr["s_V"]

        def _cmp2(ve, j=j, wv2=wv2, ld_done=ld_done, neg_done=neg_done):
            wait(ve, "vector", "s_V", neg_done)
            wait(ve, "vector", wldq(j), ld_done)
            if j >= 2:
                wait(ve, "vector", scw_sem(j - 2), scw_count(j - 2))
            hf = hfbc[:, None, 0:128].broadcast_to([128, hch, 128])
            nhf = nhfbc[:, None, 0:128].broadcast_to([128, hch, 128])
            cb1v = cb1[:].rearrange("p c (a b) -> p (c a) b", b=128)[:, 0:hch, :]
            cb2v = cb2[:].rearrange("p c (a b) -> p (c a) b", b=128)[:, 0:hch, :]
            ve.tensor_tensor(out=cb1v, in0=wv2[:], in1=hf, op=ALU.is_gt)
            inc(ve.tensor_tensor(out=cb2v, in0=wv2[:], in1=nhf, op=ALU.is_lt), "s_V")
        blk.vector(_cmp2)
        cmp_done = cmp_hist[j] = ctr["s_V"]

        def _dsub2(ve, cmp_done=cmp_done):
            wait(ve, "vector", "s_V", cmp_done)
            inc(ve.tensor_tensor(
                out=dbuf[:].rearrange("p c (a b) -> p (c a) b", b=128)[:, 0:hch, :],
                in0=cb1[:].rearrange("p c (a b) -> p (c a) b", b=128)[:, 0:hch, :],
                in1=cb2[:].rearrange("p c (a b) -> p (c a) b", b=128)[:, 0:hch, :],
                op=ALU.subtract), "s_V")
        blk.vector(_dsub2)
        d_done = ctr["s_V"]

        def _wrp2(ve, os_=os_, j=j, d_done=d_done):
            wait(ve, "vector", "s_V", d_done)
            sc = scb16[:, None, 0:128].broadcast_to([128, hch, 128])
            inc(ve.tensor_tensor(
                out=wr8[:, j % 2].rearrange("p c (a b) -> p (c a) b", b=128)[:, 0:hch, :],
                in0=dbuf[:].rearrange("p c (a b) -> p (c a) b", b=128)[:, 0:hch, :],
                in1=sc, op=ALU.mult), "s_V")
        blk.vector(_wrp2)
        wrp_hist[j] = ctr["s_V"]
        slice2_c_done = slice2_c_done if os_ else {}
        slice2_c_done[os_] = ctr["s_V"]

        def _wr2(g, os_=os_, j=j):
            wait(g, "gpsimd", "s_V", slice2_c_done[os_])
            assert inc(g.dma_start(
                out=wproj_sc[os_],
                in_=wr8[:, j % 2].rearrange("p c (a b) -> p (c a) b", b=128)[:, 0:hch, :]),
                scw_sem(j), 16) == scw_count(j)
        blk.gpsimd(_wr2)
        last_sv = ctr["s_V"]

    sc0_total, sc1_total = ctr["s_sc0"], ctr["s_sc1"]
    tern_sv_total = ctr["s_V"]
    sums_total = ctr["s_T"]

    def _gx2(g):
        wait(g, "gpsimd", "s_T", summ_hist[nhs + DCH - 1])
        wait(g, "gpsimd", "s_V", cmp_hist[nhs + DCH - 1])
        inc(g.dma_start(out=x_s[:, 0], in_=xin_ap(0)), "s_x0", 16)
        inc(g.dma_start(out=x_s[:, 1], in_=xin_ap(1)), "s_x1", 16)
    blk.gpsimd(_gx2)

    mA, uA, auxA, rstdA = (statA[:, k] for k in range(4))
    qA = mA
    mB, uB, auxB, rstdB = (statB[:, k] for k in range(4))
    qB = mB

    hb_done = {}
    butp_done = {}
    ln1c_done = {}
    ln2_done = {}
    yb_done = {}
    wc_mm = {}
    wo_mm = {}
    wo_stt_done = {}
    final_done = {}
    out_done = {}
    r2_done = {}
    ffa_last = {}
    back1_sqh = {}
    sq_chunks = {}
    ffb_last = {}
    run = {"statsv1": 0, "wo_stt": 0, "final": 0, "yb": 0, "tmp2": 0,
           "tls": 0, "nz": 0}

    def emit_front1(j):
        p = j % 2
        src = x_s[:, p]

        def _sq(ve, j=j):
            if j == 0:
                wait(ve, "vector", "s_V", tern_sv_total)
                wait(ve, "vector", "s_sc0", sc0_total)
                wait(ve, "vector", "s_sc1", sc1_total)
            if j >= 1:
                wait(ve, "vector", "s_V", ln1c_done[j - 1])
            wait(ve, "vector", x_sem(j), xin_count(j))
            ve.tensor_tensor(out=sqbuf[:], in0=src, in1=src, op=ALU.mult)
            inc(ve.tensor_copy(out=xb[:], in_=src), "s_V")
        blk.vector(_sq)
        sq_done = ctr["s_V"]

        def _stats1(te, sq_done=sq_done):
            wait(te, "tensor", "s_V", sq_done)
            for dc in range(DCH):
                te.matmul(pb[0][:], ones_b[:], xb[:, dc],
                          start=(dc == 0), stop=(dc == DCH - 1))
            for dc in range(DCH):
                i2 = te.matmul(pb[1][:], ones_b[:], sqbuf[:, dc],
                               start=(dc == 0), stop=(dc == DCH - 1))
            inc(i2, "s_T")
        blk.tensor(_stats1)
        st_done = ctr["s_T"]

        def _sv1(ve, st_done=st_done):
            wait(ve, "vector", "s_T", st_done)
            ve.tensor_scalar(out=mA, in0=pb[0][:], scalar1=1.0 / D,
                             scalar2=None, op0=ALU.mult)
            inc(ve.tensor_scalar(out=uA, in0=pb[1][:], scalar1=1.0 / D,
                                 scalar2=EPS, op0=ALU.mult, op1=ALU.add), "s_V")
        blk.vector(_sv1)
        run["statsv1"] = mu = ctr["s_V"]

        def _msq(ve, mu=mu):
            wait(ve, "vector", "s_V", mu)
            inc(ve.tensor_tensor(out=auxA, in0=mA, in1=mA, op=ALU.mult), "s_V")
        blk.vector(_msq)
        ms = ctr["s_V"]

        def _varr(ve, ms=ms):
            wait(ve, "vector", "s_V", ms)
            inc(ve.tensor_tensor(out=uA, in0=uA, in1=auxA, op=ALU.subtract), "s_V")
        blk.vector(_varr)
        va = ctr["s_V"]

        def _lnv(sc, va=va):
            wait(sc, "scalar", "s_V", va)
            inc(sc.activation(out=auxA, in_=uA, func=AF.Ln), "s_S")
        blk.scalar(_lnv)
        lv = ctr["s_S"]

        def _nh(ve, lv=lv):
            wait(ve, "vector", "s_S", lv)
            inc(ve.tensor_scalar(out=auxA, in0=auxA, scalar1=-0.5,
                                 scalar2=None, op0=ALU.mult), "s_V")
        blk.vector(_nh)
        nh = ctr["s_V"]

        def _rstd(sc, nh=nh):
            wait(sc, "scalar", "s_V", nh)
            inc(sc.activation(out=rstdA, in_=auxA, func=AF.Exp), "s_S")
        blk.scalar(_rstd)
        rs = ctr["s_S"]

        return rs

    def emit_front1b(j, rs, ffb_guard):
        p = j % 2
        src = x_s[:, p]

        def _m1(ve, j=j, rs=rs, ffb_guard=ffb_guard):
            wait(ve, "vector", "s_S", rs)
            wait(ve, "vector", "s_V", run["tls"])
            wait(ve, "vector", "s_T", ffb_guard)
            ve.tensor_tensor(out=qA, in0=mA, in1=rstdA, op=ALU.mult)
            last = None
            for dc in range(DCH):
                last = ve.tensor_tensor(out=tls[:, dc], in0=src[:, dc],
                                        in1=rstdA, op=ALU.mult)
            inc(last, "s_V")
        blk.vector(_m1)
        md = ctr["s_V"]

        def _s1(ve, j=j, md=md):
            wait(ve, "vector", "s_V", md)
            last = None
            for dc in range(DCH):
                last = ve.tensor_tensor(out=xnb[:, p, dc], in0=tls[:, dc],
                                        in1=qA, op=ALU.subtract)
            inc(last, "s_V")
        blk.vector(_s1)
        ln1c_done[j] = run["tls"] = ctr["s_V"]

    def emit_front2(j):
        p = j % 2
        seq_start = (j % tps) == 0

        def _butp(te, j=j):
            wait(te, "tensor", "s_V", ln1c_done[j])
            for dc in range(DCH):
                te.matmul(pb[2][0:NS, :], wbtT_b[:, dc, 0:NS], xnb[:, p, dc],
                          start=(dc == 0), stop=(dc == DCH - 1))
            for dc in range(DCH):
                i2 = te.matmul(pb[3][0:NS, :], wbtT_b[:, dc, NS:2 * NS], xnb[:, p, dc],
                               start=(dc == 0), stop=(dc == DCH - 1))
            inc(i2, "s_T")
        blk.tensor(_butp)
        butp_done[j] = bp = ctr["s_T"]

        def _nz(ve, bp=bp):
            wait(ve, "vector", "s_T", bp)
            ve.tensor_scalar(out=e1t[:], in0=pb[3][0:NS, :], scalar1=-1.0,
                             scalar2=nbt[:, 0:1], op0=ALU.mult, op1=ALU.add)
            inc(ve.tensor_scalar(out=but[:], in0=pb[2][0:NS, :],
                                 scalar1=bb_s[:, 0:1], scalar2=None, op0=ALU.add), "s_V")
        blk.vector(_nz)
        nz = run["nz"] = ctr["s_V"]

        def _e1(sc, nz=nz):
            wait(sc, "scalar", "s_V", nz)
            inc(sc.activation(out=e1t[:], in_=e1t[:], func=AF.Exp), "s_S")
        blk.scalar(_e1)
        e1 = ctr["s_S"]

        def _e1p(ve, e1=e1):
            wait(ve, "vector", "s_S", e1)
            inc(ve.tensor_scalar(out=e1t[:], in0=e1t[:], scalar1=1.0,
                                 scalar2=None, op0=ALU.add), "s_V")
        blk.vector(_e1p)
        ep = ctr["s_V"]

        def _rsig(ve, ep=ep):
            wait(ve, "vector", "s_V", ep)
            inc(ve.reciprocal(out=rsig[:], in_=e1t[:]), "s_V")
        blk.vector(_rsig)
        rg = ctr["s_V"]

        def _zt(ve, rg=rg):
            wait(ve, "vector", "s_V", rg)
            inc(ve.tensor_scalar(out=rsig[:], in0=rsig[:], scalar1=A99[:, 0:1],
                                 scalar2=A01[:, 0:1], op0=ALU.mult, op1=ALU.add), "s_V")
        blk.vector(_zt)
        zt = ctr["s_V"]

        def _aext(sc, zt=zt):
            wait(sc, "scalar", "s_V", zt)
            inc(sc.activation(out=aext[:], in_=rsig[:], func=AF.Exp), "s_S")
        blk.scalar(_aext)
        ax = ctr["s_S"]

        def _scan(ve, j=j, seq_start=seq_start, ax=ax):
            wait(ve, "vector", "s_S", ax)
            init = 0.0 if seq_start else carry[:, 0:1]
            inc(ve.tensor_tensor_scan(out=ht[:], data0=aext[:], data1=but[:],
                                      initial=init, op0=ALU.mult, op1=ALU.add), "s_V")
        blk.vector(_scan)
        sn = ctr["s_V"]

        def _hb(ve, j=j, sn=sn):
            wait(ve, "vector", "s_V", sn)
            ve.tensor_copy(out=carry[:], in_=ht[:, TT - 1:TT])
            inc(ve.tensor_copy(out=hbt[:], in_=ht[:]), "s_V")
        blk.vector(_hb)
        hb_done[j] = ctr["s_V"]

    def emit_back1(j):
        p = j % 2
        wcd = {}
        V0 = ctr["s_V"]

        def _wc(te, j=j, V0=V0):
            wait(te, "tensor", "s_V", hb_done[j])
            wait(te, "tensor", "s_V", run["statsv1"])
            for dc in range(DCH):
                b = dc % 2
                if dc >= 2:
                    wait(te, "tensor", "s_V", V0 + ((dc - 2) // 2 + 1) * 2)
                wcd[dc] = inc(
                    te.matmul(pb[b][:], wcT_b[:, dc], hbt[:], start=True, stop=True), "s_T")
        blk.tensor(_wc)
        wc_mm[j] = dict(wcd)

        prev_ybp = 0
        for pp in range(DCH // 2):
            def _t2(ve, j=j, pp=pp, prev_ybp=prev_ybp):
                if pp:
                    wait(ve, "vector", "s_V", prev_ybp)
                else:
                    wait(ve, "vector", "s_V", run["tmp2"])
                ve.tensor_scalar(out=tmp2[:, 0], in0=xnb[:, p, 2 * pp],
                                 scalar1=dd_s[:, 2 * pp:2 * pp + 1], scalar2=None,
                                 op0=ALU.mult)
                inc(ve.tensor_scalar(out=tmp2[:, 1], in0=xnb[:, p, 2 * pp + 1],
                                     scalar1=dd_s[:, 2 * pp + 1:2 * pp + 2], scalar2=None,
                                     op0=ALU.mult), "s_V")
            blk.vector(_t2)
            t2d = ctr["s_V"]

            def _ybp(ve, j=j, pp=pp, t2d=t2d):
                wait(ve, "vector", "s_V", t2d)
                for q in range(2):
                    dc = 2 * pp + q
                    wait(ve, "vector", "s_T", wcd[dc])
                    ins = ve.scalar_tensor_tensor(out=yb[:, dc], in0=pb[dc % 2][:],
                                                  scalar=bc_s[:, dc:dc + 1],
                                                  in1=tmp2[:, q],
                                                  op0=ALU.add, op1=ALU.add)
                inc(ins, "s_V")
            blk.vector(_ybp)
            prev_ybp = ctr["s_V"]
        yb_done[j] = run["yb"] = run["tmp2"] = ctr["s_V"]

        wod = {}
        V1 = ctr["s_V"]

        STTN = [1, 2, 3, 4, 6, 7, 8, 9]

        def _wo(te, j=j, V1=V1):
            wait(te, "tensor", "s_V", yb_done[j])
            for oc in range(DCH):
                b = 4 + oc % 2
                if oc >= 2:
                    wait(te, "tensor", "s_V", V1 + STTN[oc - 2])
                for kc in range(DCH):
                    i2 = te.matmul(pb[b][:], woT_b[:, kc, oc * 128:(oc + 1) * 128],
                                   yb[:, kc], start=(kc == 0), stop=(kc == DCH - 1))
                wod[oc] = inc(i2, "s_T")
        blk.tensor(_wo)
        wo_mm[j] = dict(wod)

        sqh = {}

        def _wo_stt(ve, j=j):
            if j >= 2:
                wait(ve, "vector", "s_out%d" % (j % 2), 16 * ((j - 2) // 2 + 1))
            for g2 in range(2):
                for q in range(4):
                    oc = 4 * g2 + q
                    b = 4 + oc % 2
                    wait(ve, "vector", "s_T", wod[oc])
                    inc(ve.scalar_tensor_tensor(out=xn32[:, p, oc], in0=pb[b][:],
                                                scalar=bo_s[:, oc:oc + 1],
                                                in1=x_s[:, p, oc],
                                                op0=ALU.add, op1=ALU.add), "s_V")
                hw = ctr["s_V"]

                def _sqh(ve2, j=j, g2=g2, hw=hw):
                    wait(ve2, "vector", "s_V", hw)
                    sl = slice(4 * g2, 4 * g2 + 4)
                    inc(ve2.tensor_tensor(out=xnb[:, p, sl], in0=xn32[:, p, sl],
                                          in1=xn32[:, p, sl], op=ALU.mult), "s_V")
                _sqh(ve)
                sqh[g2] = ctr["s_V"]
            inc(ve.tensor_copy(out=yb[:], in_=xn32[:, p]), "s_V")
            sqh[2] = ctr["s_V"]
        blk.vector(_wo_stt)
        wo_stt_done[j] = run["wo_stt"] = ws = ctr["s_V"]

        if j + 2 < nt:
            def _pref(g, j=j, ws=ws):
                wait(g, "gpsimd", "s_V", ws)
                assert inc(g.dma_start(out=x_s[:, j % 2], in_=xin_ap(j + 2)),
                           x_sem(j), 16) == xin_count(j + 2)
            blk.gpsimd(_pref)

        back1_sqh[j] = dict(sqh)
        return ctr["s_V"]

    def emit_back2(j, sq2_done):
        p = j % 2

        def _stats2(te, j=j, sq2_done=sq2_done):
            wait(te, "tensor", "s_V", back1_sqh[j][0])
            wait(te, "tensor", "s_V", run["final"])
            for dc in range(4):
                te.matmul(pb[7][:], ones_b[:], xnb[:, p, dc],
                          start=(dc == 0), stop=False)
            wait(te, "tensor", "s_V", back1_sqh[j][1])
            for dc in range(4, DCH):
                te.matmul(pb[7][:], ones_b[:], xnb[:, p, dc],
                          start=False, stop=(dc == DCH - 1))
            wait(te, "tensor", "s_V", back1_sqh[j][2])
            for dc in range(DCH):
                i2 = te.matmul(pb[6][:], ones_b[:], yb[:, dc],
                               start=(dc == 0), stop=(dc == DCH - 1))
            inc(i2, "s_T")
        blk.tensor(_stats2)
        st = ctr["s_T"]

        def _sv2(ve, st=st):
            wait(ve, "vector", "s_T", st)
            ve.tensor_scalar(out=mB, in0=pb[6][:], scalar1=1.0 / D,
                             scalar2=None, op0=ALU.mult)
            inc(ve.tensor_scalar(out=uB, in0=pb[7][:], scalar1=1.0 / D,
                                 scalar2=EPS, op0=ALU.mult, op1=ALU.add), "s_V")
        blk.vector(_sv2)
        mu = ctr["s_V"]

        def _msq2(ve, mu=mu):
            wait(ve, "vector", "s_V", mu)
            inc(ve.tensor_tensor(out=auxB, in0=mB, in1=mB, op=ALU.mult), "s_V")
        blk.vector(_msq2)
        ms = ctr["s_V"]

        def _varr2(ve, ms=ms):
            wait(ve, "vector", "s_V", ms)
            inc(ve.tensor_tensor(out=uB, in0=uB, in1=auxB, op=ALU.subtract), "s_V")
        blk.vector(_varr2)
        va = ctr["s_V"]

        def _lnv2(sc, va=va):
            wait(sc, "scalar", "s_V", va)
            inc(sc.activation(out=auxB, in_=uB, func=AF.Ln), "s_S")
        blk.scalar(_lnv2)
        lv = ctr["s_S"]

        def _nh2(ve, lv=lv):
            wait(ve, "vector", "s_S", lv)
            inc(ve.tensor_scalar(out=auxB, in0=auxB, scalar1=-0.5,
                                 scalar2=None, op0=ALU.mult), "s_V")
        blk.vector(_nh2)
        nh = ctr["s_V"]

        def _rstd2(sc, nh=nh):
            wait(sc, "scalar", "s_V", nh)
            inc(sc.activation(out=rstdB, in_=auxB, func=AF.Exp), "s_S")
        blk.scalar(_rstd2)
        rs = ctr["s_S"]

        def _m2(ve, j=j, rs=rs):
            wait(ve, "vector", "s_S", rs)
            wait(ve, "vector", "s_V", run["tls"])
            ve.tensor_tensor(out=qB, in0=mB, in1=rstdB, op=ALU.mult)
            last = None
            for dc in range(DCH):
                last = ve.tensor_tensor(out=tls[:, dc], in0=xn32[:, p, dc],
                                        in1=rstdB, op=ALU.mult)
            inc(last, "s_V")
        blk.vector(_m2)
        md = ctr["s_V"]

        def _s2(ve, j=j, md=md):
            wait(ve, "vector", "s_V", md)
            last = None
            for dc in range(DCH):
                last = ve.tensor_tensor(out=xnb[:, p, dc], in0=tls[:, dc],
                                        in1=qB, op=ALU.subtract)
            inc(last, "s_V")
        blk.vector(_s2)
        ln2_done[j] = run["tls"] = ctr["s_V"]

    def emit_ffnA(i):
        p = i % 2
        fad = {}
        S0 = ctr["s_S"]

        def _ffa(te, i=i, S0=S0):
            wait(te, "tensor", "s_V", ln2_done[i])
            wait(te, "tensor", "s_V", run["nz"])
            for hc in range(hch):
                hs = hc // 4
                wait(te, "tensor", wcmb_sem(hs), wcmb_ld(i, hs))
                if hc >= 2:
                    wait(te, "tensor", "s_S", S0 + hc - 1)
                b = 2 + hc % 2
                for dc in range(DCH):
                    i2 = te.matmul(pb[b][:],
                                   wcmb_st[:, hs % 2, dc, (hc % 4) * 128:(hc % 4 + 1) * 128],
                                   xnb[:, p, dc], start=(dc == 0), stop=(dc == DCH - 1))
                fad[hc] = inc(i2, "s_T")
                if hc % 4 == 3:
                    wcmb_use[(i, hc // 4)] = ctr["s_T"]
        blk.tensor(_ffa)
        ffa_last[i] = ctr["s_T"]

        if nhs > 2:
            def _wsB(sy, i=i):
                for hs in range(2, nhs):
                    wait(sy, "sync", "s_T", wcmb_use[(i, hs - 2)])
                    assert inc(sy.dma_start(out=wcmb_st[:, hs % 2], in_=wcmb_sc[hs]),
                               wcmb_sem(hs), 16) == wcmb_ld(i, hs)
            blk.sync(_wsB)

        def _relu(sc, S0=S0):
            for hc in range(hch):
                b = 2 + hc % 2
                wait(sc, "scalar", "s_T", fad[hc])
                inc(sc.activation(out=r2b[:, hc], in_=pb[b][:], func=AF.Relu), "s_S")
        blk.scalar(_relu)

        QS = max(1, hch // 4)
        sqc = {}

        def _sq2(ve, S0=S0):
            for qq in range(hch // QS):
                wait(ve, "vector", "s_S", S0 + (qq + 1) * QS)
                inc(ve.tensor_tensor(out=r2b[:, qq * QS:(qq + 1) * QS],
                                     in0=r2b[:, qq * QS:(qq + 1) * QS],
                                     in1=r2b[:, qq * QS:(qq + 1) * QS],
                                     op=ALU.mult), "s_V")
                sqc[qq] = ctr["s_V"]
        blk.vector(_sq2)
        r2_done[i] = ctr["s_V"]
        sq_chunks[i] = dict(sqc)

    def emit_ffnB(i):
        p = i % 2
        fbd = {}
        V0 = ctr["s_V"]

        QS = max(1, hch // 4)

        def _ffb(te, i=i, V0=V0):
            for oc in range(DCH):
                wait(te, "tensor", wproj_sem(oc), wproj_ld(i, oc))
                if oc >= 2:
                    wait(te, "tensor", "s_V", V0 + oc - 1)
                b = 6 + oc % 2
                for kc in range(hch):
                    if oc == 0 and kc % QS == 0:
                        wait(te, "tensor", "s_V", sq_chunks[i][kc // QS])
                    i2 = te.matmul(pb[b][:], wproj_st[:, oc % 2, kc, :],
                                   r2b[:, kc], start=(kc == 0), stop=(kc == hch - 1))
                fbd[oc] = inc(i2, "s_T")
                wproj_use[(i, oc)] = ctr["s_T"]
        blk.tensor(_ffb)
        ffb_last[i] = ctr["s_T"]

        def _wsC(sy, i=i):
            for oc in range(2, DCH):
                wait(sy, "sync", "s_T", wproj_use[(i, oc - 2)])
                assert inc(sy.dma_start(out=wproj_st[:, oc % 2], in_=wproj_sc[oc]),
                           wproj_sem(oc), 16) == wproj_ld(i, oc)
        blk.sync(_wsC)

        def _final(ve, i=i, p=p):
            for oc in range(DCH):
                b = 6 + oc % 2
                wait(ve, "vector", "s_T", fbd[oc])
                inc(ve.tensor_tensor(out=xn32[:, p, oc], in0=xn32[:, p, oc],
                                     in1=pb[b][:], op=ALU.add), "s_V")
        blk.vector(_final)
        final_done[i] = run["final"] = ctr["s_V"]

        def _outd(g, i=i, p=p):
            wait(g, "gpsimd", "s_V", final_done[i])
            out_done[i] = inc(g.dma_start(out=yout_ap(i), in_=xn32[:, p]),
                              "s_out%d" % (i % 2), 16)
        blk.gpsimd(_outd)

        if i + 1 < nt:
            def _wsA(sy, i=i):
                for hs in range(2):
                    wait(sy, "sync", "s_T", wcmb_use[(i, hs + nhs - 2)])
                    assert inc(sy.dma_start(out=wcmb_st[:, hs], in_=wcmb_sc[hs]),
                               wcmb_sem(hs), 16) == wcmb_ld(i + 1, hs)
                for oc in range(2):
                    wait(sy, "sync", "s_T", wproj_use[(i, oc + 6)])
                    assert inc(sy.dma_start(out=wproj_st[:, oc], in_=wproj_sc[oc]),
                               wproj_sem(oc), 16) == wproj_ld(i + 1, oc)
            blk.sync(_wsA)

    def _wsA0(sy):
        wait(sy, "sync", "s_sc0", sc0_total)
        wait(sy, "sync", "s_sc1", sc1_total)
        for hs in range(2):
            assert inc(sy.dma_start(out=wcmb_st[:, hs], in_=wcmb_sc[hs]),
                       wcmb_sem(hs), 16) == wcmb_ld(0, hs)
        for oc in range(2):
            assert inc(sy.dma_start(out=wproj_st[:, oc], in_=wproj_sc[oc]),
                       wproj_sem(oc), 16) == wproj_ld(0, oc)
    blk.sync(_wsA0)

    emit_front1b(0, emit_front1(0), 0)
    emit_front2(0)
    s2 = emit_back1(0)
    emit_back2(0, s2)
    emit_front1b(1, emit_front1(1), 0)
    emit_front2(1)

    for i in range(nt):
        emit_ffnA(i)
        rs1 = emit_front1(i + 2) if i + 2 < nt else None
        emit_ffnB(i)
        if i + 1 < nt:
            s2 = emit_back1(i + 1)
        if i + 2 < nt:
            emit_front1b(i + 2, rs1, ffb_last[i])
        if i + 1 < nt:
            emit_back2(i + 1, s2)
        if i + 2 < nt:
            emit_front2(i + 2)


    def _fin(sy):
        wait(sy, "sync", "s_out0", ctr["s_out0"])
        wait(sy, "sync", "s_out1", ctr["s_out1"])
    blk.sync(_fin)

    blk.__exit__(None, None, None)
    return nc


_NC_CACHE = None


def make_in_maps(x, An, W_b, b_b, W_c, b_c, Dg, W_t, b_t, W_o, b_o,
                 fast_b, w_fc, w_proj, u, v, gate, spc=SEQ_PER_CORE):
    shared = dict(
        wfcT=np.ascontiguousarray(w_fc.T),
        wprojT=np.ascontiguousarray(w_proj.T),
        woT=np.ascontiguousarray(W_o.T),
        wbtT=np.ascontiguousarray(
            np.concatenate([W_b.T, np.repeat(W_t.T, NS, axis=1)], axis=1)),
        wcT=np.ascontiguousarray(W_c.T),
        vmat=np.ascontiguousarray(v),
        uT=np.ascontiguousarray(u.T),
        fbm=np.ascontiguousarray(fast_b),
        avec=np.ascontiguousarray(An.reshape(NS, 1)),
        bb16=np.ascontiguousarray(b_b.reshape(NS, 1)),
        bt16=np.ascontiguousarray(np.broadcast_to(b_t.reshape(1, 1), (NS, 1))),
        gate16=np.ascontiguousarray(np.broadcast_to(gate.reshape(1, 1), (NS, 1))),
        bc8=np.ascontiguousarray(b_c.reshape(DCH, 128).T),
        bo8=np.ascontiguousarray(b_o.reshape(DCH, 128).T),
        dd8=np.ascontiguousarray(Dg.reshape(DCH, 128).T),
    )
    ncores = x.shape[0] // spc
    in_maps = []
    for c in range(ncores):
        m = dict(shared)
        m["xT"] = np.ascontiguousarray(
            x[c * spc:(c + 1) * spc].transpose(0, 2, 1))
        in_maps.append(m)
    return in_maps


def kernel(L0_attn_x, L0_attn_A, L0_attn_W_b, L0_attn_b_b, L0_attn_W_c,
           L0_attn_b_c, L0_attn_D, L0_attn_W_t, L0_attn_b_t, L0_attn_W_o,
           L0_attn_b_o, L0_ffn_fast_b, L0_ffn_w_fc, L0_ffn_w_proj, L0_ffn_u,
           L0_ffn_v, L0_ffn_gate):
    global _NC_CACHE
    in_maps = make_in_maps(
        np.asarray(L0_attn_x, dtype=np.float32),
        np.asarray(L0_attn_A, dtype=np.float32),
        np.asarray(L0_attn_W_b, dtype=np.float32),
        np.asarray(L0_attn_b_b, dtype=np.float32),
        np.asarray(L0_attn_W_c, dtype=np.float32),
        np.asarray(L0_attn_b_c, dtype=np.float32),
        np.asarray(L0_attn_D, dtype=np.float32),
        np.asarray(L0_attn_W_t, dtype=np.float32),
        np.asarray(L0_attn_b_t, dtype=np.float32),
        np.asarray(L0_attn_W_o, dtype=np.float32),
        np.asarray(L0_attn_b_o, dtype=np.float32),
        np.asarray(L0_ffn_fast_b, dtype=np.float32),
        np.asarray(L0_ffn_w_fc, dtype=np.float32),
        np.asarray(L0_ffn_w_proj, dtype=np.float32),
        np.asarray(L0_ffn_u, dtype=np.float32),
        np.asarray(L0_ffn_v, dtype=np.float32),
        np.asarray(L0_ffn_gate, dtype=np.float32),
    )

    if _NC_CACHE is None:
        _NC_CACHE = build_nc()
    res = run_bass_kernel_spmd(_NC_CACHE, in_maps, core_ids=list(range(NCORES)))

    out = np.empty((BATCH, S, D), dtype=np.float32)
    for c in range(NCORES):
        out[c * SEQ_PER_CORE:(c + 1) * SEQ_PER_CORE] = \
            res.results[c]["yT"].transpose(0, 2, 1)
    return out
